# version 04
# baseline (speedup 1.0000x reference)
import os
import numpy as np
import concourse.bass as bass
import concourse.mybir as mybir
from concourse.bass_utils import run_bass_kernel_spmd

F32 = mybir.dt.float32
BF16 = mybir.dt.bfloat16
I32 = mybir.dt.int32
AF = mybir.ActivationFunctionType
ALU = mybir.AluOpType
AXX = mybir.AxisListType.X

S, E, H, KVH, D = 4096, 4096, 32, 8, 128
GRP = H // KVH
HID = 11008
NHC = HID // 128
NEC = E // 128
NTB = S // 128
KEEP = 820.0
SQD = float(1.0 / np.sqrt(D))
NEGC = -1000.0

SLOT_EXT = [4, 32, 8, 28, 12, 24, 16, 20]
SLOT_OFF = [0]
for e_ in SLOT_EXT:
    SLOT_OFF.append(SLOT_OFF[-1] + e_)
SIMW = SLOT_OFF[-1] * 128
NSLOT = 8

def blk_of(j, s):
    k, hi = divmod(s, 2)
    return (31 - j - 4 * k) if hi else (j + 4 * k)

ENGINES = ("tensor", "vector", "scalar", "gpsimd", "sync")


class Sched:
    def __init__(self):
        self.ops = {e: [] for e in ENGINES}
        self.cnt = {"pe": 0, "dv": 0, "ac": 0}
        self.esem = {"tensor": "pe", "vector": "dv", "scalar": "ac"}
        self.dum = {}

    def op(self, eng, fn, waits=(), inc=True, pw=None, fence=False):
        if pw:
            spfn = self.dum.get(eng)
            if spfn is not None:
                self.op(eng, spfn, list(pw), inc=True)
                self.op(eng, spfn, [], inc=True)
            waits = list(waits) + list(pw)
        if os.environ.get("SIM_STRICT"):
            own = self.esem[eng]
            waits = list(waits) + [(own, self.cnt[own], None)]
        wl = {}
        for t in waits:
            if t is None:
                continue
            sem, val, src = t
            if (src == eng and not sem.startswith("q_")
                    and not os.environ.get("SIM_STRICT")):
                continue
            if wl.get(sem, -1) < val:
                wl[sem] = val
        tok = None
        if inc:
            sem = self.esem[eng]
            self.cnt[sem] += 1
            tok = (sem, self.cnt[sem], eng)
        self.ops[eng].append((sorted(wl.items()), fn, (sem, 1) if inc else None,
                              fence))
        return tok

    def dma(self, qeng, dst, src, waits=(), sem=None):
        if sem is None:
            sem = "q_" + dst.tensor.name
        else:
            sem = "q_" + sem
        wl = {}
        for t in waits:
            if t is None:
                continue
            s_, v_, src_ = t
            if wl.get(s_, -1) < v_:
                wl[s_] = v_
        self.cnt[sem] = self.cnt.get(sem, 0) + 16
        self.ops[qeng].append((sorted(wl.items()),
                               lambda e, d=dst, s=src: e.dma_start(out=d, in_=s),
                               (sem, 16), False))
        return (sem, self.cnt[sem], qeng)

    def snap(self):
        return [(sem, v, None) for sem, v in self.cnt.items() if v > 0]


def build_nc():
    from contextlib import ExitStack
    nc = bass.Bass("TRN2", target_bir_lowering=False, debug=False)

    def din(name, shape, dtype=F32):
        return nc.dram_tensor(name, list(shape), dtype, kind="ExternalInput").ap()

    xT = din("xT", (E, S))
    xq = din("xq", (E, 1024))
    wqTt = din("wqTt", (NEC, 128, NEC * 128))
    wkvTt = din("wkvTt", (16, 128, NEC * 128))
    woTt = din("woTt", (NEC, 128, NEC * 128))
    hp1 = din("hp1", (H, D, D))
    hp2 = din("hp2", (H, D, D))
    b1c = din("b1c", (D, H))
    b2c = din("b2c", (D, H))
    cosM = din("cosM", (128, NTB, D))
    nsinM = din("nsinM", (128, NTB, D))
    cosQ = din("cosQ", (128, NSLOT, D))
    nsinQ = din("nsinQ", (128, NSLOT, D))
    rowv = din("rowv", (128, NSLOT))
    iotaF = din("iotaF", (128, S))
    identM = din("identM", (128, 128))
    zers = din("zers", (128, 64))
    onesM = din("onesM", (128, 128))
    w1p = din("w1p", (NHC, 64, 32, 128), I32)
    w3p = din("w3p", (NHC, 64, 32, 128), I32)
    w2p = din("w2p", (NEC, 64, NHC, 128), I32)
    s1t = din("s1t", (NHC * 64, 128))
    s3t = din("s3t", (NHC * 64, 128))
    s2t = din("s2t", (NEC * 172, 128))
    outT = nc.dram_tensor("outT", [E, 1024], F32, kind="ExternalOutput").ap()
    DBG = globals().get("DEBUG_DUMPS", False)
    if DBG:
        dbg_qrT = nc.dram_tensor("dbg_qrT", [128, 1024], BF16, kind="ExternalOutput").ap()
        dbg_krT = nc.dram_tensor("dbg_krT", [128, S], BF16, kind="ExternalOutput").ap()
        dbg_v = nc.dram_tensor("dbg_v", [128, S], BF16, kind="ExternalOutput").ap()
        dbg_attnT = nc.dram_tensor("dbg_attnT", [128, 1024], BF16, kind="ExternalOutput").ap()
        dbg_x1 = nc.dram_tensor("dbg_x1", [128, 1024], F32, kind="ExternalOutput").ap()
        dbg_sim = nc.dram_tensor("dbg_sim", [128, SIMW], BF16, kind="ExternalOutput").ap()
        dbg_kh = nc.dram_tensor("dbg_kh", [128, S], BF16, kind="ExternalOutput").ap()
        dbg_qh = nc.dram_tensor("dbg_qh", [128, 1024], BF16, kind="ExternalOutput").ap()
        dbg_small = nc.dram_tensor("dbg_small", [128, 8], F32, kind="ExternalOutput").ap()
        dbg_mneg = nc.dram_tensor("dbg_mneg", [128, 1], F32, kind="ExternalOutput").ap()
        dbg_ht = nc.dram_tensor("dbg_ht", [128, 1024], BF16, kind="ExternalOutput").ap()
        dbg_nrm = nc.dram_tensor("dbg_nrm", [1, 4], F32, kind="ExternalOutput").ap()
        dbg_pn = nc.dram_tensor("dbg_pn", [1, 128], F32, kind="ExternalOutput").ap()
        dbg_x2pre = nc.dram_tensor("dbg_x2pre", [128, 1024], F32, kind="ExternalOutput").ap()
        dbg_ln2 = nc.dram_tensor("dbg_ln2", [1, 2048], F32, kind="ExternalOutput").ap()
        dbg_w2s = nc.dram_tensor("dbg_w2s", [128, 1024], BF16, kind="ExternalOutput").ap()

    def scr(name, shape, dtype):
        return nc.dram_tensor(name, list(shape), dtype, kind="Internal").ap()

    qrT_sc = scr("qrT_sc", (H, 128, 1024), BF16)
    krT_sc = scr("krT_sc", (KVH, 128, S), BF16)
    v_sc = scr("v_sc", (KVH, 128, S), BF16)
    attnT_sc = scr("attnT_sc", (H, 128, 1024), BF16)
    x1pre_sc = scr("x1pre_sc", (NEC, 128, 1024), F32)
    x1_sc = scr("x1_sc", (NEC, 128, 1024), F32)
    x1bf_sc = scr("x1bf_sc", (NEC, 128, 1024), BF16)
    hT_sc = scr("hT_sc", (NHC, 128, 1024), BF16)
    w2bf_sc = scr("w2bf_sc", (NEC, 128, HID), BF16)
    x2pre_sc = scr("x2pre_sc", (NEC, 128, 1024), F32)
    s1b_sc = scr("s1b_sc", (NHC * 64, 128), BF16)
    s3b_sc = scr("s3b_sc", (NHC * 64, 128), BF16)
    s2b_sc = scr("s2b_sc", (NEC * 172, 128), BF16)

    sc = Sched()
    P = sc.op
    ctx = ExitStack()
    sems = {}
    SB = lambda st, n, sh, dt: st.enter_context(nc.sbuf_tensor(n, list(sh), dt))
    PS = lambda st, n, sh, dt: st.enter_context(nc.psum_tensor(n, list(sh), dt))

    ident_b = SB(ctx, "ident_b", (128, 128), BF16)
    ones_b = SB(ctx, "ones_b", (128, 128), BF16)
    mneg_sb = SB(ctx, "mneg_sb", (128, 1), F32)
    dumv_sb = SB(ctx, "dumv_sb", (128, 64), F32)
    duma_sb = SB(ctx, "duma_sb", (128, 64), F32)
    dumPS = PS(ctx, "dumPS", (128, 512), F32)

    def SPV(waits=()):
        return P("vector", lambda e: e.tensor_copy(out=dumv_sb[:, 32:64],
                 in_=dumv_sb[:, 0:32]), list(waits))

    def SPA(waits=()):
        return P("scalar", lambda e: e.activation(duma_sb[:, 32:64],
                 duma_sb[:, 0:32], AF.Copy, bias=0.0, scale=1.0), list(waits))

    sc.dum["vector"] = lambda e: e.tensor_copy(out=dumv_sb[:, 32:64],
                                               in_=dumv_sb[:, 0:32])
    sc.dum["scalar"] = lambda e: e.activation(duma_sb[:, 32:64], duma_sb[:, 0:32],
                                              AF.Copy, bias=0.0, scale=1.0)

    P("vector", lambda e: e.memset(dumv_sb[:], 0.0), [])
    tdz = sc.dma("gpsimd", duma_sb[:], zers)
    sc.ops["scalar"].append(([("q_duma_sb", 16)], lambda e: e.activation(
        duma_sb[:, 32:64], duma_sb[:, 0:32], AF.Copy, bias=0.0, scale=1.0),
        ("ac", 1), False))
    sc.cnt["ac"] += 1
    t_ld = [sc.dma("gpsimd", ident_b[:], identM),
            sc.dma("gpsimd", ones_b[:], onesM)]

    p0 = ExitStack()
    sclf = SB(p0, "sclf", (128, 5504), F32)
    sclb = SB(p0, "sclb", (128, 5504), BF16)
    tok_scl = []
    prev = None
    for (src_, dst_, nrow) in ((s1t, s1b_sc, NHC * 64), (s3t, s3b_sc, NHC * 64),
                               (s2t, s2b_sc, NEC * 172)):
        nch = nrow // 128
        sv = src_.rearrange("(n p) c -> p n c", p=128)
        dv_ = dst_.rearrange("(n p) c -> p n c", p=128)
        tl = sc.dma("gpsimd", sclf[:, :nch * 128].rearrange("p (n c) -> p n c", c=128),
                    sv, [prev])
        tc_ = P("vector", lambda e, n=nch: e.tensor_copy(out=sclb[:, :n * 128],
                in_=sclf[:, :n * 128]), [tl, prev])
        ts_ = sc.dma("sync", dv_, sclb[:, :nch * 128].rearrange("p (n c) -> p n c", c=128),
                     [tc_])
        tok_scl.append(ts_)
        prev = ts_

    p1 = ExitStack()
    xres_sb = SB(p1, "xres_sb", (128, NEC * 2048), BF16)
    wtile_sb = [SB(p1, f"wtile{i}", (128, NEC * 128), BF16) for i in range(2)]
    panel_sb = SB(p1, "panel_sb", (128, 2048), BF16)
    tm_sb = SB(p1, "tm_sb", (128, 128), BF16)
    rope_a = SB(p1, "rope_a", (128, 128), BF16)
    rope_b = SB(p1, "rope_b", (128, 128), BF16)
    cos_sb = SB(p1, "cos_sb", (128, D), BF16)
    nsin_sb = SB(p1, "nsin_sb", (128, D), BF16)
    ftile_sb = SB(p1, "ftile_sb", (128, 128), BF16)
    sqp_sb = SB(p1, "sqp_sb", (128, 2048), BF16)
    nacc_sb = SB(p1, "nacc_sb", (1, 4), F32)
    colz_sb = SB(p1, "colz_sb", (1, 1), BF16)
    kvqPS = PS(p1, "kvqPS", (128, 2048), F32)
    trPS = [PS(p1, f"trPS{i}", (128, 1024), BF16) for i in range(2)]
    nrmPS = PS(p1, "nrmPS", (128, 512), F32)

    tz = P("vector", lambda e: e.memset(nacc_sb[:], 0.0), [])
    tz = SPV([tz])
    norm_state = {"acc": tz, "sq": None, "ps": None, "i": 0}
    pn_sb = SB(p1, "pn_sb", (1, 128), F32) if DBG else None
    if DBG:
        P("vector", lambda e: e.memset(pn_sb[:], 0.0), [])

    def panel_norm(width, mm_last):
        tsq_ = P("scalar", lambda e, w_=width: e.activation(sqp_sb[:, 0:w_],
                kvqPS[:, 0:w_], AF.Square, bias=0.0, scale=1.0),
                [norm_state["sq"]], pw=[mm_last])
        tsqf = SPA([tsq_])
        for c4 in range(width // 512):
            tnm = P("tensor", lambda e, c_=c4: e.matmul(nrmPS[0:1, :],
                   ones_b[:, 0:1], sqp_sb[:, c_ * 512:(c_ + 1) * 512],
                   start=True, stop=True), [tsqf, norm_state["ps"]] + t_ld,
                   fence=True)
            trd_ = P("vector", lambda e: e.tensor_reduce(out=nacc_sb[:, 1:2],
                    in_=nrmPS[0:1, :], axis=AXX, op=ALU.max),
                    [norm_state["acc"]], pw=[tnm])
            norm_state["ps"] = trd_
            s1_ = SPV([trd_])
            tmx_ = P("vector", lambda e: e.tensor_tensor(out=nacc_sb[:, 0:1],
                    in0=nacc_sb[:, 0:1], in1=nacc_sb[:, 1:2], op=ALU.max), [s1_])
            norm_state["acc"] = SPV([tmx_])
            if DBG and norm_state["i"] < 128:
                i_ = norm_state["i"]
                norm_state["i"] += 1
                P("vector", lambda e, ii=i_: e.tensor_copy(out=pn_sb[:, ii:ii + 1],
                  in_=nacc_sb[0:1, 1:2]), [norm_state["acc"]])
                norm_state["acc"] = SPV()
        norm_state["sq"] = norm_state["acc"]
        return tsq_
    wt_tok = [None, None]
    tr_tok = [None, None]
    tr_pe = [None, None]
    tmj = {"tm": None, "ra": None, "rb": None, "ft": None, "cs": None}
    panel_tok = []
    tp_i = [0]

    def transpose_to(src_ap, waits):
        x = tp_i[0] % 2
        tp_i[0] += 1
        t = P("tensor", lambda e, s_=src_ap, x_=x: e.transpose(trPS[x_][:, 0:128], s_,
              ident_b[:]), list(waits) + [tr_tok[x]] + t_ld, fence=True)
        tr_pe[x] = t
        return trPS[x][:, 0:128], t, x

    def evac_tr(dst_ap, psrc, pe_tok, x, extra=()):
        t = P("vector", lambda e, d_=dst_ap, s_=psrc: e.tensor_copy(out=d_, in_=s_),
              list(extra), pw=[pe_tok])
        tr_tok[x] = t
        return t

    for th in range(2):
        tx_toks = []
        for ec in range(NEC):
            src = xT.rearrange("(ne p) s -> p ne s", p=128)[:, ec, th * 2048:(th + 1) * 2048]
            tx_toks.append(sc.dma("gpsimd", xres_sb[:, ec * 2048:(ec + 1) * 2048], src,
                                  sc.snap() if (th > 0 and ec == 0) else []))
        for dkv in range(16):
            wb = dkv % 2
            twl = sc.dma("gpsimd", wtile_sb[wb][:], wkvTt[dkv], [wt_tok[wb]])
            mms = []
            for ec in range(NEC):
                for q4 in range(4):
                    mms.append(P("tensor", lambda e, ec_=ec, q_=q4, wb_=wb: e.matmul(
                        kvqPS[:, q_ * 512:(q_ + 1) * 512],
                        wtile_sb[wb_][:, ec_ * 128:(ec_ + 1) * 128],
                        xres_sb[:, ec_ * 2048 + q_ * 512: ec_ * 2048 + (q_ + 1) * 512],
                        start=(ec_ == 0), stop=(ec_ == NEC - 1)),
                        [twl, tx_toks[-1]] + (list(panel_tok) if ec == 0 else []),
                        fence=(ec == NEC - 1)))
            wt_tok[wb] = mms[-1]
            is_k = dkv < 8
            tsqtok = panel_norm(2048, mms[-1]) if is_k else None
            tev = P("vector", lambda e: e.tensor_copy(out=panel_sb[:], in_=kvqPS[:]),
                    [], pw=mms[-1:])
            panel_tok.clear()
            panel_tok.extend([tev] + ([tsqtok] if tsqtok else []))
            tev = SPV([tev])
            kvh = dkv if is_k else dkv - 8
            for tbl in range(16):
                tb = th * 16 + tbl
                ps1, tpe1, x1_ = transpose_to(panel_sb[:, tbl * 128:(tbl + 1) * 128],
                                              [tev])
                t_tm = evac_tr(tm_sb[:], ps1, tpe1, x1_, [tmj["tm"]])
                t_tm = SPV([t_tm])
                tmj["tm"] = None
                if is_k:
                    tcs = sc.dma("gpsimd", cos_sb[:], cosM[:, tb, :], [tmj["cs"]])
                    tns = sc.dma("gpsimd", nsin_sb[:], nsinM[:, tb, :], [tmj["cs"]])
                    ta = P("vector", lambda e: e.tensor_mul(out=rope_a[:], in0=tm_sb[:],
                          in1=cos_sb[:]), [t_tm, tcs, tmj.get("t2")])
                    tb1 = P("vector", lambda e: e.tensor_mul(out=rope_b[:, 0:64],
                           in0=tm_sb[:, 64:128], in1=nsin_sb[:, 0:64]), [t_tm, tns])
                    tb2 = P("vector", lambda e: e.tensor_mul(out=rope_b[:, 64:128],
                           in0=tm_sb[:, 0:64], in1=nsin_sb[:, 64:128]), [t_tm, tns])
                    sp_ = SPV([ta, tb1, tb2])
                    tsum = P("vector", lambda e: e.tensor_add(out=rope_a[:], in0=rope_a[:],
                            in1=rope_b[:]), [sp_])
                    tmj["cs"] = tsum
                    spt = SPV([tsum])
                    ps2, tpe2, x2_ = transpose_to(rope_a[:], [spt])
                    tmj["t2"] = tpe2
                    tfe = evac_tr(ftile_sb[:], ps2, tpe2, x2_, [tmj["ft"]])
                    tst = sc.dma("sync", krT_sc[kvh, :, tb * 128:(tb + 1) * 128],
                                 ftile_sb[:], [tfe])
                    tmj["ft"] = tst
                    tmj["tm"] = tsum
                else:
                    tst = sc.dma("sync", v_sc[kvh, :, tb * 128:(tb + 1) * 128],
                                 tm_sb[:], [t_tm])
                    tmj["tm"] = tst

    txq_toks = []
    for ec in range(NEC):
        src = xq.rearrange("(ne p) s -> p ne s", p=128)[:, ec, :]
        txq_toks.append(sc.dma("gpsimd", xres_sb[:, ec * 1024:(ec + 1) * 1024], src,
                               sc.snap() if ec == 0 else []))
    for dq in range(NEC):
        wb = dq % 2
        twl = sc.dma("gpsimd", wtile_sb[wb][:], wqTt[dq], [wt_tok[wb]])
        mms = []
        for ec in range(NEC):
            for q2 in range(2):
                mms.append(P("tensor", lambda e, ec_=ec, q_=q2, wb_=wb: e.matmul(
                    kvqPS[:, q_ * 512:(q_ + 1) * 512],
                    wtile_sb[wb_][:, ec_ * 128:(ec_ + 1) * 128],
                    xres_sb[:, ec_ * 1024 + q_ * 512: ec_ * 1024 + (q_ + 1) * 512],
                    start=(ec_ == 0), stop=(ec_ == NEC - 1)),
                    [twl, txq_toks[-1]] + (list(panel_tok) if ec == 0 else []),
                    fence=(ec == NEC - 1)))
        wt_tok[wb] = mms[-1]
        tsqtok = panel_norm(1024, mms[-1])
        tev = P("vector", lambda e: e.tensor_copy(out=panel_sb[:, 0:1024],
                in_=kvqPS[:, 0:1024]), [], pw=mms[-1:])
        panel_tok.clear()
        panel_tok.extend([tev, tsqtok])
        tev = SPV([tev])
        h = dq
        for s_ in range(NSLOT):
            ps1, tpe1, x1_ = transpose_to(panel_sb[:, s_ * 128:(s_ + 1) * 128], [tev])
            t_tm = evac_tr(tm_sb[:], ps1, tpe1, x1_, [tmj["tm"]])
            t_tm = SPV([t_tm])
            tcs = sc.dma("gpsimd", cos_sb[:], cosQ[:, s_, :], [tmj["cs"]])
            tns = sc.dma("gpsimd", nsin_sb[:], nsinQ[:, s_, :], [tmj["cs"]])
            ta = P("vector", lambda e: e.tensor_mul(out=rope_a[:], in0=tm_sb[:],
                  in1=cos_sb[:]), [t_tm, tcs, tmj.get("t2")])
            tb1 = P("vector", lambda e: e.tensor_mul(out=rope_b[:, 0:64],
                   in0=tm_sb[:, 64:128], in1=nsin_sb[:, 0:64]), [t_tm, tns])
            tb2 = P("vector", lambda e: e.tensor_mul(out=rope_b[:, 64:128],
                   in0=tm_sb[:, 0:64], in1=nsin_sb[:, 64:128]), [t_tm, tns])
            sp_ = SPV([ta, tb1, tb2])
            tsum = P("vector", lambda e: e.tensor_add(out=rope_a[:], in0=rope_a[:],
                    in1=rope_b[:]), [sp_])
            tmj["cs"] = tsum
            spt = SPV([tsum])
            ps2, tpe2, x2_ = transpose_to(rope_a[:], [spt])
            tmj["t2"] = tpe2
            tfe = evac_tr(ftile_sb[:], ps2, tpe2, x2_, [tmj["ft"]])
            tst = sc.dma("sync", qrT_sc[h, :, s_ * 128:(s_ + 1) * 128], ftile_sb[:],
                         [tfe])
            tmj["ft"] = tst

    tcc = P("vector", lambda e: e.tensor_copy(out=colz_sb[:], in_=nacc_sb[:, 0:1]),
           [norm_state["acc"]])
    spm = SPV([tcc])
    tmm_ = P("tensor", lambda e: e.matmul(nrmPS[:, 0:1], ones_b[0:1, :], colz_sb[:],
            start=True, stop=True), [spm, norm_state["ps"]] + t_ld, fence=True)
    tmn = P("vector", lambda e: e.tensor_scalar(out=mneg_sb[:], in0=nrmPS[:, 0:1],
           scalar1=-SQD, scalar2=None, op0=ALU.mult), [], pw=[tmm_])
    tmn = SPV([tmn])
    if DBG:
        sc.dma("sync", dbg_nrm, nacc_sb[0:1, :], [tmn])
        sc.dma("sync", dbg_pn, pn_sb[:], [tmn])

    p1_done = sc.snap()
    p1.close()
    p0.close()

    PHL = globals().get("PHASE_LIMIT", int(os.environ.get("KPHL", "99")))

    def finish():
        fin_ = sc.cnt.get("q_outT", 0)
        sc.ops["sync"].append(([("q_outT", fin_)], lambda e: e.nop(), None, False))
        for k in sc.cnt:
            sems[k] = ctx.enter_context(nc.semaphore(k))
        with nc.Block() as block:
            def runner(engname):
                def run(eng):
                    if engname == "tensor" and "q_ident_b" in sems:
                        eng.wait_ge(sems["q_ident_b"], 16)
                    for waits, fn, inc, fence in sc.ops[engname]:
                        for sem, val in waits:
                            eng.wait_ge(sems[sem], val)
                        inst = fn(eng)
                        if inc is not None:
                            if fence and engname == "tensor" and globals().get("FENCE_ON", True):
                                eng.matmul(dumPS[0:1, 0:1], ident_b[:, 0:1],
                                           ident_b[:, 0:1], start=True,
                                           stop=True).then_inc(sems[inc[0]], inc[1])
                            else:
                                inst.then_inc(sems[inc[0]], inc[1])
                return run
            block.tensor(runner("tensor"))
            block.vector(runner("vector"))
            block.scalar(runner("scalar"))
            block.gpsimd(runner("gpsimd"))
            block.sync(runner("sync"))
        ctx.close()
        return nc

    if PHL == 1:
        sc.dma("gpsimd", outT.rearrange("(h p) s -> h p s", p=128), qrT_sc[:],
               p1_done)
        return finish()

    p3 = ExitStack()
    b1_sb = SB(p3, "b1_sb", (128, H), F32)
    b1h_sb = SB(p3, "b1h_sb", (128, H), F32)
    b2_sb = SB(p3, "b2_sb", (128, H), F32)
    rowv_sb = SB(p3, "rowv_sb", (128, NSLOT), F32)
    iota_sb = SB(p3, "iota_sb", (128, S), F32)
    t_ld3 = [sc.dma("gpsimd", b1_sb[:], b1c),
             sc.dma("gpsimd", b2_sb[:], b2c),
             sc.dma("gpsimd", rowv_sb[:], rowv),
             sc.dma("gpsimd", iota_sb[:], iotaF)]
    tb1h = P("vector", lambda e: e.tensor_scalar(out=b1h_sb[:], in0=b1_sb[:],
        scalar1=0.5, scalar2=None, op0=ALU.mult), t_ld3)
    krT_kv = SB(p3, "krT_kv", (128, S), BF16)
    v_kv = SB(p3, "v_kv", (128, S), BF16)
    qrT_h = SB(p3, "qrT_h", (128, 1024), BF16)
    qhT_h = SB(p3, "qhT_h", (128, 1024), BF16)
    khT_h = SB(p3, "khT_h", (128, S), BF16)
    p1w = [SB(p3, f"p1w{i}", (128, D), BF16) for i in range(2)]
    p2w = [SB(p3, f"p2w{i}", (128, D), BF16) for i in range(2)]
    sim_sb = SB(p3, "sim_sb", (128, SIMW), BF16)
    cmt_sb = SB(p3, "cmt_sb", (128, NSLOT * 512), BF16)
    hsh_t = SB(p3, "hsh_t", (128, 512), BF16)
    hsh_u = SB(p3, "hsh_u", (128, 512), BF16)
    hsh_r = SB(p3, "hsh_r", (128, 512), BF16)
    z1_sb = SB(p3, "z1_sb", (128, 512), BF16)
    wsg_sb = SB(p3, "wsg_sb", (128, 512), BF16)
    e_sb = [SB(p3, f"e_sb{i}", (128, 512), BF16) for i in range(2)]
    pp_sb = [SB(p3, f"pp_sb{i}", (128, 512), BF16) for i in range(2)]
    pt_sb = [SB(p3, f"pt_sb{i}", (128, 512), BF16) for i in range(2)]
    srch_sb = SB(p3, "srch_sb", (128, 4096), BF16)
    tT_sb = SB(p3, "tT_sb", (128, NSLOT), F32)
    tC_sb = SB(p3, "tC_sb", (128, NSLOT), F32)
    cnt_sb = SB(p3, "cnt_sb", (128, NSLOT), F32)
    ok_sb = SB(p3, "ok_sb", (128, NSLOT), F32)
    den_sb = SB(p3, "den_sb", (128, 8), F32)
    den1_sb = SB(p3, "den1_sb", (128, 1), F32)
    rden_sb = SB(p3, "rden_sb", (128, 1), F32)
    attn_sb = SB(p3, "attn_sb", (128, D), BF16)
    attnT_sb = SB(p3, "attnT_sb", (128, D), BF16)
    simPS = [PS(p3, f"simPS{i}", (128, 512), F32) for i in range(2)]
    scPS = [PS(p3, f"scPS{i}", (128, 512), F32) for i in range(2)]
    ptPS = [PS(p3, f"ptPS{i}", (128, 1024), BF16) for i in range(2)]
    pvPS = PS(p3, "pvPS", (128, 512), F32)

    cm_toks = []
    for s_ in range(NSLOT):
        a0 = (SLOT_EXT[s_] - 4) * 128
        cm_toks.append(P("vector", lambda e, s__=s_, a0_=a0: e.tensor_scalar(
            out=cmt_sb[:, s__ * 512:(s__ + 1) * 512], in0=iota_sb[:, a0_:a0_ + 512],
            scalar1=rowv_sb[:, s__:s__ + 1], scalar2=NEGC, op0=ALU.is_gt, op1=ALU.mult),
            p1_done + t_ld3))

    st_tex = [None, None]
    st_tap = [None, None]
    st_ttr = [None, None]
    st_tevp = [None, None]
    st_pv = [None, None]
    st_sev = [None, None]
    st_smm = [None, None]
    last_ttra = [None]
    last_teva = [None]
    last_tdiv = [None]
    hb_tok = [None, None]
    qr_free = [None]
    kv_free = [None]
    kh_free = [None]

    hash_t6 = [None]

    def hash_seq(dst, src, nchunk, pb1, pb2, t_w1, t_w2, t_src, h):
        toks = []
        for c in range(nchunk):
            cs = slice(c * 512, (c + 1) * 512)
            t1 = P("tensor", lambda e, cs_=cs, pb_=pb1, s_=src: e.matmul(scPS[0][:],
                  pb_[:], s_[:, cs_], start=True, stop=True),
                  [t_w1, t_src] + toks[-1:] + [st_tex[0], st_tex[1], tb1h] + t_ld3)
            t2 = P("scalar", lambda e, h_=h: e.activation(hsh_t[:], scPS[0][:], AF.Tanh,
                  bias=b1h_sb[:, h_:h_ + 1], scale=0.5), t_ld, pw=[t1])
            t3 = P("vector", lambda e, h_=h: e.scalar_tensor_tensor(out=hsh_u[:],
                  in0=scPS[0][:], scalar=b1_sb[:, h_:h_ + 1], in1=hsh_t[:],
                  op0=ALU.add, op1=ALU.mult), [SPA([t2])])
            t4 = P("vector", lambda e, h_=h: e.scalar_tensor_tensor(out=hsh_r[:],
                  in0=scPS[0][:], scalar=b1_sb[:, h_:h_ + 1], in1=hsh_u[:],
                  op0=ALU.add, op1=ALU.add), [SPV([t3])])
            t5 = P("vector", lambda e, cs_=cs, s_=src: e.scalar_tensor_tensor(
                  out=z1_sb[:], in0=hsh_r[:], scalar=0.5, in1=s_[:, cs_],
                  op0=ALU.mult, op1=ALU.add), [SPV([t4]), hash_t6[0]])
            t6 = P("tensor", lambda e, pb_=pb2: e.matmul(scPS[1][:], pb_[:], z1_sb[:],
                  start=True, stop=True), [SPV([t5]), t_w2, st_tex[1]], fence=True)
            hash_t6[0] = t6
            t7 = P("vector", lambda e, h_=h: e.scalar_tensor_tensor(out=wsg_sb[:],
                  in0=scPS[1][:], scalar=b2_sb[:, h_:h_ + 1], in1=z1_sb[:],
                  op0=ALU.add, op1=ALU.add), [], pw=[t6])
            t8 = P("vector", lambda e, cs_=cs, d_=dst: e.tensor_scalar(out=d_[:, cs_],
                  in0=wsg_sb[:], scalar1=0.0, scalar2=0.5, op0=ALU.is_ge,
                  op1=ALU.subtract), [SPV([t7])])
            toks.append(t8)
        return toks

    for kvh in range(KVH):
        tkr = sc.dma("gpsimd", krT_kv[:], krT_sc[kvh].rearrange("p s -> p s"),
                     p1_done + [kv_free[0]])
        tv = sc.dma("gpsimd", v_kv[:], v_sc[kvh].rearrange("p s -> p s"),
                    p1_done + [kv_free[0]])
        for g in range(GRP):
            h = kvh * GRP + g
            hb = h % 2
            tw1 = sc.dma("gpsimd", p1w[hb][:], hp1[h], [hb_tok[hb]])
            tw2 = sc.dma("gpsimd", p2w[hb][:], hp2[h], [hb_tok[hb]])
            tqr = sc.dma("gpsimd", qrT_h[:], qrT_sc[h].rearrange("p s -> p s"),
                         p1_done + [qr_free[0]])
            kh_toks = hash_seq(khT_h, krT_kv, 8, p1w[hb], p2w[hb], tw1, tw2, tkr, h)
            qh_toks = hash_seq(qhT_h, qrT_h, 2, p1w[hb], p2w[hb], tw1, tw2, tqr, h)
            hb_tok[hb] = qh_toks[-1]
            hash_done = [kh_toks[-1], qh_toks[-1]]
            if kh_free[0] is not None:
                pass
            hsp = SPV(hash_done)
            for s_ in range(NSLOT):
                ext = SLOT_EXT[s_]
                off = SLOT_OFF[s_] * 128
                nch = ext * 128 // 512
                for c in range(nch):
                    x = c % 2
                    tmm = P("tensor", lambda e, x_=x, s__=s_, c_=c: e.matmul(
                        simPS[x_][:], qhT_h[:, s__ * 128:(s__ + 1) * 128],
                        khT_h[:, c_ * 512:(c_ + 1) * 512], start=True, stop=True),
                        [hsp, st_sev[x], kh_free[0]], fence=True)
                    st_smm[x] = tmm
                    if c == nch - 1:
                        tev = P("vector", lambda e, x_=x, s__=s_, c_=c, o_=off:
                            e.tensor_add(out=sim_sb[:, o_ + c_ * 512:o_ + (c_ + 1) * 512],
                                         in0=simPS[x_][:],
                                         in1=cmt_sb[:, s__ * 512:(s__ + 1) * 512]),
                            cm_toks, pw=[tmm])
                    else:
                        tev = P("vector", lambda e, x_=x, c_=c, o_=off: e.tensor_copy(
                            out=sim_sb[:, o_ + c_ * 512:o_ + (c_ + 1) * 512],
                            in_=simPS[x_][:]), [], pw=[tmm])
                    st_sev[x] = tev
            P("vector", lambda e: e.memset(tT_sb[:], -32.0), [])
            SPV()
            for w_ in (32.0, 16.0, 8.0, 4.0, 2.0, 1.0, 0.5):
                P("vector", lambda e, w__=w_: e.tensor_scalar(
                    out=tC_sb[:], in0=tT_sb[:], scalar1=w__, scalar2=None,
                    op0=ALU.add), [])
                SPV()
                for s_ in range(NSLOT):
                    P("vector", lambda e, s__=s_, o_=SLOT_OFF[s_] * 128,
                      n_=SLOT_EXT[s_] * 128: e.tensor_scalar(
                        out=srch_sb[:, 0:n_], in0=sim_sb[:, o_:o_ + n_],
                        scalar1=tC_sb[:, s__:s__ + 1], scalar2=0.0, op0=ALU.is_ge,
                        op1=ALU.add, accum_out=cnt_sb[:, s__:s__ + 1]), [])
                SPV()
                P("vector", lambda e: e.tensor_scalar(
                    out=ok_sb[:], in0=cnt_sb[:], scalar1=KEEP - 0.5, scalar2=None,
                    op0=ALU.is_ge), [])
                SPV()
                P("vector", lambda e, w__=w_: e.scalar_tensor_tensor(
                    out=tT_sb[:], in0=ok_sb[:], scalar=w__, in1=tT_sb[:],
                    op0=ALU.mult, op1=ALU.add), [])
                SPV()
            for s_ in range(NSLOT):
                ext = SLOT_EXT[s_]
                off = SLOT_OFF[s_] * 128
                nch = ext * 128 // 512
                for c in range(nch):
                    x = c % 2
                    tmm = P("tensor", lambda e, x_=x, s__=s_, c_=c: e.matmul(
                        scPS[x_][:], qrT_h[:, s__ * 128:(s__ + 1) * 128],
                        krT_kv[:, c_ * 512:(c_ + 1) * 512], start=True, stop=True),
                        [tqr, tkr, st_tex[x]] + (hash_done if c < 2 else []),
                        fence=True)
                    tex = P("scalar", lambda e, x_=x: e.activation(e_sb[x_][:],
                          scPS[x_][:], AF.Exp, bias=mneg_sb[:], scale=SQD),
                          [tmn, st_tap[x]], pw=[tmm])
                    st_tex[x] = tex
                    tap = P("vector", lambda e, x_=x, s__=s_, c_=c, o_=off:
                        e.scalar_tensor_tensor(out=pp_sb[x_][:],
                            in0=sim_sb[:, o_ + c_ * 512:o_ + (c_ + 1) * 512],
                            scalar=tT_sb[:, s__:s__ + 1], in1=e_sb[x_][:],
                            op0=ALU.is_ge, op1=ALU.mult,
                            accum_out=den_sb[:, c_:c_ + 1]),
                        [SPA([tex]), st_ttr[x]])
                    st_tap[x] = tap
                    tapf = SPV([tap])
                    ttrs = None
                    for j in range(4):
                        ttrs = P("tensor", lambda e, x_=x, j_=j: e.transpose(
                            ptPS[x_][:, j_ * 128:(j_ + 1) * 128],
                            pp_sb[x_][:, j_ * 128:(j_ + 1) * 128], ident_b[:]),
                            [tapf, st_tevp[x]] + ([last_teva[0]] if last_teva[0] else []),
                            fence=(j_ == 3) if False else (j == 3))
                    st_ttr[x] = ttrs
                    tevp = P("vector", lambda e, x_=x: e.tensor_copy(out=pt_sb[x_][:],
                            in_=ptPS[x_][:, 0:512]), [st_pv[x]], pw=[ttrs])
                    st_tevp[x] = tevp
                    tevpf = SPV([tevp])
                    pv_last = None
                    for j in range(4):
                        kb = c * 4 + j
                        pv_last = P("tensor", lambda e, x_=x, j_=j, kb_=kb,
                            last_=nch * 4 - 1: e.matmul(
                            pvPS[:, 0:128], pt_sb[x_][:, j_ * 128:(j_ + 1) * 128],
                            v_kv[:, kb_ * 128:(kb_ + 1) * 128],
                            start=(kb_ == 0), stop=(kb_ == last_)),
                            [tevpf, tv] + ([last_tdiv[0]] if kb == 0 else []),
                            fence=(kb == nch * 4 - 1))
                    st_pv[x] = pv_last
                sd0 = SPV([st_tap[0], st_tap[1]])
                td1 = P("vector", lambda e, n_=nch: e.tensor_reduce(out=den1_sb[:],
                       in_=den_sb[:, 0:n_], axis=AXX, op=ALU.add), [sd0])
                sd1 = SPV([td1])
                trd = P("vector", lambda e: e.reciprocal(out=rden_sb[:], in_=den1_sb[:]),
                       [sd1])
                sd2 = SPV([trd])
                tdiv = P("vector", lambda e: e.tensor_scalar(out=attn_sb[:],
                        in0=pvPS[:, 0:128], scalar1=rden_sb[:], scalar2=None,
                        op0=ALU.mult), [sd2, last_ttra[0]],
                        pw=[st_pv[0], st_pv[1]])
                last_tdiv[0] = tdiv
                sd3 = SPV([tdiv])
                ttra = P("tensor", lambda e: e.transpose(ptPS[0][:, 0:128], attn_sb[:],
                        ident_b[:]), [sd3, st_tevp[0]], fence=True)
                last_ttra[0] = ttra
                st_ttr[0] = ttra
                teva = P("vector", lambda e: e.tensor_copy(out=attnT_sb[:],
                        in_=ptPS[0][:, 0:128]), [last_teva[0]], pw=[ttra])
                tstA = sc.dma("sync", attnT_sc[h, :, s_ * 128:(s_ + 1) * 128],
                              attnT_sb[:], [teva])
                last_teva[0] = tstA
                st_tevp[0] = teva
            if DBG and h == 0:
                sc.dma("sync", dbg_small, tT_sb[:], [last_tdiv[0]])
                sc.dma("sync", dbg_mneg, mneg_sb[:], [tmn])
                sc.dma("sync", dbg_kh, khT_h[:], [kh_toks[-1]])
                sc.dma("sync", dbg_qh, qhT_h[:], [qh_toks[-1]])
                sc.dma("sync", dbg_sim, sim_sb[:], [last_tdiv[0]])
            qr_free[0] = last_tdiv[0]
            kh_free[0] = st_smm[0] if st_smm[0] else None
            kh_free[0] = (st_smm[0] if (st_smm[1] is None or
                          (st_smm[0] and st_smm[0][1] > st_smm[1][1])) else st_smm[1])
        kv_free[0] = last_tdiv[0]

    p3_done = sc.snap()
    p3.close()

    if PHL == 3:
        sc.dma("gpsimd", outT.rearrange("(h p) s -> h p s", p=128), attnT_sc[:],
               p3_done)
        return finish()

    def ln_block(tag, st, mm_schedule, res_src, pre_sc_, out_fp_sc, out_bf_sc,
                 out_direct, deps0):
        pass

    p5 = ExitStack()
    att_res = SB(p5, "att_res", (128, NEC * 512), BF16)
    wo_sb = [SB(p5, f"wo_sb{i}", (128, NEC * 128), BF16) for i in range(2)]
    x1pre_sb = SB(p5, "x1pre_sb", (128, 512), F32)
    xr_sb = SB(p5, "xr_sb", (128, 512), F32)
    xb_sb = SB(p5, "xb_sb", (128, 512), BF16)
    sq5_sb = SB(p5, "sq5_sb", (128, 512), BF16)
    mu5_sb = SB(p5, "mu5_sb", (1, 512), F32)
    ex25_sb = SB(p5, "ex25_sb", (1, 512), F32)
    var5_sb = SB(p5, "var5_sb", (1, 512), F32)
    rstd5_sb = SB(p5, "rstd5_sb", (1, 512), F32)
    mu5b_sb = SB(p5, "mu5b_sb", (1, 512), BF16)
    rstd5b_sb = SB(p5, "rstd5b_sb", (1, 512), BF16)
    xn_sb = SB(p5, "xn_sb", (128, 512), F32)
    acc1_sb = SB(p5, "acc1_sb", (1, 512), F32)
    acc2_sb = SB(p5, "acc2_sb", (1, 512), F32)
    woPS = PS(p5, "woPS", (128, 512), F32)
    stPS = PS(p5, "stPS", (128, 512), F32)
    st2PS = PS(p5, "st2PS", (128, 512), F32)
    muPS = PS(p5, "muPS", (128, 512), F32)
    rsPS = PS(p5, "rsPS", (128, 512), F32)

    wo_tok = [None, None]
    stat_free = [None]
    stat_acc = {"a1": None, "a2": None}
    prev_xn_store = [None]
    prev_bc5 = [None]
    for tc in range(2):
        ta = sc.dma("gpsimd", att_res[:],
                    attnT_sc[:, :, tc * 512:(tc + 1) * 512].rearrange("h p t -> p h t"),
                    p3_done + [stat_free[0]])
        stat_mm = []
        prev_str = None
        prev_res = None
        prev_cb = None
        for ec in range(NEC):
            wb = ec % 2
            twl = sc.dma("gpsimd", wo_sb[wb][:], woTt[ec], [wo_tok[wb]])
            txr = sc.dma("gpsimd", xr_sb[:],
                         xq.rearrange("(ne p) s -> p ne s", p=128)[:, ec,
                         tc * 512:(tc + 1) * 512], [prev_res])
            mms = []
            for dc in range(NEC):
                mms.append(P("tensor", lambda e, wb_=wb, dc_=dc: e.matmul(
                    woPS[:], wo_sb[wb_][:, dc_ * 128:(dc_ + 1) * 128],
                    att_res[:, dc_ * 512:(dc_ + 1) * 512],
                    start=(dc_ == 0), stop=(dc_ == NEC - 1)),
                    [twl, ta] + ([prev_res] if dc == 0 else []),
                    fence=(dc == NEC - 1)))
            wo_tok[wb] = mms[-1]
            tres = P("vector", lambda e: e.tensor_add(out=x1pre_sb[:], in0=woPS[:],
                    in1=xr_sb[:]), [mms[-1], txr, prev_str])
            prev_res = tres
            tstr = sc.dma("sync", x1pre_sc[ec, :, tc * 512:(tc + 1) * 512], x1pre_sb[:],
                          [tres])
            prev_str = tstr
            tcb = P("vector", lambda e: e.tensor_copy(out=xb_sb[:], in_=x1pre_sb[:]),
                   [SPV([tres]), prev_cb])
            tsq = P("vector", lambda e: e.tensor_mul(out=sq5_sb[:], in0=xb_sb[:],
                   in1=xb_sb[:]), [SPV([tcb])])
            m1 = P("tensor", lambda e: e.matmul(stPS[0:1, :], ones_b[:, 0:1],
                  xb_sb[:], start=True, stop=True),
                  [SPV([tcb]), stat_acc["a1"]] + ([stat_free[0]] if ec == 0 else []),
                  fence=True)
            a1 = P("vector", lambda e, ec_=ec: (
                  e.tensor_copy(out=acc1_sb[:], in_=stPS[0:1, :]) if ec_ == 0 else
                  e.tensor_add(out=acc1_sb[:], in0=acc1_sb[:], in1=stPS[0:1, :])),
                  [SPV()], pw=[m1])
            stat_acc["a1"] = SPV([a1])
            m2 = P("tensor", lambda e: e.matmul(st2PS[0:1, :], ones_b[:, 0:1],
                  sq5_sb[:], start=True, stop=True),
                  [SPV([tsq]), stat_acc["a2"]] + ([stat_free[0]] if ec == 0 else []),
                  fence=True)
            a2 = P("vector", lambda e, ec_=ec: (
                  e.tensor_copy(out=acc2_sb[:], in_=st2PS[0:1, :]) if ec_ == 0 else
                  e.tensor_add(out=acc2_sb[:], in0=acc2_sb[:], in1=st2PS[0:1, :])),
                  [SPV()], pw=[m2])
            stat_acc["a2"] = SPV([a2])
            stat_mm = [stat_acc["a1"], stat_acc["a2"]]
            prev_cb = m2
        tmu = P("vector", lambda e: e.tensor_scalar(out=mu5_sb[:], in0=acc1_sb[:],
               scalar1=1.0 / E, scalar2=None, op0=ALU.mult), stat_mm)
        tex2 = P("vector", lambda e: e.tensor_scalar(out=ex25_sb[:],
                in0=acc2_sb[:], scalar1=1.0 / E, scalar2=None, op0=ALU.mult),
                stat_mm)
        s_ = SPV([tmu, tex2])
        P("vector", lambda e: e.tensor_mul(out=var5_sb[:], in0=mu5_sb[:],
          in1=mu5_sb[:]), [s_])
        s_ = SPV()
        P("vector", lambda e: e.tensor_sub(out=var5_sb[:], in0=ex25_sb[:],
          in1=var5_sb[:]), [s_])
        s_ = SPV()
        tve = P("vector", lambda e: e.tensor_scalar(out=var5_sb[:],
               in0=var5_sb[:], scalar1=1e-5, scalar2=None, op0=ALU.add), [s_])
        s_ = SPV([tve])
        tln = P("scalar", lambda e: e.activation(rstd5_sb[:], var5_sb[:], AF.Ln,
               bias=0.0, scale=1.0), [s_])
        sa_ = SPA([tln])
        tsx = P("scalar", lambda e: e.activation(rstd5_sb[:], rstd5_sb[:], AF.Exp,
               bias=0.0, scale=-0.5), [sa_])
        sa2_ = SPA([tsx])
        tcbs = P("vector", lambda e: e.tensor_copy(out=mu5b_sb[:], in_=mu5_sb[:]),
                [sa2_, prev_bc5[0]])
        tcbs = P("vector", lambda e: e.tensor_copy(out=rstd5b_sb[:], in_=rstd5_sb[:]),
                [tcbs])
        tcbs = SPV([tcbs])
        tbmu = P("tensor", lambda e: e.matmul(muPS[:], ones_b[0:1, :], mu5b_sb[:],
                start=True, stop=True), [tcbs, stat_free[0]], fence=True)
        tbrs = P("tensor", lambda e: e.matmul(rsPS[:], ones_b[0:1, :], rstd5b_sb[:],
                start=True, stop=True), [tcbs, stat_free[0]], fence=True)
        prev_bc5[0] = tbrs
        prev_n1 = None
        prev_sb = None
        prev_sc2 = None
        for ec in range(NEC):
            tld = sc.dma("gpsimd", x1pre_sb[:], x1pre_sc[ec, :, tc * 512:(tc + 1) * 512],
                         [prev_str, prev_n1])
            ts1 = P("vector", lambda e: e.tensor_sub(out=xn_sb[:], in0=x1pre_sb[:],
                   in1=muPS[:]), [tld, prev_sb, prev_xn_store[0]], pw=[tbmu, tbrs])
            prev_n1 = ts1
            ts2 = P("vector", lambda e: e.tensor_mul(out=xn_sb[:], in0=xn_sb[:],
                   in1=rsPS[:]), [SPV([ts1]), tbrs])
            tso = sc.dma("sync", x1_sc[ec, :, tc * 512:(tc + 1) * 512], xn_sb[:], [ts2])
            prev_xn_store[0] = tso
            tcb2 = P("vector", lambda e: e.tensor_copy(out=xb_sb[:], in_=xn_sb[:]),
                    [SPV([ts2]), prev_sc2])
            prev_sb = tcb2
            tso2 = sc.dma("sync", x1bf_sc[ec, :, tc * 512:(tc + 1) * 512], xb_sb[:],
                          [tcb2])
            prev_sc2 = tso2
            prev_str = tso
        stat_free[0] = prev_n1

    p5_done = sc.snap()
    p5.close()

    if PHL == 5:
        sc.dma("gpsimd", outT.rearrange("(ec p) s -> ec p s", p=128), x1_sc[:],
               p5_done)
        return finish()

    p6 = ExitStack()
    x1t_sb = SB(p6, "x1t_sb", (128, NEC * 1024), BF16)
    pk_sb = [SB(p6, f"pk_sb{i}", (128, 4096), I32) for i in range(2)]
    nib_sb = [SB(p6, f"nib_sb{i}", (128, 4096), I32) for i in range(2)]
    wde_sb = [SB(p6, f"wde_sb{i}", (128, 4096), BF16) for i in range(2)]
    wsc_sb = [SB(p6, f"wsc_sb{i}", (128, 4096), BF16) for i in range(2)]
    sil_sb = SB(p6, "sil_sb", (128, 1024), BF16)
    ht_sb = SB(p6, "ht_sb", (128, 1024), BF16)
    aPS = PS(p6, "aPS", (128, 1024), F32)
    bPS = PS(p6, "bPS", (128, 1024), F32)

    tx1 = [sc.dma("gpsimd", x1t_sb[:, ec * 1024:(ec + 1) * 1024],
                  x1bf_sc[ec].rearrange("p s -> p s"), p5_done) for ec in range(NEC)]

    dq_state = [{"tn": None, "tca": None, "tmu": None, "mm": None, "tl": None}
                for _ in range(2)]

    def dequant(idx, pk_src, scb_src, ncol=4096):
        st = dq_state[idx]
        pk = pk_sb[idx]
        nib = nib_sb[idx]
        wde = wde_sb[idx]
        wsc = wsc_sb[idx]
        tl0 = sc.dma("gpsimd",
                     pk[0:64, 0:ncol].rearrange("p (a b) -> p a b", b=128), pk_src,
                     [st["tn"]])
        tl = sc.dma("gpsimd",
                    pk[64:128, 0:ncol].rearrange("p (a b) -> p a b", b=128), pk_src,
                    [st["tn"], tl0])
        ts0 = sc.dma("gpsimd",
                     wsc[0:64, 0:ncol].rearrange("p (a b) -> p a b", b=128), scb_src[0],
                     [st["tmu"]])
        tsc_ = sc.dma("gpsimd",
                      wsc[64:128, 0:ncol].rearrange("p (a b) -> p a b", b=128),
                      scb_src[1], [st["tmu"], ts0])
        tl = (tl[0], tl[1], tl[2])
        tn1 = P("vector", lambda e, n_=ncol: e.tensor_scalar(out=nib[0:64, 0:n_],
               in0=pk[0:64, 0:n_], scalar1=24.0, scalar2=28.0,
               op0=ALU.logical_shift_left, op1=ALU.arith_shift_right),
               [tl, st["tca"]])
        tn2 = P("vector", lambda e, n_=ncol: e.tensor_scalar(out=nib[64:128, 0:n_],
               in0=pk[64:128, 0:n_], scalar1=28.0, scalar2=28.0,
               op0=ALU.logical_shift_left, op1=ALU.arith_shift_right),
               [tl, st["tca"]])
        tca = P("scalar", lambda e, n_=ncol: e.activation(wde[:, 0:n_], nib[:, 0:n_],
               AF.Copy, bias=0.0, scale=1.0), [SPV([tn1, tn2]), st["mm"]])
        tmu = P("vector", lambda e, n_=ncol: e.tensor_mul(out=wde[:, 0:n_],
               in0=wde[:, 0:n_], in1=wsc[:, 0:n_]), [SPA([tca]), tsc_] + tok_scl)
        st["tn"], st["tca"], st["tmu"] = tn2, tca, tmu
        return SPV([tmu])

    def pk_rep(arr, i0):
        return arr[i0]

    def sc_rep(scb, i0, nj):
        v = scb.rearrange("(i j) h -> i j h", j=2 * nj)[i0]
        v4 = v.rearrange("(jj nib) h -> nib jj h", nib=2)
        return (v4[0].partition_broadcast(64), v4[1].partition_broadcast(64))

    prev_si = [None]
    prev_ht = [None]
    prev_hstore = [None]
    for hc in range(NHC):
        idx = hc % 2
        t1w = dequant(idx, pk_rep(w1p, hc), sc_rep(s1b_sc, hc, 32))
        mm1 = None
        for J in range(NEC):
            for tk in range(2):
                mm1 = P("tensor", lambda e, J_=J, tk_=tk, i_=idx: e.matmul(
                    aPS[:, tk_ * 512:(tk_ + 1) * 512],
                    wde_sb[i_][:, J_ * 128:(J_ + 1) * 128],
                    x1t_sb[:, J_ * 1024 + tk_ * 512: J_ * 1024 + (tk_ + 1) * 512],
                    start=(J_ == 0), stop=(J_ == NEC - 1)),
                    [t1w, tx1[-1]] + ([prev_si[0]] if J == 0 else []),
                    fence=(J == NEC - 1 and tk == 1))
        dq_state[idx]["mm"] = mm1
        t3w = dequant(idx, pk_rep(w3p, hc), sc_rep(s3b_sc, hc, 32))
        mm3 = None
        for J in range(NEC):
            for tk in range(2):
                mm3 = P("tensor", lambda e, J_=J, tk_=tk, i_=idx: e.matmul(
                    bPS[:, tk_ * 512:(tk_ + 1) * 512],
                    wde_sb[i_][:, J_ * 128:(J_ + 1) * 128],
                    x1t_sb[:, J_ * 1024 + tk_ * 512: J_ * 1024 + (tk_ + 1) * 512],
                    start=(J_ == 0), stop=(J_ == NEC - 1)),
                    [t3w, tx1[-1]] + ([prev_ht[0]] if J == 0 else []),
                    fence=(J == NEC - 1 and tk == 1))
        dq_state[idx]["mm"] = mm3
        tsi = P("scalar", lambda e: e.activation(sil_sb[:], aPS[:], AF.Silu, bias=0.0,
               scale=1.0), [mm3, prev_ht[0]], pw=[mm1])
        prev_si[0] = tsi
        tht = P("vector", lambda e: e.tensor_mul(out=ht_sb[:], in0=sil_sb[:],
               in1=bPS[:]), [SPA([tsi]), prev_hstore[0]], pw=[mm3])
        prev_ht[0] = tht
        prev_hstore[0] = sc.dma("sync", hT_sc[hc].rearrange("p s -> p s"), ht_sb[:],
                                [tht])

    for oc in range(NEC):
        for jg, (j0, nj) in enumerate(((0, 32), (32, 32), (64, 22))):
            idx = (oc * 3 + jg) % 2
            pk_src = w2p[oc][:, j0:j0 + nj, :]
            v = s2b_sc.rearrange("(i j) h -> i j h", j=172)[oc][2 * j0:2 * (j0 + nj)]
            v4 = v.rearrange("(jj nib) h -> nib jj h", nib=2)
            sc_src = (v4[0].partition_broadcast(64), v4[1].partition_broadcast(64))
            tmu = dequant(idx, pk_src, sc_src, nj * 128)
            tws = sc.dma("sync",
                         w2bf_sc[oc, :, j0 * 128:(j0 + nj) * 128].rearrange(
                             "p (a b) -> p a b", b=128),
                         wde_sb[idx][:, 0:nj * 128].rearrange("p (a b) -> p a b", b=128),
                         [tmu], sem=f"w2bf{idx}")
            dq_state[idx]["mm"] = tws

    p6_done = sc.snap()
    p6.close()

    if PHL == 6:
        sc.dma("gpsimd", outT.rearrange("(ec p) s -> ec p s", p=128), x1bf_sc[:],
               p6_done + sc.snap())
        return finish()

    p7 = ExitStack()
    hT_res = SB(p7, "hT_res", (128, NHC * 512), BF16)
    w2_sb = [SB(p7, f"w2_sb{i}", (128, HID), BF16) for i in range(2)]
    x2pre_sb = SB(p7, "x2pre_sb", (128, 512), F32)
    x1r_sb = SB(p7, "x1r_sb", (128, 512), F32)
    x2b_sb = SB(p7, "x2b_sb", (128, 512), BF16)
    sq7_sb = SB(p7, "sq7_sb", (128, 512), BF16)
    mu7_sb = SB(p7, "mu7_sb", (1, 512), F32)
    ex27_sb = SB(p7, "ex27_sb", (1, 512), F32)
    var7_sb = SB(p7, "var7_sb", (1, 512), F32)
    rstd7_sb = SB(p7, "rstd7_sb", (1, 512), F32)
    mu7b_sb = SB(p7, "mu7b_sb", (1, 512), BF16)
    rstd7b_sb = SB(p7, "rstd7b_sb", (1, 512), BF16)
    x2n_sb = SB(p7, "x2n_sb", (128, 512), F32)
    acc17_sb = SB(p7, "acc17_sb", (1, 512), F32)
    acc27_sb = SB(p7, "acc27_sb", (1, 512), F32)
    fPS = PS(p7, "fPS", (128, 512), F32)
    s7PS = PS(p7, "s7PS", (128, 512), F32)
    s72PS = PS(p7, "s72PS", (128, 512), F32)
    mu7PS = PS(p7, "mu7PS", (128, 512), F32)
    rs7PS = PS(p7, "rs7PS", (128, 512), F32)

    w2_tok = [None, None]
    half_mm_last = [None]
    stat_free7 = [None]
    stat_acc7 = {"a1": None, "a2": None}
    prev_out_store = [None]
    prev_bc7 = [None]
    for half in range(2):
        th_toks = []
        for J2 in range(NHC):
            th_toks.append(sc.dma("gpsimd", hT_res[:, J2 * 512:(J2 + 1) * 512],
                           hT_sc[J2, :, half * 512:(half + 1) * 512],
                           p6_done + [half_mm_last[0]]))
        stat_mm = []
        prev_str = None
        prev_res = None
        prev_cb = None
        for oc in range(NEC):
            wb = oc % 2
            tw_ready = sc.dma("gpsimd", w2_sb[wb][:],
                              w2bf_sc[oc].rearrange("p s -> p s"),
                              p6_done + [w2_tok[wb]])
            mms = None
            for J2 in range(NHC):
                mms = P("tensor", lambda e, wb_=wb, J_=J2: e.matmul(
                    fPS[:], w2_sb[wb_][:, J_ * 128:(J_ + 1) * 128],
                    hT_res[:, J_ * 512:(J_ + 1) * 512],
                    start=(J_ == 0), stop=(J_ == NHC - 1)),
                    [tw_ready, th_toks[-1]] + ([prev_res] if J2 == 0 else []),
                    fence=(J2 == NHC - 1))
            w2_tok[wb] = mms
            half_mm_last[0] = mms
            txr = sc.dma("gpsimd", x1r_sb[:], x1_sc[oc, :, half * 512:(half + 1) * 512],
                         [prev_res])
            tres = P("vector", lambda e: e.tensor_add(out=x2pre_sb[:], in0=fPS[:],
                    in1=x1r_sb[:]), [txr, prev_str], pw=[mms])
            prev_res = tres
            tstr = sc.dma("sync", x2pre_sc[oc, :, half * 512:(half + 1) * 512],
                          x2pre_sb[:], [tres])
            prev_str = tstr
            tcb = P("vector", lambda e: e.tensor_copy(out=x2b_sb[:], in_=x2pre_sb[:]),
                   [SPV([tres]), prev_cb])
            tsq = P("vector", lambda e: e.tensor_mul(out=sq7_sb[:], in0=x2b_sb[:],
                   in1=x2b_sb[:]), [SPV([tcb])])
            m1 = P("tensor", lambda e: e.matmul(s7PS[0:1, :], ones_b[:, 0:1],
                  x2b_sb[:], start=True, stop=True),
                  [SPV([tcb]), stat_acc7["a1"]] + ([stat_free7[0]] if oc == 0 else []),
                  fence=True)
            a1 = P("vector", lambda e, oc_=oc: (
                  e.tensor_copy(out=acc17_sb[:], in_=s7PS[0:1, :]) if oc_ == 0 else
                  e.tensor_add(out=acc17_sb[:], in0=acc17_sb[:], in1=s7PS[0:1, :])),
                  [SPV()], pw=[m1])
            stat_acc7["a1"] = SPV([a1])
            m2 = P("tensor", lambda e: e.matmul(s72PS[0:1, :], ones_b[:, 0:1],
                  sq7_sb[:], start=True, stop=True),
                  [SPV([tsq]), stat_acc7["a2"]] + ([stat_free7[0]] if oc == 0 else []),
                  fence=True)
            a2 = P("vector", lambda e, oc_=oc: (
                  e.tensor_copy(out=acc27_sb[:], in_=s72PS[0:1, :]) if oc_ == 0 else
                  e.tensor_add(out=acc27_sb[:], in0=acc27_sb[:], in1=s72PS[0:1, :])),
                  [SPV()], pw=[m2])
            stat_acc7["a2"] = SPV([a2])
            stat_mm = [stat_acc7["a1"], stat_acc7["a2"]]
            prev_cb = m2
        tmu = P("vector", lambda e: e.tensor_scalar(out=mu7_sb[:], in0=acc17_sb[:],
               scalar1=1.0 / E, scalar2=None, op0=ALU.mult), stat_mm)
        tex2 = P("vector", lambda e: e.tensor_scalar(out=ex27_sb[:],
                in0=acc27_sb[:], scalar1=1.0 / E, scalar2=None, op0=ALU.mult),
                stat_mm)
        s_ = SPV([tmu, tex2])
        P("vector", lambda e: e.tensor_mul(out=var7_sb[:], in0=mu7_sb[:],
          in1=mu7_sb[:]), [s_])
        s_ = SPV()
        P("vector", lambda e: e.tensor_sub(out=var7_sb[:], in0=ex27_sb[:],
          in1=var7_sb[:]), [s_])
        s_ = SPV()
        tve = P("vector", lambda e: e.tensor_scalar(out=var7_sb[:],
               in0=var7_sb[:], scalar1=1e-5, scalar2=None, op0=ALU.add), [s_])
        s_ = SPV([tve])
        tln = P("scalar", lambda e: e.activation(rstd7_sb[:], var7_sb[:], AF.Ln,
               bias=0.0, scale=1.0), [s_])
        sa_ = SPA([tln])
        tsx = P("scalar", lambda e: e.activation(rstd7_sb[:], rstd7_sb[:], AF.Exp,
               bias=0.0, scale=-0.5), [sa_])
        sa2_ = SPA([tsx])
        tcbs = P("vector", lambda e: e.tensor_copy(out=mu7b_sb[:], in_=mu7_sb[:]),
                [sa2_, prev_bc7[0]])
        tcbs = P("vector", lambda e: e.tensor_copy(out=rstd7b_sb[:], in_=rstd7_sb[:]),
                [tcbs])
        tcbs = SPV([tcbs])
        tbmu = P("tensor", lambda e: e.matmul(mu7PS[:], ones_b[0:1, :], mu7b_sb[:],
                start=True, stop=True), [tcbs, stat_free7[0]], fence=True)
        tbrs = P("tensor", lambda e: e.matmul(rs7PS[:], ones_b[0:1, :], rstd7b_sb[:],
                start=True, stop=True), [tcbs, stat_free7[0]], fence=True)
        prev_bc7[0] = tbrs
        if DBG:
            sc.dma("sync", dbg_ln2[:, half * 1024:half * 1024 + 512], mu7_sb[:], [tbmu])
            sc.dma("sync", dbg_ln2[:, half * 1024 + 512:half * 1024 + 1024],
                   rstd7_sb[:], [tbmu])
        prev_n1 = None
        for oc in range(NEC):
            tld = sc.dma("gpsimd", x2pre_sb[:],
                         x2pre_sc[oc, :, half * 512:(half + 1) * 512],
                         [prev_str, prev_n1])
            ts1 = P("vector", lambda e: e.tensor_sub(out=x2n_sb[:], in0=x2pre_sb[:],
                   in1=mu7PS[:]), [tld, prev_out_store[0]], pw=[tbmu, tbrs])
            prev_n1 = ts1
            ts2 = P("vector", lambda e: e.tensor_mul(out=x2n_sb[:], in0=x2n_sb[:],
                   in1=rs7PS[:]), [SPV([ts1]), tbrs])
            prev_out_store[0] = sc.dma("sync",
                   outT[oc * 128:(oc + 1) * 128, half * 512:(half + 1) * 512],
                   x2n_sb[:], [ts2])
        stat_free7[0] = prev_n1

    if DBG:
        endt7 = sc.snap()
        sc.dma("sync", dbg_x2pre, x2pre_sc[0].rearrange("p s -> p s"), endt7)
        sc.dma("sync", dbg_w2s, w2bf_sc[0, :, 0:1024].rearrange("p s -> p s"), endt7)
    p7.close()
    if DBG:
        endt = sc.snap()
        sc.dma("sync", dbg_qrT, qrT_sc[0].rearrange("p s -> p s"), endt)
        sc.dma("sync", dbg_krT, krT_sc[0].rearrange("p s -> p s"), endt)
        sc.dma("sync", dbg_v, v_sc[0].rearrange("p s -> p s"), endt)
        sc.dma("sync", dbg_attnT, attnT_sc[0].rearrange("p s -> p s"), endt)
        sc.dma("sync", dbg_x1, x1_sc[0].rearrange("p s -> p s"), endt)
        sc.dma("sync", dbg_ht, hT_sc[0].rearrange("p s -> p s"), endt)
    return finish()


def _rope_tables():
    inv = 1.0 / 10000.0 ** (np.arange(0, D, 2, dtype=np.float32) / D)
    t = np.arange(S, dtype=np.float32)
    fr = np.outer(t, inv)
    emb = np.concatenate([fr, fr], -1)
    cos = np.cos(emb).astype(np.float32)
    nsin = np.sin(emb).astype(np.float32)
    nsin[:, :D // 2] *= -1.0
    return cos, nsin


_NC_CACHE = [None]


def kernel(**inp):
    x = inp["L0_attn_hidden_states"]
    Wq, Wk, Wv, Wo = (inp["L0_attn_Wq"], inp["L0_attn_Wk"], inp["L0_attn_Wv"],
                      inp["L0_attn_Wo"])
    p1_, b1_ = inp["L0_attn_hash_proj1"], inp["L0_attn_hash_bias1"]
    p2_, b2_ = inp["L0_attn_hash_proj2"], inp["L0_attn_hash_bias2"]
    w1, s1 = inp["L0_ffn_w1"], inp["L0_ffn_s1"]
    w2, s2 = inp["L0_ffn_w2"], inp["L0_ffn_s2"]
    w3, s3 = inp["L0_ffn_w3"], inp["L0_ffn_s3"]

    cos, nsin = _rope_tables()
    cosM = np.ascontiguousarray(cos.reshape(NTB, 128, D).transpose(1, 0, 2))
    nsinM = np.ascontiguousarray(nsin.reshape(NTB, 128, D).transpose(1, 0, 2))
    wqTt = np.ascontiguousarray(
        Wq.T.reshape(NEC, 128, NEC, 128).transpose(2, 1, 0, 3).reshape(NEC, 128, NEC * 128))
    wkvT = np.concatenate([Wk.T, Wv.T], axis=1)
    wkvTt = np.ascontiguousarray(
        wkvT.reshape(NEC, 128, 16, 128).transpose(2, 1, 0, 3).reshape(16, 128, NEC * 128))
    woTt = np.ascontiguousarray(
        Wo.T.reshape(NEC, 128, NEC, 128).transpose(2, 1, 0, 3).reshape(NEC, 128, NEC * 128))
    w1p = np.ascontiguousarray(w1.reshape(NHC, 128, 32, 64).transpose(0, 3, 2, 1))
    w3p = np.ascontiguousarray(w3.reshape(NHC, 128, 32, 64).transpose(0, 3, 2, 1))
    w2p = np.ascontiguousarray(w2.reshape(NEC, 128, NHC, 64).transpose(0, 3, 2, 1))
    s1t = np.ascontiguousarray(s1.reshape(NHC, 128, 64).transpose(0, 2, 1)
                               ).reshape(NHC * 64, 128)
    s3t = np.ascontiguousarray(s3.reshape(NHC, 128, 64).transpose(0, 2, 1)
                               ).reshape(NHC * 64, 128)
    s2t = np.ascontiguousarray(s2.reshape(NEC, 128, 172).transpose(0, 2, 1)
                               ).reshape(NEC * 172, 128)
    b1c = np.ascontiguousarray(b1_[:, 0, :].T)
    b2c = np.ascontiguousarray(b2_[:, 0, :].T)
    iotaF = np.broadcast_to(np.arange(S, dtype=np.float32)[None, :], (128, S)).copy()
    identM = np.eye(128, dtype=np.float32)
    onesM = np.ones((128, 128), dtype=np.float32)

    in_maps = []
    for core in range(8):
        b = core // 4
        j = core % 4
        blks = [blk_of(j, s_) for s_ in range(NSLOT)]
        xTb = np.ascontiguousarray(x[b].T)
        xqb = np.ascontiguousarray(
            np.concatenate([xTb[:, bk * 128:(bk + 1) * 128] for bk in blks], axis=1))
        cosQ = np.ascontiguousarray(cosM[:, blks, :])
        nsinQ = np.ascontiguousarray(nsinM[:, blks, :])
        rowvb = (np.array(blks, dtype=np.float32)[None, :] * 128
                 + np.arange(128, dtype=np.float32)[:, None]).astype(np.float32)
        in_maps.append(dict(
            xT=xTb, xq=xqb, wqTt=wqTt, wkvTt=wkvTt, woTt=woTt,
            hp1=p1_, hp2=p2_, b1c=b1c, b2c=b2c,
            cosM=cosM, nsinM=nsinM, cosQ=cosQ, nsinQ=nsinQ, rowv=rowvb,
            iotaF=iotaF, identM=identM, onesM=onesM,
            zers=np.zeros((128, 64), np.float32),
            w1p=w1p, w3p=w3p, w2p=w2p, s1t=s1t, s3t=s3t, s2t=s2t,
        ))

    if _NC_CACHE[0] is None:
        _NC_CACHE[0] = build_nc()
    nc = _NC_CACHE[0]
    res = run_bass_kernel_spmd(nc, in_maps, core_ids=list(range(8)))

    out = np.empty((2, S, E), dtype=np.float32)
    for core in range(8):
        b = core // 4
        j = core % 4
        blks = [blk_of(j, s_) for s_ in range(NSLOT)]
        oT = res.results[core]["outT"]
        for si, bk in enumerate(blks):
            out[b, bk * 128:(bk + 1) * 128, :] = oT[:, si * 128:(si + 1) * 128].T
    return out


# version 05
# speedup vs baseline: 1.0021x; 1.0021x over previous
import os
import numpy as np
import concourse.bass as bass
import concourse.mybir as mybir
from concourse.bass_utils import run_bass_kernel_spmd

F32 = mybir.dt.float32
BF16 = mybir.dt.bfloat16
I32 = mybir.dt.int32
AF = mybir.ActivationFunctionType
ALU = mybir.AluOpType
AXX = mybir.AxisListType.X

S, E, H, KVH, D = 4096, 4096, 32, 8, 128
GRP = H // KVH
HID = 11008
NHC = HID // 128
NEC = E // 128
NTB = S // 128
KEEP = 820.0
SQD = float(1.0 / np.sqrt(D))
NEGC = -1000.0

SLOT_EXT = [4, 32, 8, 28, 12, 24, 16, 20]
SLOT_OFF = [0]
for e_ in SLOT_EXT:
    SLOT_OFF.append(SLOT_OFF[-1] + e_)
SIMW = SLOT_OFF[-1] * 128
NSLOT = 8

def blk_of(j, s):
    k, hi = divmod(s, 2)
    return (31 - j - 4 * k) if hi else (j + 4 * k)

ENGINES = ("tensor", "vector", "scalar", "gpsimd", "sync")


class Sched:
    def __init__(self):
        self.ops = {e: [] for e in ENGINES}
        self.cnt = {"pe": 0, "dv": 0, "ac": 0}
        self.esem = {"tensor": "pe", "vector": "dv", "scalar": "ac"}
        self.dum = {}

    def op(self, eng, fn, waits=(), inc=True, pw=None, fence=False):
        if pw:
            spfn = self.dum.get(eng)
            if spfn is not None:
                self.op(eng, spfn, list(pw), inc=True)
                self.op(eng, spfn, [], inc=True)
            waits = list(waits) + list(pw)
        if os.environ.get("SIM_STRICT"):
            own = self.esem[eng]
            waits = list(waits) + [(own, self.cnt[own], None)]
        wl = {}
        for t in waits:
            if t is None:
                continue
            sem, val, src = t
            if (src == eng and not sem.startswith("q_")
                    and not os.environ.get("SIM_STRICT")):
                continue
            if wl.get(sem, -1) < val:
                wl[sem] = val
        tok = None
        if inc:
            sem = self.esem[eng]
            self.cnt[sem] += 1
            tok = (sem, self.cnt[sem], eng)
        self.ops[eng].append((sorted(wl.items()), fn, (sem, 1) if inc else None,
                              fence))
        return tok

    def dma(self, qeng, dst, src, waits=(), sem=None):
        if sem is None:
            sem = "q_" + dst.tensor.name
        else:
            sem = "q_" + sem
        wl = {}
        for t in waits:
            if t is None:
                continue
            s_, v_, src_ = t
            if wl.get(s_, -1) < v_:
                wl[s_] = v_
        self.cnt[sem] = self.cnt.get(sem, 0) + 16
        self.ops[qeng].append((sorted(wl.items()),
                               lambda e, d=dst, s=src: e.dma_start(out=d, in_=s),
                               (sem, 16), False))
        return (sem, self.cnt[sem], qeng)

    def snap(self):
        return [(sem, v, None) for sem, v in self.cnt.items() if v > 0]


def build_nc():
    from contextlib import ExitStack
    nc = bass.Bass("TRN2", target_bir_lowering=False, debug=False)

    def din(name, shape, dtype=F32):
        return nc.dram_tensor(name, list(shape), dtype, kind="ExternalInput").ap()

    xT = din("xT", (E, S))
    xq = din("xq", (E, 1024))
    wqTt = din("wqTt", (NEC, 128, NEC * 128))
    wkvTt = din("wkvTt", (16, 128, NEC * 128))
    woTt = din("woTt", (NEC, 128, NEC * 128))
    hp1 = din("hp1", (H, D, D))
    hp2 = din("hp2", (H, D, D))
    b1c = din("b1c", (D, H))
    b2c = din("b2c", (D, H))
    cosM = din("cosM", (128, NTB, D))
    nsinM = din("nsinM", (128, NTB, D))
    cosQ = din("cosQ", (128, NSLOT, D))
    nsinQ = din("nsinQ", (128, NSLOT, D))
    rowv = din("rowv", (128, NSLOT))
    iotaF = din("iotaF", (128, S))
    identM = din("identM", (128, 128))
    zers = din("zers", (128, 64))
    onesM = din("onesM", (128, 128))
    w1p = din("w1p", (NHC, 64, 32, 128), I32)
    w3p = din("w3p", (NHC, 64, 32, 128), I32)
    w2p = din("w2p", (NEC, 64, NHC, 128), I32)
    s1t = din("s1t", (NHC * 64, 128))
    s3t = din("s3t", (NHC * 64, 128))
    s2t = din("s2t", (NEC * 172, 128))
    outT = nc.dram_tensor("outT", [E, 1024], F32, kind="ExternalOutput").ap()
    DBG = globals().get("DEBUG_DUMPS", False)
    if DBG:
        dbg_qrT = nc.dram_tensor("dbg_qrT", [128, 1024], BF16, kind="ExternalOutput").ap()
        dbg_krT = nc.dram_tensor("dbg_krT", [128, S], BF16, kind="ExternalOutput").ap()
        dbg_v = nc.dram_tensor("dbg_v", [128, S], BF16, kind="ExternalOutput").ap()
        dbg_attnT = nc.dram_tensor("dbg_attnT", [128, 1024], BF16, kind="ExternalOutput").ap()
        dbg_x1 = nc.dram_tensor("dbg_x1", [128, 1024], F32, kind="ExternalOutput").ap()
        dbg_sim = nc.dram_tensor("dbg_sim", [128, SIMW], BF16, kind="ExternalOutput").ap()
        dbg_kh = nc.dram_tensor("dbg_kh", [128, S], BF16, kind="ExternalOutput").ap()
        dbg_qh = nc.dram_tensor("dbg_qh", [128, 1024], BF16, kind="ExternalOutput").ap()
        dbg_small = nc.dram_tensor("dbg_small", [128, 8], F32, kind="ExternalOutput").ap()
        dbg_mneg = nc.dram_tensor("dbg_mneg", [128, 1], F32, kind="ExternalOutput").ap()
        dbg_ht = nc.dram_tensor("dbg_ht", [128, 1024], BF16, kind="ExternalOutput").ap()
        dbg_nrm = nc.dram_tensor("dbg_nrm", [1, 4], F32, kind="ExternalOutput").ap()
        dbg_pn = nc.dram_tensor("dbg_pn", [1, 128], F32, kind="ExternalOutput").ap()
        dbg_x2pre = nc.dram_tensor("dbg_x2pre", [128, 1024], F32, kind="ExternalOutput").ap()
        dbg_ln2 = nc.dram_tensor("dbg_ln2", [1, 2048], F32, kind="ExternalOutput").ap()
        dbg_w2s = nc.dram_tensor("dbg_w2s", [128, 1024], BF16, kind="ExternalOutput").ap()

    def scr(name, shape, dtype):
        return nc.dram_tensor(name, list(shape), dtype, kind="Internal").ap()

    qrT_sc = scr("qrT_sc", (H, 128, 1024), BF16)
    krT_sc = scr("krT_sc", (KVH, 128, S), BF16)
    v_sc = scr("v_sc", (KVH, 128, S), BF16)
    attnT_sc = scr("attnT_sc", (H, 128, 1024), BF16)
    x1pre_sc = scr("x1pre_sc", (NEC, 128, 1024), F32)
    x1_sc = scr("x1_sc", (NEC, 128, 1024), F32)
    x1bf_sc = scr("x1bf_sc", (NEC, 128, 1024), BF16)
    hT_sc = scr("hT_sc", (NHC, 128, 1024), BF16)
    w2bf_sc = scr("w2bf_sc", (NEC, 128, HID), BF16)
    x2pre_sc = scr("x2pre_sc", (NEC, 128, 1024), F32)
    s1b_sc = scr("s1b_sc", (NHC * 64, 128), BF16)
    s3b_sc = scr("s3b_sc", (NHC * 64, 128), BF16)
    s2b_sc = scr("s2b_sc", (NEC * 172, 128), BF16)

    sc = Sched()
    P = sc.op
    ctx = ExitStack()
    sems = {}
    SB = lambda st, n, sh, dt: st.enter_context(nc.sbuf_tensor(n, list(sh), dt))
    PS = lambda st, n, sh, dt: st.enter_context(nc.psum_tensor(n, list(sh), dt))

    ident_b = SB(ctx, "ident_b", (128, 128), BF16)
    ones_b = SB(ctx, "ones_b", (128, 128), BF16)
    mneg_sb = SB(ctx, "mneg_sb", (128, 1), F32)
    dumv_sb = SB(ctx, "dumv_sb", (128, 64), F32)
    duma_sb = SB(ctx, "duma_sb", (128, 64), F32)
    dumPS = PS(ctx, "dumPS", (128, 512), F32)

    def SPV(waits=()):
        return P("vector", lambda e: e.tensor_copy(out=dumv_sb[:, 32:64],
                 in_=dumv_sb[:, 0:32]), list(waits))

    def SPA(waits=()):
        return P("scalar", lambda e: e.activation(duma_sb[:, 32:64],
                 duma_sb[:, 0:32], AF.Copy, bias=0.0, scale=1.0), list(waits))

    sc.dum["vector"] = lambda e: e.tensor_copy(out=dumv_sb[:, 32:64],
                                               in_=dumv_sb[:, 0:32])
    sc.dum["scalar"] = lambda e: e.activation(duma_sb[:, 32:64], duma_sb[:, 0:32],
                                              AF.Copy, bias=0.0, scale=1.0)

    P("vector", lambda e: e.memset(dumv_sb[:], 0.0), [])
    tdz = sc.dma("gpsimd", duma_sb[:], zers)
    sc.ops["scalar"].append(([("q_duma_sb", 16)], lambda e: e.activation(
        duma_sb[:, 32:64], duma_sb[:, 0:32], AF.Copy, bias=0.0, scale=1.0),
        ("ac", 1), False))
    sc.cnt["ac"] += 1
    t_ld = [sc.dma("gpsimd", ident_b[:], identM),
            sc.dma("gpsimd", ones_b[:], onesM)]

    p0 = ExitStack()
    sclf = SB(p0, "sclf", (128, 5504), F32)
    sclb = SB(p0, "sclb", (128, 5504), BF16)
    tok_scl = []
    prev = None
    for (src_, dst_, nrow) in ((s1t, s1b_sc, NHC * 64), (s3t, s3b_sc, NHC * 64),
                               (s2t, s2b_sc, NEC * 172)):
        nch = nrow // 128
        sv = src_.rearrange("(n p) c -> p n c", p=128)
        dv_ = dst_.rearrange("(n p) c -> p n c", p=128)
        tl = sc.dma("gpsimd", sclf[:, :nch * 128].rearrange("p (n c) -> p n c", c=128),
                    sv, [prev])
        tc_ = P("vector", lambda e, n=nch: e.tensor_copy(out=sclb[:, :n * 128],
                in_=sclf[:, :n * 128]), [tl, prev])
        ts_ = sc.dma("sync", dv_, sclb[:, :nch * 128].rearrange("p (n c) -> p n c", c=128),
                     [tc_])
        tok_scl.append(ts_)
        prev = ts_

    p1 = ExitStack()
    xres_sb = SB(p1, "xres_sb", (128, NEC * 2048), BF16)
    wtile_sb = [SB(p1, f"wtile{i}", (128, NEC * 128), BF16) for i in range(2)]
    panel_sb = SB(p1, "panel_sb", (128, 2048), BF16)
    tm_sb = SB(p1, "tm_sb", (128, 128), BF16)
    rope_a = SB(p1, "rope_a", (128, 128), BF16)
    rope_b = SB(p1, "rope_b", (128, 128), BF16)
    cos_sb = SB(p1, "cos_sb", (128, D), BF16)
    nsin_sb = SB(p1, "nsin_sb", (128, D), BF16)
    ftile_sb = SB(p1, "ftile_sb", (128, 128), BF16)
    sqp_sb = SB(p1, "sqp_sb", (128, 2048), BF16)
    nacc_sb = SB(p1, "nacc_sb", (1, 4), F32)
    colz_sb = SB(p1, "colz_sb", (1, 1), BF16)
    kvqPS = PS(p1, "kvqPS", (128, 2048), F32)
    trPS = [PS(p1, f"trPS{i}", (128, 1024), BF16) for i in range(2)]
    nrmPS = PS(p1, "nrmPS", (128, 512), F32)

    tz = P("vector", lambda e: e.memset(nacc_sb[:], 0.0), [])
    tz = SPV([tz])
    norm_state = {"acc": tz, "sq": None, "ps": None, "i": 0}
    pn_sb = SB(p1, "pn_sb", (1, 128), F32) if DBG else None
    if DBG:
        P("vector", lambda e: e.memset(pn_sb[:], 0.0), [])

    def panel_norm(width, mm_last):
        tsq_ = P("scalar", lambda e, w_=width: e.activation(sqp_sb[:, 0:w_],
                kvqPS[:, 0:w_], AF.Square, bias=0.0, scale=1.0),
                [norm_state["sq"]], pw=[mm_last])
        tsqf = SPA([tsq_])
        for c4 in range(width // 512):
            tnm = P("tensor", lambda e, c_=c4: e.matmul(nrmPS[0:1, :],
                   ones_b[:, 0:1], sqp_sb[:, c_ * 512:(c_ + 1) * 512],
                   start=True, stop=True), [tsqf, norm_state["ps"]] + t_ld,
                   fence=True)
            trd_ = P("vector", lambda e: e.tensor_reduce(out=nacc_sb[:, 1:2],
                    in_=nrmPS[0:1, :], axis=AXX, op=ALU.max),
                    [norm_state["acc"]], pw=[tnm])
            norm_state["ps"] = trd_
            s1_ = SPV([trd_])
            tmx_ = P("vector", lambda e: e.tensor_tensor(out=nacc_sb[:, 0:1],
                    in0=nacc_sb[:, 0:1], in1=nacc_sb[:, 1:2], op=ALU.max), [s1_])
            norm_state["acc"] = SPV([tmx_])
            if DBG and norm_state["i"] < 128:
                i_ = norm_state["i"]
                norm_state["i"] += 1
                P("vector", lambda e, ii=i_: e.tensor_copy(out=pn_sb[:, ii:ii + 1],
                  in_=nacc_sb[0:1, 1:2]), [norm_state["acc"]])
                norm_state["acc"] = SPV()
        norm_state["sq"] = norm_state["acc"]
        return tsq_
    wt_tok = [None, None]
    tr_tok = [None, None]
    tr_pe = [None, None]
    tmj = {"tm": None, "ra": None, "rb": None, "ft": None, "cs": None}
    panel_tok = []
    tp_i = [0]

    def transpose_to(src_ap, waits):
        x = tp_i[0] % 2
        tp_i[0] += 1
        t = P("tensor", lambda e, s_=src_ap, x_=x: e.transpose(trPS[x_][:, 0:128], s_,
              ident_b[:]), list(waits) + [tr_tok[x]] + t_ld, fence=True)
        tr_pe[x] = t
        return trPS[x][:, 0:128], t, x

    def evac_tr(dst_ap, psrc, pe_tok, x, extra=()):
        t = P("vector", lambda e, d_=dst_ap, s_=psrc: e.tensor_copy(out=d_, in_=s_),
              list(extra), pw=[pe_tok])
        tr_tok[x] = t
        return t

    for th in range(2):
        tx_toks = []
        for ec in range(NEC):
            src = xT.rearrange("(ne p) s -> p ne s", p=128)[:, ec, th * 2048:(th + 1) * 2048]
            tx_toks.append(sc.dma("gpsimd", xres_sb[:, ec * 2048:(ec + 1) * 2048], src,
                                  sc.snap() if (th > 0 and ec == 0) else []))
        for dkv in range(16):
            wb = dkv % 2
            twl = sc.dma("gpsimd", wtile_sb[wb][:], wkvTt[dkv], [wt_tok[wb]])
            mms = []
            for ec in range(NEC):
                for q4 in range(4):
                    mms.append(P("tensor", lambda e, ec_=ec, q_=q4, wb_=wb: e.matmul(
                        kvqPS[:, q_ * 512:(q_ + 1) * 512],
                        wtile_sb[wb_][:, ec_ * 128:(ec_ + 1) * 128],
                        xres_sb[:, ec_ * 2048 + q_ * 512: ec_ * 2048 + (q_ + 1) * 512],
                        start=(ec_ == 0), stop=(ec_ == NEC - 1)),
                        [twl, tx_toks[-1]] + (list(panel_tok) if ec == 0 else []),
                        fence=(ec == NEC - 1)))
            wt_tok[wb] = mms[-1]
            is_k = dkv < 8
            tsqtok = panel_norm(2048, mms[-1]) if is_k else None
            tev = P("vector", lambda e: e.tensor_copy(out=panel_sb[:], in_=kvqPS[:]),
                    [], pw=mms[-1:])
            panel_tok.clear()
            panel_tok.extend([tev] + ([tsqtok] if tsqtok else []))
            tev = SPV([tev])
            kvh = dkv if is_k else dkv - 8
            for tbl in range(16):
                tb = th * 16 + tbl
                ps1, tpe1, x1_ = transpose_to(panel_sb[:, tbl * 128:(tbl + 1) * 128],
                                              [tev])
                t_tm = evac_tr(tm_sb[:], ps1, tpe1, x1_, [tmj["tm"]])
                t_tm = SPV([t_tm])
                tmj["tm"] = None
                if is_k:
                    tcs = sc.dma("gpsimd", cos_sb[:], cosM[:, tb, :], [tmj["cs"]])
                    tns = sc.dma("gpsimd", nsin_sb[:], nsinM[:, tb, :], [tmj["cs"]])
                    ta = P("vector", lambda e: e.tensor_mul(out=rope_a[:], in0=tm_sb[:],
                          in1=cos_sb[:]), [t_tm, tcs, tmj.get("t2")])
                    tb1 = P("vector", lambda e: e.tensor_mul(out=rope_b[:, 0:64],
                           in0=tm_sb[:, 64:128], in1=nsin_sb[:, 0:64]), [t_tm, tns])
                    tb2 = P("vector", lambda e: e.tensor_mul(out=rope_b[:, 64:128],
                           in0=tm_sb[:, 0:64], in1=nsin_sb[:, 64:128]), [t_tm, tns])
                    sp_ = SPV([ta, tb1, tb2])
                    tsum = P("vector", lambda e: e.tensor_add(out=rope_a[:], in0=rope_a[:],
                            in1=rope_b[:]), [sp_])
                    tmj["cs"] = tsum
                    spt = SPV([tsum])
                    ps2, tpe2, x2_ = transpose_to(rope_a[:], [spt])
                    tmj["t2"] = tpe2
                    tfe = evac_tr(ftile_sb[:], ps2, tpe2, x2_, [tmj["ft"]])
                    tst = sc.dma("sync", krT_sc[kvh, :, tb * 128:(tb + 1) * 128],
                                 ftile_sb[:], [tfe])
                    tmj["ft"] = tst
                    tmj["tm"] = tsum
                else:
                    tst = sc.dma("sync", v_sc[kvh, :, tb * 128:(tb + 1) * 128],
                                 tm_sb[:], [t_tm])
                    tmj["tm"] = tst

    txq_toks = []
    for ec in range(NEC):
        src = xq.rearrange("(ne p) s -> p ne s", p=128)[:, ec, :]
        txq_toks.append(sc.dma("gpsimd", xres_sb[:, ec * 1024:(ec + 1) * 1024], src,
                               sc.snap() if ec == 0 else []))
    for dq in range(NEC):
        wb = dq % 2
        twl = sc.dma("gpsimd", wtile_sb[wb][:], wqTt[dq], [wt_tok[wb]])
        mms = []
        for ec in range(NEC):
            for q2 in range(2):
                mms.append(P("tensor", lambda e, ec_=ec, q_=q2, wb_=wb: e.matmul(
                    kvqPS[:, q_ * 512:(q_ + 1) * 512],
                    wtile_sb[wb_][:, ec_ * 128:(ec_ + 1) * 128],
                    xres_sb[:, ec_ * 1024 + q_ * 512: ec_ * 1024 + (q_ + 1) * 512],
                    start=(ec_ == 0), stop=(ec_ == NEC - 1)),
                    [twl, txq_toks[-1]] + (list(panel_tok) if ec == 0 else []),
                    fence=(ec == NEC - 1)))
        wt_tok[wb] = mms[-1]
        tsqtok = panel_norm(1024, mms[-1])
        tev = P("vector", lambda e: e.tensor_copy(out=panel_sb[:, 0:1024],
                in_=kvqPS[:, 0:1024]), [], pw=mms[-1:])
        panel_tok.clear()
        panel_tok.extend([tev, tsqtok])
        tev = SPV([tev])
        h = dq
        for s_ in range(NSLOT):
            ps1, tpe1, x1_ = transpose_to(panel_sb[:, s_ * 128:(s_ + 1) * 128], [tev])
            t_tm = evac_tr(tm_sb[:], ps1, tpe1, x1_, [tmj["tm"]])
            t_tm = SPV([t_tm])
            tcs = sc.dma("gpsimd", cos_sb[:], cosQ[:, s_, :], [tmj["cs"]])
            tns = sc.dma("gpsimd", nsin_sb[:], nsinQ[:, s_, :], [tmj["cs"]])
            ta = P("vector", lambda e: e.tensor_mul(out=rope_a[:], in0=tm_sb[:],
                  in1=cos_sb[:]), [t_tm, tcs, tmj.get("t2")])
            tb1 = P("vector", lambda e: e.tensor_mul(out=rope_b[:, 0:64],
                   in0=tm_sb[:, 64:128], in1=nsin_sb[:, 0:64]), [t_tm, tns])
            tb2 = P("vector", lambda e: e.tensor_mul(out=rope_b[:, 64:128],
                   in0=tm_sb[:, 0:64], in1=nsin_sb[:, 64:128]), [t_tm, tns])
            sp_ = SPV([ta, tb1, tb2])
            tsum = P("vector", lambda e: e.tensor_add(out=rope_a[:], in0=rope_a[:],
                    in1=rope_b[:]), [sp_])
            tmj["cs"] = tsum
            spt = SPV([tsum])
            ps2, tpe2, x2_ = transpose_to(rope_a[:], [spt])
            tmj["t2"] = tpe2
            tfe = evac_tr(ftile_sb[:], ps2, tpe2, x2_, [tmj["ft"]])
            tst = sc.dma("sync", qrT_sc[h, :, s_ * 128:(s_ + 1) * 128], ftile_sb[:],
                         [tfe])
            tmj["ft"] = tst

    tcc = P("vector", lambda e: e.tensor_copy(out=colz_sb[:], in_=nacc_sb[:, 0:1]),
           [norm_state["acc"]])
    spm = SPV([tcc])
    tmm_ = P("tensor", lambda e: e.matmul(nrmPS[:, 0:1], ones_b[0:1, :], colz_sb[:],
            start=True, stop=True), [spm, norm_state["ps"]] + t_ld, fence=True)
    tmn = P("vector", lambda e: e.tensor_scalar(out=mneg_sb[:], in0=nrmPS[:, 0:1],
           scalar1=-SQD, scalar2=None, op0=ALU.mult), [], pw=[tmm_])
    tmn = SPV([tmn])
    if DBG:
        sc.dma("sync", dbg_nrm, nacc_sb[0:1, :], [tmn])
        sc.dma("sync", dbg_pn, pn_sb[:], [tmn])

    p1_done = sc.snap()
    p1.close()
    p0.close()

    PHL = globals().get("PHASE_LIMIT", int(os.environ.get("KPHL", "99")))

    def finish():
        fin_ = sc.cnt.get("q_outT", 0)
        sc.ops["sync"].append(([("q_outT", fin_)], lambda e: e.nop(), None, False))
        for k in sc.cnt:
            sems[k] = ctx.enter_context(nc.semaphore(k))
        with nc.Block() as block:
            def runner(engname):
                def run(eng):
                    if engname == "tensor" and "q_ident_b" in sems:
                        eng.wait_ge(sems["q_ident_b"], 16)
                    for waits, fn, inc, fence in sc.ops[engname]:
                        for sem, val in waits:
                            eng.wait_ge(sems[sem], val)
                        inst = fn(eng)
                        if inc is not None:
                            if fence and engname == "tensor" and globals().get("FENCE_ON", True):
                                eng.matmul(dumPS[0:1, 0:1], ident_b[:, 0:1],
                                           ident_b[:, 0:1], start=True,
                                           stop=True).then_inc(sems[inc[0]], inc[1])
                            else:
                                inst.then_inc(sems[inc[0]], inc[1])
                return run
            block.tensor(runner("tensor"))
            block.vector(runner("vector"))
            block.scalar(runner("scalar"))
            block.gpsimd(runner("gpsimd"))
            block.sync(runner("sync"))
        ctx.close()
        return nc

    if PHL == 1:
        sc.dma("gpsimd", outT.rearrange("(h p) s -> h p s", p=128), qrT_sc[:],
               p1_done)
        return finish()

    p3 = ExitStack()
    b1_sb = SB(p3, "b1_sb", (128, H), F32)
    b1h_sb = SB(p3, "b1h_sb", (128, H), F32)
    b2_sb = SB(p3, "b2_sb", (128, H), F32)
    rowv_sb = SB(p3, "rowv_sb", (128, NSLOT), F32)
    iota_sb = SB(p3, "iota_sb", (128, S), F32)
    t_ld3 = [sc.dma("gpsimd", b1_sb[:], b1c),
             sc.dma("gpsimd", b2_sb[:], b2c),
             sc.dma("gpsimd", rowv_sb[:], rowv),
             sc.dma("gpsimd", iota_sb[:], iotaF)]
    tb1h = P("vector", lambda e: e.tensor_scalar(out=b1h_sb[:], in0=b1_sb[:],
        scalar1=0.5, scalar2=None, op0=ALU.mult), t_ld3)
    krT_kv = SB(p3, "krT_kv", (128, S), BF16)
    v_kv = SB(p3, "v_kv", (128, S), BF16)
    qrT_h = SB(p3, "qrT_h", (128, 1024), BF16)
    qhT_h = SB(p3, "qhT_h", (128, 1024), BF16)
    khT_h = SB(p3, "khT_h", (128, S), BF16)
    p1w = [SB(p3, f"p1w{i}", (128, D), BF16) for i in range(2)]
    p2w = [SB(p3, f"p2w{i}", (128, D), BF16) for i in range(2)]
    sim_sb = SB(p3, "sim_sb", (128, SIMW), BF16)
    cmt_sb = SB(p3, "cmt_sb", (128, NSLOT * 512), BF16)
    hsh_t = SB(p3, "hsh_t", (128, 512), BF16)
    hsh_u = SB(p3, "hsh_u", (128, 512), BF16)
    hsh_r = SB(p3, "hsh_r", (128, 512), BF16)
    z1_sb = SB(p3, "z1_sb", (128, 512), BF16)
    wsg_sb = SB(p3, "wsg_sb", (128, 512), BF16)
    e_sb = [SB(p3, f"e_sb{i}", (128, 512), BF16) for i in range(2)]
    pp_sb = [SB(p3, f"pp_sb{i}", (128, 512), BF16) for i in range(2)]
    pt_sb = [SB(p3, f"pt_sb{i}", (128, 512), BF16) for i in range(2)]
    srch_sb = SB(p3, "srch_sb", (128, 4096), BF16)
    tT_sb = SB(p3, "tT_sb", (128, NSLOT), F32)
    tC_sb = SB(p3, "tC_sb", (128, NSLOT), F32)
    cnt_sb = SB(p3, "cnt_sb", (128, NSLOT), F32)
    ok_sb = SB(p3, "ok_sb", (128, NSLOT), F32)
    den_sb = SB(p3, "den_sb", (128, 8), F32)
    den1_sb = SB(p3, "den1_sb", (128, 1), F32)
    rden_sb = SB(p3, "rden_sb", (128, 1), F32)
    attn_sb = SB(p3, "attn_sb", (128, D), BF16)
    attnT_sb = SB(p3, "attnT_sb", (128, D), BF16)
    simPS = [PS(p3, f"simPS{i}", (128, 512), F32) for i in range(2)]
    scPS = [PS(p3, f"scPS{i}", (128, 512), F32) for i in range(2)]
    ptPS = [PS(p3, f"ptPS{i}", (128, 1024), BF16) for i in range(2)]
    pvPS = PS(p3, "pvPS", (128, 512), F32)

    cm_toks = []
    for s_ in range(NSLOT):
        a0 = (SLOT_EXT[s_] - 4) * 128
        cm_toks.append(P("vector", lambda e, s__=s_, a0_=a0: e.tensor_scalar(
            out=cmt_sb[:, s__ * 512:(s__ + 1) * 512], in0=iota_sb[:, a0_:a0_ + 512],
            scalar1=rowv_sb[:, s__:s__ + 1], scalar2=NEGC, op0=ALU.is_gt, op1=ALU.mult),
            p1_done + t_ld3))

    st_tex = [None, None]
    st_tap = [None, None]
    st_ttr = [None, None]
    st_tevp = [None, None]
    st_pv = [None, None]
    st_sev = [None, None]
    st_smm = [None, None]
    last_ttra = [None]
    last_teva = [None]
    last_tdiv = [None]
    hb_tok = [None, None]
    qr_free = [None]
    kv_free = [None]
    kh_free = [None]

    hash_t6 = [None]

    def hash_seq(dst, src, nchunk, pb1, pb2, t_w1, t_w2, t_src, h):
        toks = []
        for c in range(nchunk):
            cs = slice(c * 512, (c + 1) * 512)
            t1 = P("tensor", lambda e, cs_=cs, pb_=pb1, s_=src: e.matmul(scPS[0][:],
                  pb_[:], s_[:, cs_], start=True, stop=True),
                  [t_w1, t_src] + toks[-1:] + [st_tex[0], st_tex[1], tb1h] + t_ld3)
            t2 = P("scalar", lambda e, h_=h: e.activation(hsh_t[:], scPS[0][:], AF.Tanh,
                  bias=b1h_sb[:, h_:h_ + 1], scale=0.5), t_ld, pw=[t1])
            t3 = P("vector", lambda e, h_=h: e.scalar_tensor_tensor(out=hsh_u[:],
                  in0=scPS[0][:], scalar=b1_sb[:, h_:h_ + 1], in1=hsh_t[:],
                  op0=ALU.add, op1=ALU.mult), [SPA([t2])])
            t4 = P("vector", lambda e, h_=h: e.scalar_tensor_tensor(out=hsh_r[:],
                  in0=scPS[0][:], scalar=b1_sb[:, h_:h_ + 1], in1=hsh_u[:],
                  op0=ALU.add, op1=ALU.add), [SPV([t3])])
            t5 = P("vector", lambda e, cs_=cs, s_=src: e.scalar_tensor_tensor(
                  out=z1_sb[:], in0=hsh_r[:], scalar=0.5, in1=s_[:, cs_],
                  op0=ALU.mult, op1=ALU.add), [SPV([t4]), hash_t6[0]])
            t6 = P("tensor", lambda e, pb_=pb2: e.matmul(scPS[1][:], pb_[:], z1_sb[:],
                  start=True, stop=True), [SPV([t5]), t_w2, st_tex[1]], fence=True)
            hash_t6[0] = t6
            t7 = P("vector", lambda e, h_=h: e.scalar_tensor_tensor(out=wsg_sb[:],
                  in0=scPS[1][:], scalar=b2_sb[:, h_:h_ + 1], in1=z1_sb[:],
                  op0=ALU.add, op1=ALU.add), [], pw=[t6])
            t8 = P("vector", lambda e, cs_=cs, d_=dst: e.tensor_scalar(out=d_[:, cs_],
                  in0=wsg_sb[:], scalar1=0.0, scalar2=0.5, op0=ALU.is_ge,
                  op1=ALU.subtract), [SPV([t7])])
            toks.append(t8)
        return toks

    for kvh in range(KVH):
        tkr = sc.dma("gpsimd", krT_kv[:], krT_sc[kvh].rearrange("p s -> p s"),
                     p1_done + [kv_free[0]])
        tv = sc.dma("gpsimd", v_kv[:], v_sc[kvh].rearrange("p s -> p s"),
                    p1_done + [kv_free[0]])
        for g in range(GRP):
            h = kvh * GRP + g
            hb = h % 2
            tw1 = sc.dma("gpsimd", p1w[hb][:], hp1[h], [hb_tok[hb]])
            tw2 = sc.dma("gpsimd", p2w[hb][:], hp2[h], [hb_tok[hb]])
            tqr = sc.dma("gpsimd", qrT_h[:], qrT_sc[h].rearrange("p s -> p s"),
                         p1_done + [qr_free[0]])
            kh_toks = hash_seq(khT_h, krT_kv, 8, p1w[hb], p2w[hb], tw1, tw2, tkr, h)
            qh_toks = hash_seq(qhT_h, qrT_h, 2, p1w[hb], p2w[hb], tw1, tw2, tqr, h)
            hb_tok[hb] = qh_toks[-1]
            hash_done = [kh_toks[-1], qh_toks[-1]]
            if kh_free[0] is not None:
                pass
            hsp = SPV(hash_done)
            for s_ in range(NSLOT):
                ext = SLOT_EXT[s_]
                off = SLOT_OFF[s_] * 128
                nch = ext * 128 // 512
                for c in range(nch):
                    x = c % 2
                    tmm = P("tensor", lambda e, x_=x, s__=s_, c_=c: e.matmul(
                        simPS[x_][:], qhT_h[:, s__ * 128:(s__ + 1) * 128],
                        khT_h[:, c_ * 512:(c_ + 1) * 512], start=True, stop=True),
                        [hsp, st_sev[x], kh_free[0]], fence=True)
                    st_smm[x] = tmm
                    if c == nch - 1:
                        tev = P("vector", lambda e, x_=x, s__=s_, c_=c, o_=off:
                            e.tensor_add(out=sim_sb[:, o_ + c_ * 512:o_ + (c_ + 1) * 512],
                                         in0=simPS[x_][:],
                                         in1=cmt_sb[:, s__ * 512:(s__ + 1) * 512]),
                            cm_toks, pw=[tmm])
                    else:
                        tev = P("vector", lambda e, x_=x, c_=c, o_=off: e.tensor_copy(
                            out=sim_sb[:, o_ + c_ * 512:o_ + (c_ + 1) * 512],
                            in_=simPS[x_][:]), [], pw=[tmm])
                    st_sev[x] = tev
            P("vector", lambda e: e.memset(tT_sb[:], -32.0), [])
            SPV()
            for w_ in (32.0, 16.0, 8.0, 4.0, 2.0, 1.0, 0.5):
                P("vector", lambda e, w__=w_: e.tensor_scalar(
                    out=tC_sb[:], in0=tT_sb[:], scalar1=w__, scalar2=None,
                    op0=ALU.add), [])
                SPV()
                for s_ in range(NSLOT):
                    P("vector", lambda e, s__=s_, o_=SLOT_OFF[s_] * 128,
                      n_=SLOT_EXT[s_] * 128: e.tensor_scalar(
                        out=srch_sb[:, 0:n_], in0=sim_sb[:, o_:o_ + n_],
                        scalar1=tC_sb[:, s__:s__ + 1], scalar2=0.0, op0=ALU.is_ge,
                        op1=ALU.add, accum_out=cnt_sb[:, s__:s__ + 1]), [])
                SPV()
                P("vector", lambda e: e.tensor_scalar(
                    out=ok_sb[:], in0=cnt_sb[:], scalar1=KEEP - 0.5, scalar2=None,
                    op0=ALU.is_ge), [])
                SPV()
                P("vector", lambda e, w__=w_: e.scalar_tensor_tensor(
                    out=tT_sb[:], in0=ok_sb[:], scalar=w__, in1=tT_sb[:],
                    op0=ALU.mult, op1=ALU.add), [])
                SPV()
            for s_ in range(NSLOT):
                ext = SLOT_EXT[s_]
                off = SLOT_OFF[s_] * 128
                nch = ext * 128 // 512
                for c in range(nch):
                    x = c % 2
                    tmm = P("tensor", lambda e, x_=x, s__=s_, c_=c: e.matmul(
                        scPS[x_][:], qrT_h[:, s__ * 128:(s__ + 1) * 128],
                        krT_kv[:, c_ * 512:(c_ + 1) * 512], start=True, stop=True),
                        [tqr, tkr, st_tex[x]] + (hash_done if c < 2 else []),
                        fence=True)
                    tex = P("scalar", lambda e, x_=x: e.activation(e_sb[x_][:],
                          scPS[x_][:], AF.Exp, bias=mneg_sb[:], scale=SQD),
                          [tmn, st_tap[x]], pw=[tmm])
                    st_tex[x] = tex
                    tap = P("vector", lambda e, x_=x, s__=s_, c_=c, o_=off:
                        e.scalar_tensor_tensor(out=pp_sb[x_][:],
                            in0=sim_sb[:, o_ + c_ * 512:o_ + (c_ + 1) * 512],
                            scalar=tT_sb[:, s__:s__ + 1], in1=e_sb[x_][:],
                            op0=ALU.is_ge, op1=ALU.mult,
                            accum_out=den_sb[:, c_:c_ + 1]),
                        [SPA([tex]), st_ttr[x]])
                    st_tap[x] = tap
                    tapf = SPV([tap])
                    ttrs = None
                    for j in range(4):
                        ttrs = P("tensor", lambda e, x_=x, j_=j: e.transpose(
                            ptPS[x_][:, j_ * 128:(j_ + 1) * 128],
                            pp_sb[x_][:, j_ * 128:(j_ + 1) * 128], ident_b[:]),
                            [tapf, st_tevp[x]] + ([last_teva[0]] if last_teva[0] else []),
                            fence=(j_ == 3) if False else (j == 3))
                    st_ttr[x] = ttrs
                    tevp = P("vector", lambda e, x_=x: e.tensor_copy(out=pt_sb[x_][:],
                            in_=ptPS[x_][:, 0:512]), [st_pv[x]], pw=[ttrs])
                    st_tevp[x] = tevp
                    tevpf = SPV([tevp])
                    pv_last = None
                    for j in range(4):
                        kb = c * 4 + j
                        pv_last = P("tensor", lambda e, x_=x, j_=j, kb_=kb,
                            last_=nch * 4 - 1: e.matmul(
                            pvPS[:, 0:128], pt_sb[x_][:, j_ * 128:(j_ + 1) * 128],
                            v_kv[:, kb_ * 128:(kb_ + 1) * 128],
                            start=(kb_ == 0), stop=(kb_ == last_)),
                            [tevpf, tv] + ([last_tdiv[0]] if kb == 0 else []),
                            fence=(kb == nch * 4 - 1))
                    st_pv[x] = pv_last
                sd0 = SPV([st_tap[0], st_tap[1]])
                td1 = P("vector", lambda e, n_=nch: e.tensor_reduce(out=den1_sb[:],
                       in_=den_sb[:, 0:n_], axis=AXX, op=ALU.add), [sd0])
                sd1 = SPV([td1])
                trd = P("vector", lambda e: e.reciprocal(out=rden_sb[:], in_=den1_sb[:]),
                       [sd1])
                sd2 = SPV([trd])
                tdiv = P("vector", lambda e: e.tensor_scalar(out=attn_sb[:],
                        in0=pvPS[:, 0:128], scalar1=rden_sb[:], scalar2=None,
                        op0=ALU.mult), [sd2, last_ttra[0]],
                        pw=[st_pv[0], st_pv[1]])
                last_tdiv[0] = tdiv
                sd3 = SPV([tdiv])
                ttra = P("tensor", lambda e: e.transpose(ptPS[0][:, 0:128], attn_sb[:],
                        ident_b[:]), [sd3, st_tevp[0]], fence=True)
                last_ttra[0] = ttra
                st_ttr[0] = ttra
                teva = P("vector", lambda e: e.tensor_copy(out=attnT_sb[:],
                        in_=ptPS[0][:, 0:128]), [last_teva[0]], pw=[ttra])
                tstA = sc.dma("sync", attnT_sc[h, :, s_ * 128:(s_ + 1) * 128],
                              attnT_sb[:], [teva])
                last_teva[0] = tstA
                st_tevp[0] = teva
            if DBG and h == 0:
                sc.dma("sync", dbg_small, tT_sb[:], [last_tdiv[0]])
                sc.dma("sync", dbg_mneg, mneg_sb[:], [tmn])
                sc.dma("sync", dbg_kh, khT_h[:], [kh_toks[-1]])
                sc.dma("sync", dbg_qh, qhT_h[:], [qh_toks[-1]])
                sc.dma("sync", dbg_sim, sim_sb[:], [last_tdiv[0]])
            qr_free[0] = last_tdiv[0]
            kh_free[0] = st_smm[0] if st_smm[0] else None
            kh_free[0] = (st_smm[0] if (st_smm[1] is None or
                          (st_smm[0] and st_smm[0][1] > st_smm[1][1])) else st_smm[1])
        kv_free[0] = last_tdiv[0]

    p3_done = sc.snap()
    p3.close()

    if PHL == 3:
        sc.dma("gpsimd", outT.rearrange("(h p) s -> h p s", p=128), attnT_sc[:],
               p3_done)
        return finish()

    def ln_block(tag, st, mm_schedule, res_src, pre_sc_, out_fp_sc, out_bf_sc,
                 out_direct, deps0):
        pass

    p5 = ExitStack()
    att_res = SB(p5, "att_res", (128, NEC * 512), BF16)
    wo_sb = [SB(p5, f"wo_sb{i}", (128, NEC * 128), BF16) for i in range(2)]
    x1pre_sb = SB(p5, "x1pre_sb", (128, 512), F32)
    xr_sb = SB(p5, "xr_sb", (128, 512), F32)
    xb_sb = SB(p5, "xb_sb", (128, 512), BF16)
    sq5_sb = SB(p5, "sq5_sb", (128, 512), BF16)
    mu5_sb = SB(p5, "mu5_sb", (1, 512), F32)
    ex25_sb = SB(p5, "ex25_sb", (1, 512), F32)
    var5_sb = SB(p5, "var5_sb", (1, 512), F32)
    rstd5_sb = SB(p5, "rstd5_sb", (1, 512), F32)
    mu5b_sb = SB(p5, "mu5b_sb", (1, 512), BF16)
    rstd5b_sb = SB(p5, "rstd5b_sb", (1, 512), BF16)
    xn_sb = SB(p5, "xn_sb", (128, 512), F32)
    acc1_sb = SB(p5, "acc1_sb", (1, 512), F32)
    acc2_sb = SB(p5, "acc2_sb", (1, 512), F32)
    woPS = PS(p5, "woPS", (128, 512), F32)
    stPS = PS(p5, "stPS", (128, 512), F32)
    st2PS = PS(p5, "st2PS", (128, 512), F32)
    muPS = PS(p5, "muPS", (128, 512), F32)
    rsPS = PS(p5, "rsPS", (128, 512), F32)

    wo_tok = [None, None]
    stat_free = [None]
    stat_acc = {"a1": None, "a2": None}
    prev_xn_store = [None]
    prev_bc5 = [None]
    for tc in range(2):
        ta = sc.dma("gpsimd", att_res[:],
                    attnT_sc[:, :, tc * 512:(tc + 1) * 512].rearrange("h p t -> p h t"),
                    p3_done + [stat_free[0]])
        stat_mm = []
        prev_str = None
        prev_res = None
        prev_cb = None
        for ec in range(NEC):
            wb = ec % 2
            twl = sc.dma("gpsimd", wo_sb[wb][:], woTt[ec], [wo_tok[wb]])
            txr = sc.dma("gpsimd", xr_sb[:],
                         xq.rearrange("(ne p) s -> p ne s", p=128)[:, ec,
                         tc * 512:(tc + 1) * 512], [prev_res])
            mms = []
            for dc in range(NEC):
                mms.append(P("tensor", lambda e, wb_=wb, dc_=dc: e.matmul(
                    woPS[:], wo_sb[wb_][:, dc_ * 128:(dc_ + 1) * 128],
                    att_res[:, dc_ * 512:(dc_ + 1) * 512],
                    start=(dc_ == 0), stop=(dc_ == NEC - 1)),
                    [twl, ta] + ([prev_res] if dc == 0 else []),
                    fence=(dc == NEC - 1)))
            wo_tok[wb] = mms[-1]
            tres = P("vector", lambda e: e.tensor_add(out=x1pre_sb[:], in0=woPS[:],
                    in1=xr_sb[:]), [mms[-1], txr, prev_str])
            prev_res = tres
            tstr = sc.dma("sync", x1pre_sc[ec, :, tc * 512:(tc + 1) * 512], x1pre_sb[:],
                          [tres])
            prev_str = tstr
            tcb = P("vector", lambda e: e.tensor_copy(out=xb_sb[:], in_=x1pre_sb[:]),
                   [SPV([tres]), prev_cb])
            tsq = P("vector", lambda e: e.tensor_mul(out=sq5_sb[:], in0=xb_sb[:],
                   in1=xb_sb[:]), [SPV([tcb])])
            m1 = P("tensor", lambda e: e.matmul(stPS[0:1, :], ones_b[:, 0:1],
                  xb_sb[:], start=True, stop=True),
                  [SPV([tcb]), stat_acc["a1"]] + ([stat_free[0]] if ec == 0 else []),
                  fence=True)
            a1 = P("vector", lambda e, ec_=ec: (
                  e.tensor_copy(out=acc1_sb[:], in_=stPS[0:1, :]) if ec_ == 0 else
                  e.tensor_add(out=acc1_sb[:], in0=acc1_sb[:], in1=stPS[0:1, :])),
                  [SPV()], pw=[m1])
            stat_acc["a1"] = SPV([a1])
            m2 = P("tensor", lambda e: e.matmul(st2PS[0:1, :], ones_b[:, 0:1],
                  sq5_sb[:], start=True, stop=True),
                  [SPV([tsq]), stat_acc["a2"]] + ([stat_free[0]] if ec == 0 else []),
                  fence=True)
            a2 = P("vector", lambda e, ec_=ec: (
                  e.tensor_copy(out=acc2_sb[:], in_=st2PS[0:1, :]) if ec_ == 0 else
                  e.tensor_add(out=acc2_sb[:], in0=acc2_sb[:], in1=st2PS[0:1, :])),
                  [SPV()], pw=[m2])
            stat_acc["a2"] = SPV([a2])
            stat_mm = [stat_acc["a1"], stat_acc["a2"]]
            prev_cb = m2
        tmu = P("vector", lambda e: e.tensor_scalar(out=mu5_sb[:], in0=acc1_sb[:],
               scalar1=1.0 / E, scalar2=None, op0=ALU.mult), stat_mm)
        tex2 = P("vector", lambda e: e.tensor_scalar(out=ex25_sb[:],
                in0=acc2_sb[:], scalar1=1.0 / E, scalar2=None, op0=ALU.mult),
                stat_mm)
        s_ = SPV([tmu, tex2])
        P("vector", lambda e: e.tensor_mul(out=var5_sb[:], in0=mu5_sb[:],
          in1=mu5_sb[:]), [s_])
        s_ = SPV()
        P("vector", lambda e: e.tensor_sub(out=var5_sb[:], in0=ex25_sb[:],
          in1=var5_sb[:]), [s_])
        s_ = SPV()
        tve = P("vector", lambda e: e.tensor_scalar(out=var5_sb[:],
               in0=var5_sb[:], scalar1=1e-5, scalar2=None, op0=ALU.add), [s_])
        s_ = SPV([tve])
        tln = P("scalar", lambda e: e.activation(rstd5_sb[:], var5_sb[:], AF.Ln,
               bias=0.0, scale=1.0), [s_])
        sa_ = SPA([tln])
        tsx = P("scalar", lambda e: e.activation(rstd5_sb[:], rstd5_sb[:], AF.Exp,
               bias=0.0, scale=-0.5), [sa_])
        sa2_ = SPA([tsx])
        tcbs = P("vector", lambda e: e.tensor_copy(out=mu5b_sb[:], in_=mu5_sb[:]),
                [sa2_, prev_bc5[0]])
        tcbs = P("vector", lambda e: e.tensor_copy(out=rstd5b_sb[:], in_=rstd5_sb[:]),
                [tcbs])
        tcbs = SPV([tcbs])
        tbmu = P("tensor", lambda e: e.matmul(muPS[:], ones_b[0:1, :], mu5b_sb[:],
                start=True, stop=True), [tcbs, stat_free[0]], fence=True)
        tbrs = P("tensor", lambda e: e.matmul(rsPS[:], ones_b[0:1, :], rstd5b_sb[:],
                start=True, stop=True), [tcbs, stat_free[0]], fence=True)
        prev_bc5[0] = tbrs
        prev_n1 = None
        prev_sb = None
        prev_sc2 = None
        for ec in range(NEC):
            tld = sc.dma("gpsimd", x1pre_sb[:], x1pre_sc[ec, :, tc * 512:(tc + 1) * 512],
                         [prev_str, prev_n1] + stat_mm)
            ts1 = P("vector", lambda e: e.tensor_sub(out=xn_sb[:], in0=x1pre_sb[:],
                   in1=muPS[:]), [tld, prev_sb, prev_xn_store[0]], pw=[tbmu, tbrs])
            prev_n1 = ts1
            ts2 = P("vector", lambda e: e.tensor_mul(out=xn_sb[:], in0=xn_sb[:],
                   in1=rsPS[:]), [SPV([ts1]), tbrs])
            tso = sc.dma("sync", x1_sc[ec, :, tc * 512:(tc + 1) * 512], xn_sb[:], [ts2])
            prev_xn_store[0] = tso
            tcb2 = P("vector", lambda e: e.tensor_copy(out=xb_sb[:], in_=xn_sb[:]),
                    [SPV([ts2]), prev_sc2])
            prev_sb = tcb2
            tso2 = sc.dma("sync", x1bf_sc[ec, :, tc * 512:(tc + 1) * 512], xb_sb[:],
                          [tcb2])
            prev_sc2 = tso2
            prev_str = tso
        stat_free[0] = prev_n1

    p5_done = sc.snap()
    p5.close()

    if PHL == 5:
        sc.dma("gpsimd", outT.rearrange("(ec p) s -> ec p s", p=128), x1_sc[:],
               p5_done)
        return finish()

    p6 = ExitStack()
    x1t_sb = SB(p6, "x1t_sb", (128, NEC * 1024), BF16)
    pk_sb = [SB(p6, f"pk_sb{i}", (128, 4096), I32) for i in range(2)]
    nib_sb = [SB(p6, f"nib_sb{i}", (128, 4096), I32) for i in range(2)]
    wde_sb = [SB(p6, f"wde_sb{i}", (128, 4096), BF16) for i in range(2)]
    wsc_sb = [SB(p6, f"wsc_sb{i}", (128, 4096), BF16) for i in range(2)]
    sil_sb = SB(p6, "sil_sb", (128, 1024), BF16)
    ht_sb = SB(p6, "ht_sb", (128, 1024), BF16)
    aPS = PS(p6, "aPS", (128, 1024), F32)
    bPS = PS(p6, "bPS", (128, 1024), F32)

    tx1 = [sc.dma("gpsimd", x1t_sb[:, ec * 1024:(ec + 1) * 1024],
                  x1bf_sc[ec].rearrange("p s -> p s"), p5_done) for ec in range(NEC)]

    dq_state = [{"tn": None, "tca": None, "tmu": None, "mm": None, "tl": None}
                for _ in range(2)]

    def dequant(idx, pk_src, scb_src, ncol=4096):
        st = dq_state[idx]
        pk = pk_sb[idx]
        nib = nib_sb[idx]
        wde = wde_sb[idx]
        wsc = wsc_sb[idx]
        tl0 = sc.dma("gpsimd",
                     pk[0:64, 0:ncol].rearrange("p (a b) -> p a b", b=128), pk_src,
                     [st["tn"]])
        tl = sc.dma("gpsimd",
                    pk[64:128, 0:ncol].rearrange("p (a b) -> p a b", b=128), pk_src,
                    [st["tn"], tl0])
        ts0 = sc.dma("gpsimd",
                     wsc[0:64, 0:ncol].rearrange("p (a b) -> p a b", b=128), scb_src[0],
                     [st["tmu"]])
        tsc_ = sc.dma("gpsimd",
                      wsc[64:128, 0:ncol].rearrange("p (a b) -> p a b", b=128),
                      scb_src[1], [st["tmu"], ts0])
        tl = (tl[0], tl[1], tl[2])
        tn1 = P("vector", lambda e, n_=ncol: e.tensor_scalar(out=nib[0:64, 0:n_],
               in0=pk[0:64, 0:n_], scalar1=24.0, scalar2=28.0,
               op0=ALU.logical_shift_left, op1=ALU.arith_shift_right),
               [tl, st["tca"]])
        tn2 = P("vector", lambda e, n_=ncol: e.tensor_scalar(out=nib[64:128, 0:n_],
               in0=pk[64:128, 0:n_], scalar1=28.0, scalar2=28.0,
               op0=ALU.logical_shift_left, op1=ALU.arith_shift_right),
               [tl, st["tca"]])
        tca = P("scalar", lambda e, n_=ncol: e.activation(wde[:, 0:n_], nib[:, 0:n_],
               AF.Copy, bias=0.0, scale=1.0), [SPV([tn1, tn2]), st["mm"]])
        tmu = P("vector", lambda e, n_=ncol: e.tensor_mul(out=wde[:, 0:n_],
               in0=wde[:, 0:n_], in1=wsc[:, 0:n_]), [SPA([tca]), tsc_] + tok_scl)
        st["tn"], st["tca"], st["tmu"] = tn2, tca, tmu
        return SPV([tmu])

    def pk_rep(arr, i0):
        return arr[i0]

    def sc_rep(scb, i0, nj):
        v = scb.rearrange("(i j) h -> i j h", j=2 * nj)[i0]
        v4 = v.rearrange("(jj nib) h -> nib jj h", nib=2)
        return (v4[0].partition_broadcast(64), v4[1].partition_broadcast(64))

    prev_si = [None]
    prev_ht = [None]
    prev_hstore = [None]
    for hc in range(NHC):
        idx = hc % 2
        t1w = dequant(idx, pk_rep(w1p, hc), sc_rep(s1b_sc, hc, 32))
        mm1 = None
        for J in range(NEC):
            for tk in range(2):
                mm1 = P("tensor", lambda e, J_=J, tk_=tk, i_=idx: e.matmul(
                    aPS[:, tk_ * 512:(tk_ + 1) * 512],
                    wde_sb[i_][:, J_ * 128:(J_ + 1) * 128],
                    x1t_sb[:, J_ * 1024 + tk_ * 512: J_ * 1024 + (tk_ + 1) * 512],
                    start=(J_ == 0), stop=(J_ == NEC - 1)),
                    [t1w, tx1[-1]] + ([prev_si[0]] if J == 0 else []),
                    fence=(J == NEC - 1 and tk == 1))
        dq_state[idx]["mm"] = mm1
        t3w = dequant(idx, pk_rep(w3p, hc), sc_rep(s3b_sc, hc, 32))
        mm3 = None
        for J in range(NEC):
            for tk in range(2):
                mm3 = P("tensor", lambda e, J_=J, tk_=tk, i_=idx: e.matmul(
                    bPS[:, tk_ * 512:(tk_ + 1) * 512],
                    wde_sb[i_][:, J_ * 128:(J_ + 1) * 128],
                    x1t_sb[:, J_ * 1024 + tk_ * 512: J_ * 1024 + (tk_ + 1) * 512],
                    start=(J_ == 0), stop=(J_ == NEC - 1)),
                    [t3w, tx1[-1]] + ([prev_ht[0]] if J == 0 else []),
                    fence=(J == NEC - 1 and tk == 1))
        dq_state[idx]["mm"] = mm3
        tsi = P("scalar", lambda e: e.activation(sil_sb[:], aPS[:], AF.Silu, bias=0.0,
               scale=1.0), [mm3, prev_ht[0]], pw=[mm1])
        prev_si[0] = tsi
        tht = P("vector", lambda e: e.tensor_mul(out=ht_sb[:], in0=sil_sb[:],
               in1=bPS[:]), [SPA([tsi]), prev_hstore[0]], pw=[mm3])
        prev_ht[0] = tht
        prev_hstore[0] = sc.dma("sync", hT_sc[hc].rearrange("p s -> p s"), ht_sb[:],
                                [tht])

    for oc in range(NEC):
        for jg, (j0, nj) in enumerate(((0, 32), (32, 32), (64, 22))):
            idx = (oc * 3 + jg) % 2
            pk_src = w2p[oc][:, j0:j0 + nj, :]
            v = s2b_sc.rearrange("(i j) h -> i j h", j=172)[oc][2 * j0:2 * (j0 + nj)]
            v4 = v.rearrange("(jj nib) h -> nib jj h", nib=2)
            sc_src = (v4[0].partition_broadcast(64), v4[1].partition_broadcast(64))
            tmu = dequant(idx, pk_src, sc_src, nj * 128)
            tws = sc.dma("sync",
                         w2bf_sc[oc, :, j0 * 128:(j0 + nj) * 128].rearrange(
                             "p (a b) -> p a b", b=128),
                         wde_sb[idx][:, 0:nj * 128].rearrange("p (a b) -> p a b", b=128),
                         [tmu], sem=f"w2bf{idx}")
            dq_state[idx]["mm"] = tws

    p6_done = sc.snap()
    p6.close()

    if PHL == 6:
        sc.dma("gpsimd", outT.rearrange("(ec p) s -> ec p s", p=128), x1bf_sc[:],
               p6_done + sc.snap())
        return finish()

    p7 = ExitStack()
    hT_res = SB(p7, "hT_res", (128, NHC * 512), BF16)
    w2_sb = [SB(p7, f"w2_sb{i}", (128, HID), BF16) for i in range(2)]
    x2pre_sb = SB(p7, "x2pre_sb", (128, 512), F32)
    x1r_sb = SB(p7, "x1r_sb", (128, 512), F32)
    x2b_sb = SB(p7, "x2b_sb", (128, 512), BF16)
    sq7_sb = SB(p7, "sq7_sb", (128, 512), BF16)
    mu7_sb = SB(p7, "mu7_sb", (1, 512), F32)
    ex27_sb = SB(p7, "ex27_sb", (1, 512), F32)
    var7_sb = SB(p7, "var7_sb", (1, 512), F32)
    rstd7_sb = SB(p7, "rstd7_sb", (1, 512), F32)
    mu7b_sb = SB(p7, "mu7b_sb", (1, 512), BF16)
    rstd7b_sb = SB(p7, "rstd7b_sb", (1, 512), BF16)
    x2n_sb = SB(p7, "x2n_sb", (128, 512), F32)
    acc17_sb = SB(p7, "acc17_sb", (1, 512), F32)
    acc27_sb = SB(p7, "acc27_sb", (1, 512), F32)
    fPS = PS(p7, "fPS", (128, 512), F32)
    s7PS = PS(p7, "s7PS", (128, 512), F32)
    s72PS = PS(p7, "s72PS", (128, 512), F32)
    mu7PS = PS(p7, "mu7PS", (128, 512), F32)
    rs7PS = PS(p7, "rs7PS", (128, 512), F32)

    w2_tok = [None, None]
    half_mm_last = [None]
    stat_free7 = [None]
    stat_acc7 = {"a1": None, "a2": None}
    prev_out_store = [None]
    prev_bc7 = [None]
    for half in range(2):
        th_toks = []
        for J2 in range(NHC):
            th_toks.append(sc.dma("gpsimd", hT_res[:, J2 * 512:(J2 + 1) * 512],
                           hT_sc[J2, :, half * 512:(half + 1) * 512],
                           p6_done + [half_mm_last[0]]))
        stat_mm = []
        prev_str = None
        prev_res = None
        prev_cb = None
        for oc in range(NEC):
            wb = oc % 2
            tw_ready = sc.dma("gpsimd", w2_sb[wb][:],
                              w2bf_sc[oc].rearrange("p s -> p s"),
                              p6_done + [w2_tok[wb]])
            mms = None
            for J2 in range(NHC):
                mms = P("tensor", lambda e, wb_=wb, J_=J2: e.matmul(
                    fPS[:], w2_sb[wb_][:, J_ * 128:(J_ + 1) * 128],
                    hT_res[:, J_ * 512:(J_ + 1) * 512],
                    start=(J_ == 0), stop=(J_ == NHC - 1)),
                    [tw_ready, th_toks[-1]] + ([prev_res] if J2 == 0 else []),
                    fence=(J2 == NHC - 1))
            w2_tok[wb] = mms
            half_mm_last[0] = mms
            txr = sc.dma("gpsimd", x1r_sb[:], x1_sc[oc, :, half * 512:(half + 1) * 512],
                         [prev_res])
            tres = P("vector", lambda e: e.tensor_add(out=x2pre_sb[:], in0=fPS[:],
                    in1=x1r_sb[:]), [txr, prev_str], pw=[mms])
            prev_res = tres
            tstr = sc.dma("sync", x2pre_sc[oc, :, half * 512:(half + 1) * 512],
                          x2pre_sb[:], [tres])
            prev_str = tstr
            tcb = P("vector", lambda e: e.tensor_copy(out=x2b_sb[:], in_=x2pre_sb[:]),
                   [SPV([tres]), prev_cb])
            tsq = P("vector", lambda e: e.tensor_mul(out=sq7_sb[:], in0=x2b_sb[:],
                   in1=x2b_sb[:]), [SPV([tcb])])
            m1 = P("tensor", lambda e: e.matmul(s7PS[0:1, :], ones_b[:, 0:1],
                  x2b_sb[:], start=True, stop=True),
                  [SPV([tcb]), stat_acc7["a1"]] + ([stat_free7[0]] if oc == 0 else []),
                  fence=True)
            a1 = P("vector", lambda e, oc_=oc: (
                  e.tensor_copy(out=acc17_sb[:], in_=s7PS[0:1, :]) if oc_ == 0 else
                  e.tensor_add(out=acc17_sb[:], in0=acc17_sb[:], in1=s7PS[0:1, :])),
                  [SPV()], pw=[m1])
            stat_acc7["a1"] = SPV([a1])
            m2 = P("tensor", lambda e: e.matmul(s72PS[0:1, :], ones_b[:, 0:1],
                  sq7_sb[:], start=True, stop=True),
                  [SPV([tsq]), stat_acc7["a2"]] + ([stat_free7[0]] if oc == 0 else []),
                  fence=True)
            a2 = P("vector", lambda e, oc_=oc: (
                  e.tensor_copy(out=acc27_sb[:], in_=s72PS[0:1, :]) if oc_ == 0 else
                  e.tensor_add(out=acc27_sb[:], in0=acc27_sb[:], in1=s72PS[0:1, :])),
                  [SPV()], pw=[m2])
            stat_acc7["a2"] = SPV([a2])
            stat_mm = [stat_acc7["a1"], stat_acc7["a2"]]
            prev_cb = m2
        tmu = P("vector", lambda e: e.tensor_scalar(out=mu7_sb[:], in0=acc17_sb[:],
               scalar1=1.0 / E, scalar2=None, op0=ALU.mult), stat_mm)
        tex2 = P("vector", lambda e: e.tensor_scalar(out=ex27_sb[:],
                in0=acc27_sb[:], scalar1=1.0 / E, scalar2=None, op0=ALU.mult),
                stat_mm)
        s_ = SPV([tmu, tex2])
        P("vector", lambda e: e.tensor_mul(out=var7_sb[:], in0=mu7_sb[:],
          in1=mu7_sb[:]), [s_])
        s_ = SPV()
        P("vector", lambda e: e.tensor_sub(out=var7_sb[:], in0=ex27_sb[:],
          in1=var7_sb[:]), [s_])
        s_ = SPV()
        tve = P("vector", lambda e: e.tensor_scalar(out=var7_sb[:],
               in0=var7_sb[:], scalar1=1e-5, scalar2=None, op0=ALU.add), [s_])
        s_ = SPV([tve])
        tln = P("scalar", lambda e: e.activation(rstd7_sb[:], var7_sb[:], AF.Ln,
               bias=0.0, scale=1.0), [s_])
        sa_ = SPA([tln])
        tsx = P("scalar", lambda e: e.activation(rstd7_sb[:], rstd7_sb[:], AF.Exp,
               bias=0.0, scale=-0.5), [sa_])
        sa2_ = SPA([tsx])
        tcbs = P("vector", lambda e: e.tensor_copy(out=mu7b_sb[:], in_=mu7_sb[:]),
                [sa2_, prev_bc7[0]])
        tcbs = P("vector", lambda e: e.tensor_copy(out=rstd7b_sb[:], in_=rstd7_sb[:]),
                [tcbs])
        tcbs = SPV([tcbs])
        tbmu = P("tensor", lambda e: e.matmul(mu7PS[:], ones_b[0:1, :], mu7b_sb[:],
                start=True, stop=True), [tcbs, stat_free7[0]], fence=True)
        tbrs = P("tensor", lambda e: e.matmul(rs7PS[:], ones_b[0:1, :], rstd7b_sb[:],
                start=True, stop=True), [tcbs, stat_free7[0]], fence=True)
        prev_bc7[0] = tbrs
        if DBG:
            sc.dma("sync", dbg_ln2[:, half * 1024:half * 1024 + 512], mu7_sb[:], [tbmu])
            sc.dma("sync", dbg_ln2[:, half * 1024 + 512:half * 1024 + 1024],
                   rstd7_sb[:], [tbmu])
        prev_n1 = None
        for oc in range(NEC):
            tld = sc.dma("gpsimd", x2pre_sb[:],
                         x2pre_sc[oc, :, half * 512:(half + 1) * 512],
                         [prev_str, prev_n1] + stat_mm)
            ts1 = P("vector", lambda e: e.tensor_sub(out=x2n_sb[:], in0=x2pre_sb[:],
                   in1=mu7PS[:]), [tld, prev_out_store[0]], pw=[tbmu, tbrs])
            prev_n1 = ts1
            ts2 = P("vector", lambda e: e.tensor_mul(out=x2n_sb[:], in0=x2n_sb[:],
                   in1=rs7PS[:]), [SPV([ts1]), tbrs])
            prev_out_store[0] = sc.dma("sync",
                   outT[oc * 128:(oc + 1) * 128, half * 512:(half + 1) * 512],
                   x2n_sb[:], [ts2])
        stat_free7[0] = prev_n1

    if DBG:
        endt7 = sc.snap()
        sc.dma("sync", dbg_x2pre, x2pre_sc[0].rearrange("p s -> p s"), endt7)
        sc.dma("sync", dbg_w2s, w2bf_sc[0, :, 0:1024].rearrange("p s -> p s"), endt7)
    p7.close()
    if DBG:
        endt = sc.snap()
        sc.dma("sync", dbg_qrT, qrT_sc[0].rearrange("p s -> p s"), endt)
        sc.dma("sync", dbg_krT, krT_sc[0].rearrange("p s -> p s"), endt)
        sc.dma("sync", dbg_v, v_sc[0].rearrange("p s -> p s"), endt)
        sc.dma("sync", dbg_attnT, attnT_sc[0].rearrange("p s -> p s"), endt)
        sc.dma("sync", dbg_x1, x1_sc[0].rearrange("p s -> p s"), endt)
        sc.dma("sync", dbg_ht, hT_sc[0].rearrange("p s -> p s"), endt)
    return finish()


def _rope_tables():
    inv = 1.0 / 10000.0 ** (np.arange(0, D, 2, dtype=np.float32) / D)
    t = np.arange(S, dtype=np.float32)
    fr = np.outer(t, inv)
    emb = np.concatenate([fr, fr], -1)
    cos = np.cos(emb).astype(np.float32)
    nsin = np.sin(emb).astype(np.float32)
    nsin[:, :D // 2] *= -1.0
    return cos, nsin


_NC_CACHE = [None]


def kernel(**inp):
    x = inp["L0_attn_hidden_states"]
    Wq, Wk, Wv, Wo = (inp["L0_attn_Wq"], inp["L0_attn_Wk"], inp["L0_attn_Wv"],
                      inp["L0_attn_Wo"])
    p1_, b1_ = inp["L0_attn_hash_proj1"], inp["L0_attn_hash_bias1"]
    p2_, b2_ = inp["L0_attn_hash_proj2"], inp["L0_attn_hash_bias2"]
    w1, s1 = inp["L0_ffn_w1"], inp["L0_ffn_s1"]
    w2, s2 = inp["L0_ffn_w2"], inp["L0_ffn_s2"]
    w3, s3 = inp["L0_ffn_w3"], inp["L0_ffn_s3"]

    cos, nsin = _rope_tables()
    cosM = np.ascontiguousarray(cos.reshape(NTB, 128, D).transpose(1, 0, 2))
    nsinM = np.ascontiguousarray(nsin.reshape(NTB, 128, D).transpose(1, 0, 2))
    wqTt = np.ascontiguousarray(
        Wq.T.reshape(NEC, 128, NEC, 128).transpose(2, 1, 0, 3).reshape(NEC, 128, NEC * 128))
    wkvT = np.concatenate([Wk.T, Wv.T], axis=1)
    wkvTt = np.ascontiguousarray(
        wkvT.reshape(NEC, 128, 16, 128).transpose(2, 1, 0, 3).reshape(16, 128, NEC * 128))
    woTt = np.ascontiguousarray(
        Wo.T.reshape(NEC, 128, NEC, 128).transpose(2, 1, 0, 3).reshape(NEC, 128, NEC * 128))
    w1p = np.ascontiguousarray(w1.reshape(NHC, 128, 32, 64).transpose(0, 3, 2, 1))
    w3p = np.ascontiguousarray(w3.reshape(NHC, 128, 32, 64).transpose(0, 3, 2, 1))
    w2p = np.ascontiguousarray(w2.reshape(NEC, 128, NHC, 64).transpose(0, 3, 2, 1))
    s1t = np.ascontiguousarray(s1.reshape(NHC, 128, 64).transpose(0, 2, 1)
                               ).reshape(NHC * 64, 128)
    s3t = np.ascontiguousarray(s3.reshape(NHC, 128, 64).transpose(0, 2, 1)
                               ).reshape(NHC * 64, 128)
    s2t = np.ascontiguousarray(s2.reshape(NEC, 128, 172).transpose(0, 2, 1)
                               ).reshape(NEC * 172, 128)
    b1c = np.ascontiguousarray(b1_[:, 0, :].T)
    b2c = np.ascontiguousarray(b2_[:, 0, :].T)
    iotaF = np.broadcast_to(np.arange(S, dtype=np.float32)[None, :], (128, S)).copy()
    identM = np.eye(128, dtype=np.float32)
    onesM = np.ones((128, 128), dtype=np.float32)

    in_maps = []
    for core in range(8):
        b = core // 4
        j = core % 4
        blks = [blk_of(j, s_) for s_ in range(NSLOT)]
        xTb = np.ascontiguousarray(x[b].T)
        xqb = np.ascontiguousarray(
            np.concatenate([xTb[:, bk * 128:(bk + 1) * 128] for bk in blks], axis=1))
        cosQ = np.ascontiguousarray(cosM[:, blks, :])
        nsinQ = np.ascontiguousarray(nsinM[:, blks, :])
        rowvb = (np.array(blks, dtype=np.float32)[None, :] * 128
                 + np.arange(128, dtype=np.float32)[:, None]).astype(np.float32)
        in_maps.append(dict(
            xT=xTb, xq=xqb, wqTt=wqTt, wkvTt=wkvTt, woTt=woTt,
            hp1=p1_, hp2=p2_, b1c=b1c, b2c=b2c,
            cosM=cosM, nsinM=nsinM, cosQ=cosQ, nsinQ=nsinQ, rowv=rowvb,
            iotaF=iotaF, identM=identM, onesM=onesM,
            zers=np.zeros((128, 64), np.float32),
            w1p=w1p, w3p=w3p, w2p=w2p, s1t=s1t, s3t=s3t, s2t=s2t,
        ))

    if _NC_CACHE[0] is None:
        _NC_CACHE[0] = build_nc()
    nc = _NC_CACHE[0]
    res = run_bass_kernel_spmd(nc, in_maps, core_ids=list(range(8)))

    out = np.empty((2, S, E), dtype=np.float32)
    for core in range(8):
        b = core // 4
        j = core % 4
        blks = [blk_of(j, s_) for s_ in range(NSLOT)]
        oT = res.results[core]["outT"]
        for si, bk in enumerate(blks):
            out[b, bk * 128:(bk + 1) * 128, :] = oT[:, si * 128:(si + 1) * 128].T
    return out


# version 06 (final)
# speedup vs baseline: 1.0200x; 1.0178x over previous
import os
import numpy as np
import concourse.bass as bass
import concourse.mybir as mybir
from concourse.bass_utils import run_bass_kernel_spmd

F32 = mybir.dt.float32
BF16 = mybir.dt.bfloat16
I32 = mybir.dt.int32
AF = mybir.ActivationFunctionType
ALU = mybir.AluOpType
AXX = mybir.AxisListType.X

S, E, H, KVH, D = 4096, 4096, 32, 8, 128
GRP = H // KVH
HID = 11008
NHC = HID // 128
NEC = E // 128
NTB = S // 128
KEEP = 820.0
SQD = float(1.0 / np.sqrt(D))
NEGC = -1000.0

SLOT_EXT = [4, 32, 8, 28, 12, 24, 16, 20]
SLOT_OFF = [0]
for e_ in SLOT_EXT:
    SLOT_OFF.append(SLOT_OFF[-1] + e_)
SIMW = SLOT_OFF[-1] * 128
NSLOT = 8

def blk_of(j, s):
    k, hi = divmod(s, 2)
    return (31 - j - 4 * k) if hi else (j + 4 * k)

ENGINES = ("tensor", "vector", "scalar", "gpsimd", "sync")


class Sched:
    def __init__(self):
        self.ops = {e: [] for e in ENGINES}
        self.cnt = {"pe": 0, "dv": 0, "ac": 0}
        self.esem = {"tensor": "pe", "vector": "dv", "scalar": "ac"}
        self.dum = {}

    def op(self, eng, fn, waits=(), inc=True, pw=None, fence=False):
        if pw:
            spfn = self.dum.get(eng)
            if spfn is not None:
                self.op(eng, spfn, list(pw), inc=True)
                self.op(eng, spfn, [], inc=True)
            waits = list(waits) + list(pw)
        if os.environ.get("SIM_STRICT"):
            own = self.esem[eng]
            waits = list(waits) + [(own, self.cnt[own], None)]
        wl = {}
        for t in waits:
            if t is None:
                continue
            sem, val, src = t
            if (src == eng and not sem.startswith("q_")
                    and not os.environ.get("SIM_STRICT")):
                continue
            if wl.get(sem, -1) < val:
                wl[sem] = val
        tok = None
        if inc:
            sem = self.esem[eng]
            self.cnt[sem] += 1
            tok = (sem, self.cnt[sem], eng)
        self.ops[eng].append((sorted(wl.items()), fn, (sem, 1) if inc else None,
                              fence))
        return tok

    def dma(self, qeng, dst, src, waits=(), sem=None):
        if sem is None:
            sem = "q_" + dst.tensor.name
        else:
            sem = "q_" + sem
        wl = {}
        for t in waits:
            if t is None:
                continue
            s_, v_, src_ = t
            if wl.get(s_, -1) < v_:
                wl[s_] = v_
        self.cnt[sem] = self.cnt.get(sem, 0) + 16
        self.ops[qeng].append((sorted(wl.items()),
                               lambda e, d=dst, s=src: e.dma_start(out=d, in_=s),
                               (sem, 16), False))
        return (sem, self.cnt[sem], qeng)

    def snap(self):
        return [(sem, v, None) for sem, v in self.cnt.items() if v > 0]


def build_nc():
    from contextlib import ExitStack
    nc = bass.Bass("TRN2", target_bir_lowering=False, debug=False)

    def din(name, shape, dtype=F32):
        return nc.dram_tensor(name, list(shape), dtype, kind="ExternalInput").ap()

    xT = din("xT", (E, S))
    xq = din("xq", (E, 1024))
    wqTt = din("wqTt", (NEC, 128, NEC * 128))
    wkvTt = din("wkvTt", (16, 128, NEC * 128))
    woTt = din("woTt", (NEC, 128, NEC * 128))
    hp1 = din("hp1", (H, D, D))
    hp2 = din("hp2", (H, D, D))
    b1c = din("b1c", (D, H))
    b2c = din("b2c", (D, H))
    cosM = din("cosM", (128, NTB, D))
    nsinM = din("nsinM", (128, NTB, D))
    cosQ = din("cosQ", (128, NSLOT, D))
    nsinQ = din("nsinQ", (128, NSLOT, D))
    rowv = din("rowv", (128, NSLOT))
    iotaF = din("iotaF", (128, S))
    identM = din("identM", (128, 128))
    zers = din("zers", (128, 64))
    onesM = din("onesM", (128, 128))
    w1p = din("w1p", (NHC, 64, 32, 128), I32)
    w3p = din("w3p", (NHC, 64, 32, 128), I32)
    w2p = din("w2p", (NEC, 64, NHC, 128), I32)
    s1t = din("s1t", (NHC * 64, 128))
    s3t = din("s3t", (NHC * 64, 128))
    s2t = din("s2t", (NEC * 172, 128))
    outT = nc.dram_tensor("outT", [E, 1024], F32, kind="ExternalOutput").ap()
    DBG = globals().get("DEBUG_DUMPS", False)
    if DBG:
        dbg_qrT = nc.dram_tensor("dbg_qrT", [128, 1024], BF16, kind="ExternalOutput").ap()
        dbg_krT = nc.dram_tensor("dbg_krT", [128, S], BF16, kind="ExternalOutput").ap()
        dbg_v = nc.dram_tensor("dbg_v", [128, S], BF16, kind="ExternalOutput").ap()
        dbg_attnT = nc.dram_tensor("dbg_attnT", [128, 1024], BF16, kind="ExternalOutput").ap()
        dbg_x1 = nc.dram_tensor("dbg_x1", [128, 1024], F32, kind="ExternalOutput").ap()
        dbg_sim = nc.dram_tensor("dbg_sim", [128, SIMW], BF16, kind="ExternalOutput").ap()
        dbg_kh = nc.dram_tensor("dbg_kh", [128, S], BF16, kind="ExternalOutput").ap()
        dbg_qh = nc.dram_tensor("dbg_qh", [128, 1024], BF16, kind="ExternalOutput").ap()
        dbg_small = nc.dram_tensor("dbg_small", [128, 8], F32, kind="ExternalOutput").ap()
        dbg_mneg = nc.dram_tensor("dbg_mneg", [128, 1], F32, kind="ExternalOutput").ap()
        dbg_ht = nc.dram_tensor("dbg_ht", [128, 1024], BF16, kind="ExternalOutput").ap()
        dbg_nrm = nc.dram_tensor("dbg_nrm", [1, 4], F32, kind="ExternalOutput").ap()
        dbg_pn = nc.dram_tensor("dbg_pn", [1, 128], F32, kind="ExternalOutput").ap()
        dbg_x2pre = nc.dram_tensor("dbg_x2pre", [128, 1024], F32, kind="ExternalOutput").ap()
        dbg_ln2 = nc.dram_tensor("dbg_ln2", [1, 2048], F32, kind="ExternalOutput").ap()
        dbg_w2s = nc.dram_tensor("dbg_w2s", [128, 1024], BF16, kind="ExternalOutput").ap()

    def scr(name, shape, dtype):
        return nc.dram_tensor(name, list(shape), dtype, kind="Internal").ap()

    qrT_sc = scr("qrT_sc", (H, 128, 1024), BF16)
    krT_sc = scr("krT_sc", (KVH, 128, S), BF16)
    v_sc = scr("v_sc", (KVH, 128, S), BF16)
    attnT_sc = scr("attnT_sc", (H, 128, 1024), BF16)
    x1pre_sc = scr("x1pre_sc", (NEC, 128, 1024), F32)
    x1_sc = scr("x1_sc", (NEC, 128, 1024), F32)
    x1bf_sc = scr("x1bf_sc", (NEC, 128, 1024), BF16)
    hT_sc = scr("hT_sc", (NHC, 128, 1024), BF16)
    w2bf_sc = scr("w2bf_sc", (NEC, 128, HID), BF16)
    x2pre_sc = scr("x2pre_sc", (NEC, 128, 1024), F32)
    s1b_sc = scr("s1b_sc", (NHC * 64, 128), BF16)
    s3b_sc = scr("s3b_sc", (NHC * 64, 128), BF16)
    s2b_sc = scr("s2b_sc", (NEC * 172, 128), BF16)

    sc = Sched()
    P = sc.op
    ctx = ExitStack()
    sems = {}
    SB = lambda st, n, sh, dt: st.enter_context(nc.sbuf_tensor(n, list(sh), dt))
    PS = lambda st, n, sh, dt: st.enter_context(nc.psum_tensor(n, list(sh), dt))

    ident_b = SB(ctx, "ident_b", (128, 128), BF16)
    ones_b = SB(ctx, "ones_b", (128, 128), BF16)
    mneg_sb = SB(ctx, "mneg_sb", (128, 1), F32)
    dumv_sb = SB(ctx, "dumv_sb", (128, 64), F32)
    duma_sb = SB(ctx, "duma_sb", (128, 64), F32)
    dumPS = PS(ctx, "dumPS", (128, 512), F32)

    def SPV(waits=()):
        return P("vector", lambda e: e.tensor_copy(out=dumv_sb[:, 32:64],
                 in_=dumv_sb[:, 0:32]), list(waits))

    def SPA(waits=()):
        return P("scalar", lambda e: e.activation(duma_sb[:, 32:64],
                 duma_sb[:, 0:32], AF.Copy, bias=0.0, scale=1.0), list(waits))

    sc.dum["vector"] = lambda e: e.tensor_copy(out=dumv_sb[:, 32:64],
                                               in_=dumv_sb[:, 0:32])
    sc.dum["scalar"] = lambda e: e.activation(duma_sb[:, 32:64], duma_sb[:, 0:32],
                                              AF.Copy, bias=0.0, scale=1.0)

    P("vector", lambda e: e.memset(dumv_sb[:], 0.0), [])
    tdz = sc.dma("gpsimd", duma_sb[:], zers)
    sc.ops["scalar"].append(([("q_duma_sb", 16)], lambda e: e.activation(
        duma_sb[:, 32:64], duma_sb[:, 0:32], AF.Copy, bias=0.0, scale=1.0),
        ("ac", 1), False))
    sc.cnt["ac"] += 1
    t_ld = [sc.dma("gpsimd", ident_b[:], identM),
            sc.dma("gpsimd", ones_b[:], onesM)]

    p0 = ExitStack()
    sclf = SB(p0, "sclf", (128, 5504), F32)
    sclb = SB(p0, "sclb", (128, 5504), BF16)
    tok_scl = []
    prev = None
    for (src_, dst_, nrow) in ((s1t, s1b_sc, NHC * 64), (s3t, s3b_sc, NHC * 64),
                               (s2t, s2b_sc, NEC * 172)):
        nch = nrow // 128
        sv = src_.rearrange("(n p) c -> p n c", p=128)
        dv_ = dst_.rearrange("(n p) c -> p n c", p=128)
        tl = sc.dma("gpsimd", sclf[:, :nch * 128].rearrange("p (n c) -> p n c", c=128),
                    sv, [prev])
        tc_ = P("vector", lambda e, n=nch: e.tensor_copy(out=sclb[:, :n * 128],
                in_=sclf[:, :n * 128]), [tl, prev])
        ts_ = sc.dma("sync", dv_, sclb[:, :nch * 128].rearrange("p (n c) -> p n c", c=128),
                     [tc_])
        tok_scl.append(ts_)
        prev = ts_

    p1 = ExitStack()
    xres_sb = SB(p1, "xres_sb", (128, NEC * 2048), BF16)
    wtile_sb = [SB(p1, f"wtile{i}", (128, NEC * 128), BF16) for i in range(2)]
    panel_sb = SB(p1, "panel_sb", (128, 2048), BF16)
    tm_sb = SB(p1, "tm_sb", (128, 128), BF16)
    rope_a = SB(p1, "rope_a", (128, 128), BF16)
    rope_b = SB(p1, "rope_b", (128, 128), BF16)
    cos_sb = SB(p1, "cos_sb", (128, D), BF16)
    nsin_sb = SB(p1, "nsin_sb", (128, D), BF16)
    ftile_sb = SB(p1, "ftile_sb", (128, 128), BF16)
    sqp_sb = SB(p1, "sqp_sb", (128, 2048), BF16)
    nacc_sb = SB(p1, "nacc_sb", (1, 4), F32)
    colz_sb = SB(p1, "colz_sb", (1, 1), BF16)
    kvqPS = PS(p1, "kvqPS", (128, 2048), F32)
    trPS = [PS(p1, f"trPS{i}", (128, 1024), BF16) for i in range(2)]
    nrmPS = PS(p1, "nrmPS", (128, 512), F32)

    tz = P("vector", lambda e: e.memset(nacc_sb[:], 0.0), [])
    tz = SPV([tz])
    norm_state = {"acc": tz, "sq": None, "ps": None, "i": 0}
    pn_sb = SB(p1, "pn_sb", (1, 128), F32) if DBG else None
    if DBG:
        P("vector", lambda e: e.memset(pn_sb[:], 0.0), [])

    def panel_norm(width, mm_last):
        tsq_ = P("scalar", lambda e, w_=width: e.activation(sqp_sb[:, 0:w_],
                kvqPS[:, 0:w_], AF.Square, bias=0.0, scale=1.0),
                [norm_state["sq"]], pw=[mm_last])
        tsqf = SPA([tsq_])
        for c4 in range(width // 512):
            tnm = P("tensor", lambda e, c_=c4: e.matmul(nrmPS[0:1, :],
                   ones_b[:, 0:1], sqp_sb[:, c_ * 512:(c_ + 1) * 512],
                   start=True, stop=True), [tsqf, norm_state["ps"]] + t_ld,
                   fence=True)
            trd_ = P("vector", lambda e: e.tensor_reduce(out=nacc_sb[:, 1:2],
                    in_=nrmPS[0:1, :], axis=AXX, op=ALU.max),
                    [norm_state["acc"]], pw=[tnm])
            norm_state["ps"] = trd_
            s1_ = SPV([trd_])
            tmx_ = P("vector", lambda e: e.tensor_tensor(out=nacc_sb[:, 0:1],
                    in0=nacc_sb[:, 0:1], in1=nacc_sb[:, 1:2], op=ALU.max), [s1_])
            norm_state["acc"] = SPV([tmx_])
            if DBG and norm_state["i"] < 128:
                i_ = norm_state["i"]
                norm_state["i"] += 1
                P("vector", lambda e, ii=i_: e.tensor_copy(out=pn_sb[:, ii:ii + 1],
                  in_=nacc_sb[0:1, 1:2]), [norm_state["acc"]])
                norm_state["acc"] = SPV()
        norm_state["sq"] = norm_state["acc"]
        return tsq_
    wt_tok = [None, None]
    tr_tok = [None, None]
    tr_pe = [None, None]
    tmj = {"tm": None, "ra": None, "rb": None, "ft": None, "cs": None}
    panel_tok = []
    tp_i = [0]

    def transpose_to(src_ap, waits):
        x = tp_i[0] % 2
        tp_i[0] += 1
        t = P("tensor", lambda e, s_=src_ap, x_=x: e.transpose(trPS[x_][:, 0:128], s_,
              ident_b[:]), list(waits) + [tr_tok[x]] + t_ld, fence=True)
        tr_pe[x] = t
        return trPS[x][:, 0:128], t, x

    def evac_tr(dst_ap, psrc, pe_tok, x, extra=()):
        t = P("vector", lambda e, d_=dst_ap, s_=psrc: e.tensor_copy(out=d_, in_=s_),
              list(extra), pw=[pe_tok])
        tr_tok[x] = t
        return t

    for th in range(2):
        tx_toks = []
        for ec in range(NEC):
            src = xT.rearrange("(ne p) s -> p ne s", p=128)[:, ec, th * 2048:(th + 1) * 2048]
            tx_toks.append(sc.dma("gpsimd", xres_sb[:, ec * 2048:(ec + 1) * 2048], src,
                                  sc.snap() if (th > 0 and ec == 0) else []))
        for dkv in range(16):
            wb = dkv % 2
            twl = sc.dma("gpsimd", wtile_sb[wb][:], wkvTt[dkv], [wt_tok[wb]])
            mms = []
            for ec in range(NEC):
                for q4 in range(4):
                    mms.append(P("tensor", lambda e, ec_=ec, q_=q4, wb_=wb: e.matmul(
                        kvqPS[:, q_ * 512:(q_ + 1) * 512],
                        wtile_sb[wb_][:, ec_ * 128:(ec_ + 1) * 128],
                        xres_sb[:, ec_ * 2048 + q_ * 512: ec_ * 2048 + (q_ + 1) * 512],
                        start=(ec_ == 0), stop=(ec_ == NEC - 1)),
                        [twl, tx_toks[-1]] + (list(panel_tok) if ec == 0 else []),
                        fence=(ec == NEC - 1)))
            wt_tok[wb] = mms[-1]
            is_k = dkv < 8
            tsqtok = panel_norm(2048, mms[-1]) if is_k else None
            tev = P("vector", lambda e: e.tensor_copy(out=panel_sb[:], in_=kvqPS[:]),
                    [], pw=mms[-1:])
            panel_tok.clear()
            panel_tok.extend([tev] + ([tsqtok] if tsqtok else []))
            tev = SPV([tev])
            kvh = dkv if is_k else dkv - 8
            for tbl in range(16):
                tb = th * 16 + tbl
                ps1, tpe1, x1_ = transpose_to(panel_sb[:, tbl * 128:(tbl + 1) * 128],
                                              [tev])
                t_tm = evac_tr(tm_sb[:], ps1, tpe1, x1_, [tmj["tm"]])
                t_tm = SPV([t_tm])
                tmj["tm"] = None
                if is_k:
                    tcs = sc.dma("gpsimd", cos_sb[:], cosM[:, tb, :], [tmj["cs"]])
                    tns = sc.dma("gpsimd", nsin_sb[:], nsinM[:, tb, :], [tmj["cs"]])
                    ta = P("vector", lambda e: e.tensor_mul(out=rope_a[:], in0=tm_sb[:],
                          in1=cos_sb[:]), [t_tm, tcs, tmj.get("t2")])
                    tb1 = P("vector", lambda e: e.tensor_mul(out=rope_b[:, 0:64],
                           in0=tm_sb[:, 64:128], in1=nsin_sb[:, 0:64]), [t_tm, tns])
                    tb2 = P("vector", lambda e: e.tensor_mul(out=rope_b[:, 64:128],
                           in0=tm_sb[:, 0:64], in1=nsin_sb[:, 64:128]), [t_tm, tns])
                    sp_ = SPV([ta, tb1, tb2])
                    tsum = P("vector", lambda e: e.tensor_add(out=rope_a[:], in0=rope_a[:],
                            in1=rope_b[:]), [sp_])
                    tmj["cs"] = tsum
                    spt = SPV([tsum])
                    ps2, tpe2, x2_ = transpose_to(rope_a[:], [spt])
                    tmj["t2"] = tpe2
                    tfe = evac_tr(ftile_sb[:], ps2, tpe2, x2_, [tmj["ft"]])
                    tst = sc.dma("sync", krT_sc[kvh, :, tb * 128:(tb + 1) * 128],
                                 ftile_sb[:], [tfe])
                    tmj["ft"] = tst
                    tmj["tm"] = tsum
                else:
                    tst = sc.dma("sync", v_sc[kvh, :, tb * 128:(tb + 1) * 128],
                                 tm_sb[:], [t_tm])
                    tmj["tm"] = tst

    txq_toks = []
    for ec in range(NEC):
        src = xq.rearrange("(ne p) s -> p ne s", p=128)[:, ec, :]
        txq_toks.append(sc.dma("gpsimd", xres_sb[:, ec * 1024:(ec + 1) * 1024], src,
                               sc.snap() if ec == 0 else []))
    for dq in range(NEC):
        wb = dq % 2
        twl = sc.dma("gpsimd", wtile_sb[wb][:], wqTt[dq], [wt_tok[wb]])
        mms = []
        for ec in range(NEC):
            for q2 in range(2):
                mms.append(P("tensor", lambda e, ec_=ec, q_=q2, wb_=wb: e.matmul(
                    kvqPS[:, q_ * 512:(q_ + 1) * 512],
                    wtile_sb[wb_][:, ec_ * 128:(ec_ + 1) * 128],
                    xres_sb[:, ec_ * 1024 + q_ * 512: ec_ * 1024 + (q_ + 1) * 512],
                    start=(ec_ == 0), stop=(ec_ == NEC - 1)),
                    [twl, txq_toks[-1]] + (list(panel_tok) if ec == 0 else []),
                    fence=(ec == NEC - 1)))
        wt_tok[wb] = mms[-1]
        tsqtok = panel_norm(1024, mms[-1])
        tev = P("vector", lambda e: e.tensor_copy(out=panel_sb[:, 0:1024],
                in_=kvqPS[:, 0:1024]), [], pw=mms[-1:])
        panel_tok.clear()
        panel_tok.extend([tev, tsqtok])
        tev = SPV([tev])
        h = dq
        for s_ in range(NSLOT):
            ps1, tpe1, x1_ = transpose_to(panel_sb[:, s_ * 128:(s_ + 1) * 128], [tev])
            t_tm = evac_tr(tm_sb[:], ps1, tpe1, x1_, [tmj["tm"]])
            t_tm = SPV([t_tm])
            tcs = sc.dma("gpsimd", cos_sb[:], cosQ[:, s_, :], [tmj["cs"]])
            tns = sc.dma("gpsimd", nsin_sb[:], nsinQ[:, s_, :], [tmj["cs"]])
            ta = P("vector", lambda e: e.tensor_mul(out=rope_a[:], in0=tm_sb[:],
                  in1=cos_sb[:]), [t_tm, tcs, tmj.get("t2")])
            tb1 = P("vector", lambda e: e.tensor_mul(out=rope_b[:, 0:64],
                   in0=tm_sb[:, 64:128], in1=nsin_sb[:, 0:64]), [t_tm, tns])
            tb2 = P("vector", lambda e: e.tensor_mul(out=rope_b[:, 64:128],
                   in0=tm_sb[:, 0:64], in1=nsin_sb[:, 64:128]), [t_tm, tns])
            sp_ = SPV([ta, tb1, tb2])
            tsum = P("vector", lambda e: e.tensor_add(out=rope_a[:], in0=rope_a[:],
                    in1=rope_b[:]), [sp_])
            tmj["cs"] = tsum
            spt = SPV([tsum])
            ps2, tpe2, x2_ = transpose_to(rope_a[:], [spt])
            tmj["t2"] = tpe2
            tfe = evac_tr(ftile_sb[:], ps2, tpe2, x2_, [tmj["ft"]])
            tst = sc.dma("sync", qrT_sc[h, :, s_ * 128:(s_ + 1) * 128], ftile_sb[:],
                         [tfe])
            tmj["ft"] = tst

    tcc = P("vector", lambda e: e.tensor_copy(out=colz_sb[:], in_=nacc_sb[:, 0:1]),
           [norm_state["acc"]])
    spm = SPV([tcc])
    tmm_ = P("tensor", lambda e: e.matmul(nrmPS[:, 0:1], ones_b[0:1, :], colz_sb[:],
            start=True, stop=True), [spm, norm_state["ps"]] + t_ld, fence=True)
    tmn = P("vector", lambda e: e.tensor_scalar(out=mneg_sb[:], in0=nrmPS[:, 0:1],
           scalar1=-SQD, scalar2=None, op0=ALU.mult), [], pw=[tmm_])
    tmn = SPV([tmn])
    if DBG:
        sc.dma("sync", dbg_nrm, nacc_sb[0:1, :], [tmn])
        sc.dma("sync", dbg_pn, pn_sb[:], [tmn])

    p1_done = sc.snap()
    p1.close()
    p0.close()

    PHL = globals().get("PHASE_LIMIT", int(os.environ.get("KPHL", "99")))

    def finish():
        fin_ = sc.cnt.get("q_outT", 0)
        sc.ops["sync"].append(([("q_outT", fin_)], lambda e: e.nop(), None, False))
        for k in sc.cnt:
            sems[k] = ctx.enter_context(nc.semaphore(k))
        with nc.Block() as block:
            def runner(engname):
                def run(eng):
                    if engname == "tensor" and "q_ident_b" in sems:
                        eng.wait_ge(sems["q_ident_b"], 16)
                    for waits, fn, inc, fence in sc.ops[engname]:
                        for sem, val in waits:
                            eng.wait_ge(sems[sem], val)
                        inst = fn(eng)
                        if inc is not None:
                            if fence and engname == "tensor" and globals().get("FENCE_ON", True):
                                eng.matmul(dumPS[0:1, 0:1], ident_b[:, 0:1],
                                           ident_b[:, 0:1], start=True,
                                           stop=True).then_inc(sems[inc[0]], inc[1])
                            else:
                                inst.then_inc(sems[inc[0]], inc[1])
                return run
            block.tensor(runner("tensor"))
            block.vector(runner("vector"))
            block.scalar(runner("scalar"))
            block.gpsimd(runner("gpsimd"))
            block.sync(runner("sync"))
        ctx.close()
        return nc

    if PHL == 1:
        sc.dma("gpsimd", outT.rearrange("(h p) s -> h p s", p=128), qrT_sc[:],
               p1_done)
        return finish()

    p3 = ExitStack()
    b1_sb = SB(p3, "b1_sb", (128, H), F32)
    b1h_sb = SB(p3, "b1h_sb", (128, H), F32)
    b2_sb = SB(p3, "b2_sb", (128, H), F32)
    rowv_sb = SB(p3, "rowv_sb", (128, NSLOT), F32)
    iota_sb = SB(p3, "iota_sb", (128, S), F32)
    t_ld3 = [sc.dma("gpsimd", b1_sb[:], b1c),
             sc.dma("gpsimd", b2_sb[:], b2c),
             sc.dma("gpsimd", rowv_sb[:], rowv),
             sc.dma("gpsimd", iota_sb[:], iotaF)]
    tb1h = P("vector", lambda e: e.tensor_scalar(out=b1h_sb[:], in0=b1_sb[:],
        scalar1=0.5, scalar2=None, op0=ALU.mult), t_ld3)
    krT_kv = SB(p3, "krT_kv", (128, S), BF16)
    v_kv = SB(p3, "v_kv", (128, S), BF16)
    qrT_h = SB(p3, "qrT_h", (128, 1024), BF16)
    qhT_h = SB(p3, "qhT_h", (128, 1024), BF16)
    khT_h = SB(p3, "khT_h", (128, S), BF16)
    p1w = [SB(p3, f"p1w{i}", (128, D), BF16) for i in range(2)]
    p2w = [SB(p3, f"p2w{i}", (128, D), BF16) for i in range(2)]
    sim_sb = SB(p3, "sim_sb", (128, SIMW), BF16)
    cmt_sb = SB(p3, "cmt_sb", (128, NSLOT * 512), BF16)
    hsh_t = SB(p3, "hsh_t", (128, 512), BF16)
    hsh_u = SB(p3, "hsh_u", (128, 512), BF16)
    hsh_r = SB(p3, "hsh_r", (128, 512), BF16)
    z1_sb = SB(p3, "z1_sb", (128, 512), BF16)
    wsg_sb = SB(p3, "wsg_sb", (128, 512), BF16)
    e_sb = [SB(p3, f"e_sb{i}", (128, 512), BF16) for i in range(2)]
    pp_sb = [SB(p3, f"pp_sb{i}", (128, 512), BF16) for i in range(2)]
    pt_sb = [SB(p3, f"pt_sb{i}", (128, 512), BF16) for i in range(2)]
    srch_sb = SB(p3, "srch_sb", (128, 4096), BF16)
    tT_sb = SB(p3, "tT_sb", (128, NSLOT), F32)
    tC_sb = SB(p3, "tC_sb", (128, NSLOT), F32)
    cnt_sb = SB(p3, "cnt_sb", (128, NSLOT), F32)
    ok_sb = SB(p3, "ok_sb", (128, NSLOT), F32)
    den_sb = SB(p3, "den_sb", (128, 8), F32)
    den1_sb = SB(p3, "den1_sb", (128, 1), F32)
    rden_sb = SB(p3, "rden_sb", (128, 1), F32)
    attn_sb = SB(p3, "attn_sb", (128, D), BF16)
    attnT_sb = SB(p3, "attnT_sb", (128, D), BF16)
    simPS = [PS(p3, f"simPS{i}", (128, 512), F32) for i in range(2)]
    scPS = [PS(p3, f"scPS{i}", (128, 512), F32) for i in range(2)]
    ptPS = [PS(p3, f"ptPS{i}", (128, 1024), BF16) for i in range(2)]
    pvPS = PS(p3, "pvPS", (128, 512), F32)

    cm_toks = []
    for s_ in range(NSLOT):
        a0 = (SLOT_EXT[s_] - 4) * 128
        cm_toks.append(P("vector", lambda e, s__=s_, a0_=a0: e.tensor_scalar(
            out=cmt_sb[:, s__ * 512:(s__ + 1) * 512], in0=iota_sb[:, a0_:a0_ + 512],
            scalar1=rowv_sb[:, s__:s__ + 1], scalar2=NEGC, op0=ALU.is_gt, op1=ALU.mult),
            p1_done + t_ld3))

    st_tex = [None, None]
    st_tap = [None, None]
    st_ttr = [None, None]
    st_tevp = [None, None]
    st_pv = [None, None]
    st_sev = [None, None]
    st_smm = [None, None]
    last_ttra = [None]
    last_teva = [None]
    last_tdiv = [None]
    hb_tok = [None, None]
    qr_free = [None]
    kv_free = [None]
    kh_free = [None]

    hash_t6 = [None]

    def hash_seq(dst, src, nchunk, pb1, pb2, t_w1, t_w2, t_src, h):
        toks = []
        for c in range(nchunk):
            cs = slice(c * 512, (c + 1) * 512)
            t1 = P("tensor", lambda e, cs_=cs, pb_=pb1, s_=src: e.matmul(scPS[0][:],
                  pb_[:], s_[:, cs_], start=True, stop=True),
                  [t_w1, t_src] + toks[-1:] + [st_tex[0], st_tex[1], tb1h] + t_ld3)
            t2 = P("scalar", lambda e, h_=h: e.activation(hsh_t[:], scPS[0][:], AF.Tanh,
                  bias=b1h_sb[:, h_:h_ + 1], scale=0.5), t_ld, pw=[t1])
            t3 = P("vector", lambda e, h_=h: e.scalar_tensor_tensor(out=hsh_u[:],
                  in0=scPS[0][:], scalar=b1_sb[:, h_:h_ + 1], in1=hsh_t[:],
                  op0=ALU.add, op1=ALU.mult), [SPA([t2])])
            t4 = P("vector", lambda e, h_=h: e.scalar_tensor_tensor(out=hsh_r[:],
                  in0=scPS[0][:], scalar=b1_sb[:, h_:h_ + 1], in1=hsh_u[:],
                  op0=ALU.add, op1=ALU.add), [SPV([t3])])
            t5 = P("vector", lambda e, cs_=cs, s_=src: e.scalar_tensor_tensor(
                  out=z1_sb[:], in0=hsh_r[:], scalar=0.5, in1=s_[:, cs_],
                  op0=ALU.mult, op1=ALU.add), [SPV([t4]), hash_t6[0]])
            t6 = P("tensor", lambda e, pb_=pb2: e.matmul(scPS[1][:], pb_[:], z1_sb[:],
                  start=True, stop=True), [SPV([t5]), t_w2, st_tex[1]], fence=True)
            hash_t6[0] = t6
            t7 = P("vector", lambda e, h_=h: e.scalar_tensor_tensor(out=wsg_sb[:],
                  in0=scPS[1][:], scalar=b2_sb[:, h_:h_ + 1], in1=z1_sb[:],
                  op0=ALU.add, op1=ALU.add), [], pw=[t6])
            t8 = P("vector", lambda e, cs_=cs, d_=dst: e.tensor_scalar(out=d_[:, cs_],
                  in0=wsg_sb[:], scalar1=0.0, scalar2=0.5, op0=ALU.is_ge,
                  op1=ALU.subtract), [SPV([t7])])
            toks.append(t8)
        return toks

    for kvh in range(KVH):
        tkr = sc.dma("gpsimd", krT_kv[:], krT_sc[kvh].rearrange("p s -> p s"),
                     p1_done + [kv_free[0]])
        tv = sc.dma("gpsimd", v_kv[:], v_sc[kvh].rearrange("p s -> p s"),
                    p1_done + [kv_free[0]])
        for g in range(GRP):
            h = kvh * GRP + g
            hb = h % 2
            tw1 = sc.dma("gpsimd", p1w[hb][:], hp1[h], [hb_tok[hb]])
            tw2 = sc.dma("gpsimd", p2w[hb][:], hp2[h], [hb_tok[hb]])
            tqr = sc.dma("gpsimd", qrT_h[:], qrT_sc[h].rearrange("p s -> p s"),
                         p1_done + [qr_free[0]])
            kh_toks = hash_seq(khT_h, krT_kv, 8, p1w[hb], p2w[hb], tw1, tw2, tkr, h)
            qh_toks = hash_seq(qhT_h, qrT_h, 2, p1w[hb], p2w[hb], tw1, tw2, tqr, h)
            hb_tok[hb] = qh_toks[-1]
            hash_done = [kh_toks[-1], qh_toks[-1]]
            if kh_free[0] is not None:
                pass
            hsp = SPV(hash_done)
            for s_ in range(NSLOT):
                ext = SLOT_EXT[s_]
                off = SLOT_OFF[s_] * 128
                nch = ext * 128 // 512
                for c in range(nch):
                    x = c % 2
                    tmm = P("tensor", lambda e, x_=x, s__=s_, c_=c: e.matmul(
                        simPS[x_][:], qhT_h[:, s__ * 128:(s__ + 1) * 128],
                        khT_h[:, c_ * 512:(c_ + 1) * 512], start=True, stop=True),
                        [hsp, st_sev[x], kh_free[0]], fence=True)
                    st_smm[x] = tmm
                    if c == nch - 1:
                        tev = P("vector", lambda e, x_=x, s__=s_, c_=c, o_=off:
                            e.tensor_add(out=sim_sb[:, o_ + c_ * 512:o_ + (c_ + 1) * 512],
                                         in0=simPS[x_][:],
                                         in1=cmt_sb[:, s__ * 512:(s__ + 1) * 512]),
                            cm_toks, pw=[tmm])
                    else:
                        tev = P("vector", lambda e, x_=x, c_=c, o_=off: e.tensor_copy(
                            out=sim_sb[:, o_ + c_ * 512:o_ + (c_ + 1) * 512],
                            in_=simPS[x_][:]), [], pw=[tmm])
                    st_sev[x] = tev
            P("vector", lambda e: e.memset(tT_sb[:], -32.0), [])
            SPV()
            for w_ in (32.0, 16.0, 8.0, 4.0, 2.0, 1.0, 0.5):
                P("vector", lambda e, w__=w_: e.tensor_scalar(
                    out=tC_sb[:], in0=tT_sb[:], scalar1=w__, scalar2=None,
                    op0=ALU.add), [])
                SPV()
                for s_ in range(NSLOT):
                    P("vector", lambda e, s__=s_, o_=SLOT_OFF[s_] * 128,
                      n_=SLOT_EXT[s_] * 128: e.tensor_scalar(
                        out=srch_sb[:, 0:n_], in0=sim_sb[:, o_:o_ + n_],
                        scalar1=tC_sb[:, s__:s__ + 1], scalar2=0.0, op0=ALU.is_ge,
                        op1=ALU.add, accum_out=cnt_sb[:, s__:s__ + 1]), [])
                SPV()
                P("vector", lambda e: e.tensor_scalar(
                    out=ok_sb[:], in0=cnt_sb[:], scalar1=KEEP - 0.5, scalar2=None,
                    op0=ALU.is_ge), [])
                SPV()
                P("vector", lambda e, w__=w_: e.scalar_tensor_tensor(
                    out=tT_sb[:], in0=ok_sb[:], scalar=w__, in1=tT_sb[:],
                    op0=ALU.mult, op1=ALU.add), [])
                SPV()
            for s_ in range(NSLOT):
                ext = SLOT_EXT[s_]
                off = SLOT_OFF[s_] * 128
                nch = ext * 128 // 512
                for c in range(nch):
                    x = c % 2
                    tmm = P("tensor", lambda e, x_=x, s__=s_, c_=c: e.matmul(
                        scPS[x_][:], qrT_h[:, s__ * 128:(s__ + 1) * 128],
                        krT_kv[:, c_ * 512:(c_ + 1) * 512], start=True, stop=True),
                        [tqr, tkr, st_tex[x]] + (hash_done if c < 2 else []),
                        fence=True)
                    tex = P("scalar", lambda e, x_=x: e.activation(e_sb[x_][:],
                          scPS[x_][:], AF.Exp, bias=mneg_sb[:], scale=SQD),
                          [tmn, st_tap[x]], pw=[tmm])
                    st_tex[x] = tex
                    tap = P("vector", lambda e, x_=x, s__=s_, c_=c, o_=off:
                        e.scalar_tensor_tensor(out=pp_sb[x_][:],
                            in0=sim_sb[:, o_ + c_ * 512:o_ + (c_ + 1) * 512],
                            scalar=tT_sb[:, s__:s__ + 1], in1=e_sb[x_][:],
                            op0=ALU.is_ge, op1=ALU.mult,
                            accum_out=den_sb[:, c_:c_ + 1]),
                        [SPA([tex]), st_ttr[x]])
                    st_tap[x] = tap
                    tapf = SPV([tap])
                    ttrs = None
                    for j in range(4):
                        ttrs = P("tensor", lambda e, x_=x, j_=j: e.transpose(
                            ptPS[x_][:, j_ * 128:(j_ + 1) * 128],
                            pp_sb[x_][:, j_ * 128:(j_ + 1) * 128], ident_b[:]),
                            [tapf, st_tevp[x]] + ([last_teva[0]] if last_teva[0] else []),
                            fence=(j_ == 3) if False else (j == 3))
                    st_ttr[x] = ttrs
                    tevp = P("vector", lambda e, x_=x: e.tensor_copy(out=pt_sb[x_][:],
                            in_=ptPS[x_][:, 0:512]), [st_pv[x]], pw=[ttrs])
                    st_tevp[x] = tevp
                    tevpf = SPV([tevp])
                    pv_last = None
                    for j in range(4):
                        kb = c * 4 + j
                        pv_last = P("tensor", lambda e, x_=x, j_=j, kb_=kb,
                            last_=nch * 4 - 1: e.matmul(
                            pvPS[:, 0:128], pt_sb[x_][:, j_ * 128:(j_ + 1) * 128],
                            v_kv[:, kb_ * 128:(kb_ + 1) * 128],
                            start=(kb_ == 0), stop=(kb_ == last_)),
                            [tevpf, tv] + ([last_tdiv[0]] if kb == 0 else []),
                            fence=(kb == nch * 4 - 1))
                    st_pv[x] = pv_last
                sd0 = SPV([st_tap[0], st_tap[1]])
                td1 = P("vector", lambda e, n_=nch: e.tensor_reduce(out=den1_sb[:],
                       in_=den_sb[:, 0:n_], axis=AXX, op=ALU.add), [sd0])
                sd1 = SPV([td1])
                trd = P("vector", lambda e: e.reciprocal(out=rden_sb[:], in_=den1_sb[:]),
                       [sd1])
                sd2 = SPV([trd])
                tdiv = P("vector", lambda e: e.tensor_scalar(out=attn_sb[:],
                        in0=pvPS[:, 0:128], scalar1=rden_sb[:], scalar2=None,
                        op0=ALU.mult), [sd2, last_ttra[0]],
                        pw=[st_pv[0], st_pv[1]])
                last_tdiv[0] = tdiv
                sd3 = SPV([tdiv])
                ttra = P("tensor", lambda e: e.transpose(ptPS[0][:, 0:128], attn_sb[:],
                        ident_b[:]), [sd3, st_tevp[0]], fence=True)
                last_ttra[0] = ttra
                st_ttr[0] = ttra
                teva = P("vector", lambda e: e.tensor_copy(out=attnT_sb[:],
                        in_=ptPS[0][:, 0:128]), [last_teva[0]], pw=[ttra])
                tstA = sc.dma("sync", attnT_sc[h, :, s_ * 128:(s_ + 1) * 128],
                              attnT_sb[:], [teva])
                last_teva[0] = tstA
                st_tevp[0] = teva
            if DBG and h == 0:
                sc.dma("sync", dbg_small, tT_sb[:], [last_tdiv[0]])
                sc.dma("sync", dbg_mneg, mneg_sb[:], [tmn])
                sc.dma("sync", dbg_kh, khT_h[:], [kh_toks[-1]])
                sc.dma("sync", dbg_qh, qhT_h[:], [qh_toks[-1]])
                sc.dma("sync", dbg_sim, sim_sb[:], [last_tdiv[0]])
            qr_free[0] = last_tdiv[0]
            kh_free[0] = st_smm[0] if st_smm[0] else None
            kh_free[0] = (st_smm[0] if (st_smm[1] is None or
                          (st_smm[0] and st_smm[0][1] > st_smm[1][1])) else st_smm[1])
        kv_free[0] = last_tdiv[0]

    p3_done = sc.snap()
    p3.close()

    if PHL == 3:
        sc.dma("gpsimd", outT.rearrange("(h p) s -> h p s", p=128), attnT_sc[:],
               p3_done)
        return finish()

    def ln_block(tag, st, mm_schedule, res_src, pre_sc_, out_fp_sc, out_bf_sc,
                 out_direct, deps0):
        pass

    p5 = ExitStack()
    att_res = SB(p5, "att_res", (128, NEC * 512), BF16)
    wo_sb = [SB(p5, f"wo_sb{i}", (128, NEC * 128), BF16) for i in range(2)]
    x1pre_sb = SB(p5, "x1pre_sb", (128, 512), F32)
    xr_sb = SB(p5, "xr_sb", (128, 512), F32)
    xb_sb = SB(p5, "xb_sb", (128, 512), BF16)
    sq5_sb = SB(p5, "sq5_sb", (128, 512), BF16)
    mu5_sb = SB(p5, "mu5_sb", (1, 512), F32)
    ex25_sb = SB(p5, "ex25_sb", (1, 512), F32)
    var5_sb = SB(p5, "var5_sb", (1, 512), F32)
    rstd5_sb = SB(p5, "rstd5_sb", (1, 512), F32)
    mu5b_sb = SB(p5, "mu5b_sb", (1, 512), BF16)
    rstd5b_sb = SB(p5, "rstd5b_sb", (1, 512), BF16)
    xn_sb = SB(p5, "xn_sb", (128, 512), F32)
    acc1_sb = SB(p5, "acc1_sb", (1, 512), F32)
    acc2_sb = SB(p5, "acc2_sb", (1, 512), F32)
    woPS = PS(p5, "woPS", (128, 512), F32)
    stPS = PS(p5, "stPS", (128, 512), F32)
    st2PS = PS(p5, "st2PS", (128, 512), F32)
    muPS = PS(p5, "muPS", (128, 512), F32)
    rsPS = PS(p5, "rsPS", (128, 512), F32)

    wo_tok = [None, None]
    stat_free = [None]
    stat_acc = {"a1": None, "a2": None}
    prev_xn_store = [None]
    prev_bc5 = [None]
    for tc in range(2):
        ta = sc.dma("gpsimd", att_res[:],
                    attnT_sc[:, :, tc * 512:(tc + 1) * 512].rearrange("h p t -> p h t"),
                    p3_done + [stat_free[0]])
        stat_mm = []
        prev_str = None
        prev_res = None
        prev_cb = None
        for ec in range(NEC):
            wb = ec % 2
            twl = sc.dma("gpsimd", wo_sb[wb][:], woTt[ec], [wo_tok[wb]])
            txr = sc.dma("gpsimd", xr_sb[:],
                         xq.rearrange("(ne p) s -> p ne s", p=128)[:, ec,
                         tc * 512:(tc + 1) * 512], [prev_res])
            mms = []
            for dc in range(NEC):
                mms.append(P("tensor", lambda e, wb_=wb, dc_=dc: e.matmul(
                    woPS[:], wo_sb[wb_][:, dc_ * 128:(dc_ + 1) * 128],
                    att_res[:, dc_ * 512:(dc_ + 1) * 512],
                    start=(dc_ == 0), stop=(dc_ == NEC - 1)),
                    [twl, ta] + ([prev_res] if dc == 0 else []),
                    fence=(dc == NEC - 1)))
            wo_tok[wb] = mms[-1]
            tres = P("vector", lambda e: e.tensor_add(out=x1pre_sb[:], in0=woPS[:],
                    in1=xr_sb[:]), [mms[-1], txr, prev_str])
            prev_res = tres
            tstr = sc.dma("sync", x1pre_sc[ec, :, tc * 512:(tc + 1) * 512], x1pre_sb[:],
                          [tres])
            prev_str = tstr
            tcb = P("vector", lambda e: e.tensor_copy(out=xb_sb[:], in_=x1pre_sb[:]),
                   [SPV([tres]), prev_cb])
            tsq = P("vector", lambda e: e.tensor_mul(out=sq5_sb[:], in0=xb_sb[:],
                   in1=xb_sb[:]), [SPV([tcb])])
            m1 = P("tensor", lambda e: e.matmul(stPS[0:1, :], ones_b[:, 0:1],
                  xb_sb[:], start=True, stop=True),
                  [SPV([tcb]), stat_acc["a1"]] + ([stat_free[0]] if ec == 0 else []),
                  fence=True)
            a1 = P("vector", lambda e, ec_=ec: (
                  e.tensor_copy(out=acc1_sb[:], in_=stPS[0:1, :]) if ec_ == 0 else
                  e.tensor_add(out=acc1_sb[:], in0=acc1_sb[:], in1=stPS[0:1, :])),
                  [SPV()], pw=[m1])
            stat_acc["a1"] = SPV([a1])
            m2 = P("tensor", lambda e: e.matmul(st2PS[0:1, :], ones_b[:, 0:1],
                  sq5_sb[:], start=True, stop=True),
                  [SPV([tsq]), stat_acc["a2"]] + ([stat_free[0]] if ec == 0 else []),
                  fence=True)
            a2 = P("vector", lambda e, ec_=ec: (
                  e.tensor_copy(out=acc2_sb[:], in_=st2PS[0:1, :]) if ec_ == 0 else
                  e.tensor_add(out=acc2_sb[:], in0=acc2_sb[:], in1=st2PS[0:1, :])),
                  [SPV()], pw=[m2])
            stat_acc["a2"] = SPV([a2])
            stat_mm = [stat_acc["a1"], stat_acc["a2"]]
            prev_cb = m2
        tmu = P("vector", lambda e: e.tensor_scalar(out=mu5_sb[:], in0=acc1_sb[:],
               scalar1=1.0 / E, scalar2=None, op0=ALU.mult), stat_mm)
        tex2 = P("vector", lambda e: e.tensor_scalar(out=ex25_sb[:],
                in0=acc2_sb[:], scalar1=1.0 / E, scalar2=None, op0=ALU.mult),
                stat_mm)
        s_ = SPV([tmu, tex2])
        P("vector", lambda e: e.tensor_mul(out=var5_sb[:], in0=mu5_sb[:],
          in1=mu5_sb[:]), [s_])
        s_ = SPV()
        P("vector", lambda e: e.tensor_sub(out=var5_sb[:], in0=ex25_sb[:],
          in1=var5_sb[:]), [s_])
        s_ = SPV()
        tve = P("vector", lambda e: e.tensor_scalar(out=var5_sb[:],
               in0=var5_sb[:], scalar1=1e-5, scalar2=None, op0=ALU.add), [s_])
        s_ = SPV([tve])
        tln = P("scalar", lambda e: e.activation(rstd5_sb[:], var5_sb[:], AF.Ln,
               bias=0.0, scale=1.0), [s_])
        sa_ = SPA([tln])
        tsx = P("scalar", lambda e: e.activation(rstd5_sb[:], rstd5_sb[:], AF.Exp,
               bias=0.0, scale=-0.5), [sa_])
        sa2_ = SPA([tsx])
        tcbs = P("vector", lambda e: e.tensor_copy(out=mu5b_sb[:], in_=mu5_sb[:]),
                [sa2_, prev_bc5[0]])
        tcbs = P("vector", lambda e: e.tensor_copy(out=rstd5b_sb[:], in_=rstd5_sb[:]),
                [tcbs])
        tcbs = SPV([tcbs])
        tbmu = P("tensor", lambda e: e.matmul(muPS[:], ones_b[0:1, :], mu5b_sb[:],
                start=True, stop=True), [tcbs, stat_free[0]], fence=True)
        tbrs = P("tensor", lambda e: e.matmul(rsPS[:], ones_b[0:1, :], rstd5b_sb[:],
                start=True, stop=True), [tcbs, stat_free[0]], fence=True)
        prev_bc5[0] = tbrs
        prev_n1 = None
        prev_sb = None
        prev_sc2 = None
        for ec in range(NEC):
            tld = sc.dma("gpsimd", x1pre_sb[:], x1pre_sc[ec, :, tc * 512:(tc + 1) * 512],
                         [prev_str, prev_n1] + stat_mm)
            ts1 = P("vector", lambda e: e.tensor_sub(out=xn_sb[:], in0=x1pre_sb[:],
                   in1=muPS[:]), [tld, prev_sb, prev_xn_store[0]], pw=[tbmu, tbrs])
            prev_n1 = ts1
            ts2 = P("vector", lambda e: e.tensor_mul(out=xn_sb[:], in0=xn_sb[:],
                   in1=rsPS[:]), [SPV([ts1]), tbrs])
            tso = sc.dma("sync", x1_sc[ec, :, tc * 512:(tc + 1) * 512], xn_sb[:], [ts2])
            prev_xn_store[0] = tso
            tcb2 = P("vector", lambda e: e.tensor_copy(out=xb_sb[:], in_=xn_sb[:]),
                    [SPV([ts2]), prev_sc2])
            prev_sb = tcb2
            tso2 = sc.dma("sync", x1bf_sc[ec, :, tc * 512:(tc + 1) * 512], xb_sb[:],
                          [tcb2])
            prev_sc2 = tso2
            prev_str = tso
        stat_free[0] = prev_n1

    p5_done = sc.snap()
    p5.close()

    if PHL == 5:
        sc.dma("gpsimd", outT.rearrange("(ec p) s -> ec p s", p=128), x1_sc[:],
               p5_done)
        return finish()

    p6 = ExitStack()
    x1t_sb = SB(p6, "x1t_sb", (128, NEC * 1024), BF16)
    pk_sb = [SB(p6, f"pk_sb{i}", (128, 4096), I32) for i in range(2)]
    nib_sb = [SB(p6, f"nib_sb{i}", (128, 4096), I32) for i in range(2)]
    wde_sb = [SB(p6, f"wde_sb{i}", (128, 4096), BF16) for i in range(2)]
    wsc_sb = [SB(p6, f"wsc_sb{i}", (128, 4096), BF16) for i in range(2)]
    sil_sb = SB(p6, "sil_sb", (128, 1024), BF16)
    ht_sb = SB(p6, "ht_sb", (128, 1024), BF16)
    aPS = PS(p6, "aPS", (128, 1024), F32)
    bPS = PS(p6, "bPS", (128, 1024), F32)

    tx1 = [sc.dma("gpsimd", x1t_sb[:, ec * 1024:(ec + 1) * 1024],
                  x1bf_sc[ec].rearrange("p s -> p s"), p5_done) for ec in range(NEC)]

    dq_state = [{"tn": None, "tca": None, "tmu": None, "mm": None, "tl": None}
                for _ in range(2)]

    def dequant(idx, pk_src, scb_src, ncol=4096):
        st = dq_state[idx]
        pk = pk_sb[idx]
        nib = nib_sb[idx]
        wde = wde_sb[idx]
        wsc = wsc_sb[idx]
        tl0 = sc.dma("gpsimd",
                     pk[0:64, 0:ncol].rearrange("p (a b) -> p a b", b=128), pk_src,
                     [st["tn"]])
        tl = sc.dma("gpsimd",
                    pk[64:128, 0:ncol].rearrange("p (a b) -> p a b", b=128), pk_src,
                    [st["tn"], tl0])
        ts0 = sc.dma("gpsimd",
                     wsc[0:64, 0:ncol].rearrange("p (a b) -> p a b", b=128), scb_src[0],
                     [st["tmu"]])
        tsc_ = sc.dma("gpsimd",
                      wsc[64:128, 0:ncol].rearrange("p (a b) -> p a b", b=128),
                      scb_src[1], [st["tmu"], ts0])
        tl = (tl[0], tl[1], tl[2])
        tn1 = P("vector", lambda e, n_=ncol: e.tensor_scalar(out=nib[0:64, 0:n_],
               in0=pk[0:64, 0:n_], scalar1=24.0, scalar2=28.0,
               op0=ALU.logical_shift_left, op1=ALU.arith_shift_right),
               [tl, st["tca"]])
        tn2 = P("vector", lambda e, n_=ncol: e.tensor_scalar(out=nib[64:128, 0:n_],
               in0=pk[64:128, 0:n_], scalar1=28.0, scalar2=28.0,
               op0=ALU.logical_shift_left, op1=ALU.arith_shift_right),
               [tl, st["tca"]])
        tca = P("scalar", lambda e, n_=ncol: e.activation(wde[:, 0:n_], nib[:, 0:n_],
               AF.Copy, bias=0.0, scale=1.0), [SPV([tn1, tn2]), st["mm"]])
        tmu = P("vector", lambda e, n_=ncol: e.tensor_mul(out=wde[:, 0:n_],
               in0=wde[:, 0:n_], in1=wsc[:, 0:n_]), [SPA([tca]), tsc_] + tok_scl)
        st["tn"], st["tca"], st["tmu"] = tn2, tca, tmu
        return SPV([tmu])

    def pk_rep(arr, i0):
        return arr[i0]

    def sc_rep(scb, i0, nj):
        v = scb.rearrange("(i j) h -> i j h", j=2 * nj)[i0]
        v4 = v.rearrange("(jj nib) h -> nib jj h", nib=2)
        return (v4[0].partition_broadcast(64), v4[1].partition_broadcast(64))

    prev_si = [None]
    prev_ht = [None]
    prev_hstore = [None]
    for hc in range(NHC):
        t1w = dequant(0, pk_rep(w1p, hc), sc_rep(s1b_sc, hc, 32))
        mm1 = None
        for J in range(NEC):
            for tk in range(2):
                mm1 = P("tensor", lambda e, J_=J, tk_=tk: e.matmul(
                    aPS[:, tk_ * 512:(tk_ + 1) * 512],
                    wde_sb[0][:, J_ * 128:(J_ + 1) * 128],
                    x1t_sb[:, J_ * 1024 + tk_ * 512: J_ * 1024 + (tk_ + 1) * 512],
                    start=(J_ == 0), stop=(J_ == NEC - 1)),
                    [t1w, tx1[-1]] + ([prev_si[0]] if J == 0 else []),
                    fence=(J == NEC - 1 and tk == 1))
        dq_state[0]["mm"] = mm1
        t3w = dequant(1, pk_rep(w3p, hc), sc_rep(s3b_sc, hc, 32))
        mm3 = None
        for J in range(NEC):
            for tk in range(2):
                mm3 = P("tensor", lambda e, J_=J, tk_=tk: e.matmul(
                    bPS[:, tk_ * 512:(tk_ + 1) * 512],
                    wde_sb[1][:, J_ * 128:(J_ + 1) * 128],
                    x1t_sb[:, J_ * 1024 + tk_ * 512: J_ * 1024 + (tk_ + 1) * 512],
                    start=(J_ == 0), stop=(J_ == NEC - 1)),
                    [t3w, tx1[-1]] + ([prev_ht[0]] if J == 0 else []),
                    fence=(J == NEC - 1 and tk == 1))
        dq_state[1]["mm"] = mm3
        tsi = P("scalar", lambda e: e.activation(sil_sb[:], aPS[:], AF.Silu, bias=0.0,
               scale=1.0), [mm3, prev_ht[0]], pw=[mm1])
        prev_si[0] = tsi
        tht = P("vector", lambda e: e.tensor_mul(out=ht_sb[:], in0=sil_sb[:],
               in1=bPS[:]), [SPA([tsi]), prev_hstore[0]], pw=[mm3])
        prev_ht[0] = tht
        prev_hstore[0] = sc.dma("sync", hT_sc[hc].rearrange("p s -> p s"), ht_sb[:],
                                [tht])

    for oc in range(NEC):
        for jg, (j0, nj) in enumerate(((0, 32), (32, 32), (64, 22))):
            idx = (oc * 3 + jg) % 2
            pk_src = w2p[oc][:, j0:j0 + nj, :]
            v = s2b_sc.rearrange("(i j) h -> i j h", j=172)[oc][2 * j0:2 * (j0 + nj)]
            v4 = v.rearrange("(jj nib) h -> nib jj h", nib=2)
            sc_src = (v4[0].partition_broadcast(64), v4[1].partition_broadcast(64))
            tmu = dequant(idx, pk_src, sc_src, nj * 128)
            tws = sc.dma("sync",
                         w2bf_sc[oc, :, j0 * 128:(j0 + nj) * 128].rearrange(
                             "p (a b) -> p a b", b=128),
                         wde_sb[idx][:, 0:nj * 128].rearrange("p (a b) -> p a b", b=128),
                         [tmu], sem=f"w2bf{idx}")
            dq_state[idx]["mm"] = tws

    p6_done = sc.snap()
    p6.close()

    if PHL == 6:
        sc.dma("gpsimd", outT.rearrange("(ec p) s -> ec p s", p=128), x1bf_sc[:],
               p6_done + sc.snap())
        return finish()

    p7 = ExitStack()
    hT_res = SB(p7, "hT_res", (128, NHC * 512), BF16)
    w2_sb = [SB(p7, f"w2_sb{i}", (128, HID), BF16) for i in range(2)]
    x2pre_sb = SB(p7, "x2pre_sb", (128, 512), F32)
    x1r_sb = SB(p7, "x1r_sb", (128, 512), F32)
    x2b_sb = SB(p7, "x2b_sb", (128, 512), BF16)
    sq7_sb = SB(p7, "sq7_sb", (128, 512), BF16)
    mu7_sb = SB(p7, "mu7_sb", (1, 512), F32)
    ex27_sb = SB(p7, "ex27_sb", (1, 512), F32)
    var7_sb = SB(p7, "var7_sb", (1, 512), F32)
    rstd7_sb = SB(p7, "rstd7_sb", (1, 512), F32)
    mu7b_sb = SB(p7, "mu7b_sb", (1, 512), BF16)
    rstd7b_sb = SB(p7, "rstd7b_sb", (1, 512), BF16)
    x2n_sb = SB(p7, "x2n_sb", (128, 512), F32)
    acc17_sb = SB(p7, "acc17_sb", (1, 512), F32)
    acc27_sb = SB(p7, "acc27_sb", (1, 512), F32)
    fPS = PS(p7, "fPS", (128, 512), F32)
    s7PS = PS(p7, "s7PS", (128, 512), F32)
    s72PS = PS(p7, "s72PS", (128, 512), F32)
    mu7PS = PS(p7, "mu7PS", (128, 512), F32)
    rs7PS = PS(p7, "rs7PS", (128, 512), F32)

    w2_tok = [None, None]
    half_mm_last = [None]
    stat_free7 = [None]
    stat_acc7 = {"a1": None, "a2": None}
    prev_out_store = [None]
    prev_bc7 = [None]
    for half in range(2):
        th_toks = []
        for J2 in range(NHC):
            th_toks.append(sc.dma("gpsimd", hT_res[:, J2 * 512:(J2 + 1) * 512],
                           hT_sc[J2, :, half * 512:(half + 1) * 512],
                           p6_done + [half_mm_last[0]]))
        stat_mm = []
        prev_str = None
        prev_res = None
        prev_cb = None
        for oc in range(NEC):
            wb = oc % 2
            tw_ready = sc.dma("gpsimd", w2_sb[wb][:],
                              w2bf_sc[oc].rearrange("p s -> p s"),
                              p6_done + [w2_tok[wb]])
            mms = None
            for J2 in range(NHC):
                mms = P("tensor", lambda e, wb_=wb, J_=J2: e.matmul(
                    fPS[:], w2_sb[wb_][:, J_ * 128:(J_ + 1) * 128],
                    hT_res[:, J_ * 512:(J_ + 1) * 512],
                    start=(J_ == 0), stop=(J_ == NHC - 1)),
                    [tw_ready, th_toks[-1]] + ([prev_res] if J2 == 0 else []),
                    fence=(J2 == NHC - 1))
            w2_tok[wb] = mms
            half_mm_last[0] = mms
            txr = sc.dma("gpsimd", x1r_sb[:], x1_sc[oc, :, half * 512:(half + 1) * 512],
                         [prev_res])
            tres = P("vector", lambda e: e.tensor_add(out=x2pre_sb[:], in0=fPS[:],
                    in1=x1r_sb[:]), [txr, prev_str], pw=[mms])
            prev_res = tres
            tstr = sc.dma("sync", x2pre_sc[oc, :, half * 512:(half + 1) * 512],
                          x2pre_sb[:], [tres])
            prev_str = tstr
            tcb = P("vector", lambda e: e.tensor_copy(out=x2b_sb[:], in_=x2pre_sb[:]),
                   [SPV([tres]), prev_cb])
            tsq = P("vector", lambda e: e.tensor_mul(out=sq7_sb[:], in0=x2b_sb[:],
                   in1=x2b_sb[:]), [SPV([tcb])])
            m1 = P("tensor", lambda e: e.matmul(s7PS[0:1, :], ones_b[:, 0:1],
                  x2b_sb[:], start=True, stop=True),
                  [SPV([tcb]), stat_acc7["a1"]] + ([stat_free7[0]] if oc == 0 else []),
                  fence=True)
            a1 = P("vector", lambda e, oc_=oc: (
                  e.tensor_copy(out=acc17_sb[:], in_=s7PS[0:1, :]) if oc_ == 0 else
                  e.tensor_add(out=acc17_sb[:], in0=acc17_sb[:], in1=s7PS[0:1, :])),
                  [SPV()], pw=[m1])
            stat_acc7["a1"] = SPV([a1])
            m2 = P("tensor", lambda e: e.matmul(s72PS[0:1, :], ones_b[:, 0:1],
                  sq7_sb[:], start=True, stop=True),
                  [SPV([tsq]), stat_acc7["a2"]] + ([stat_free7[0]] if oc == 0 else []),
                  fence=True)
            a2 = P("vector", lambda e, oc_=oc: (
                  e.tensor_copy(out=acc27_sb[:], in_=s72PS[0:1, :]) if oc_ == 0 else
                  e.tensor_add(out=acc27_sb[:], in0=acc27_sb[:], in1=s72PS[0:1, :])),
                  [SPV()], pw=[m2])
            stat_acc7["a2"] = SPV([a2])
            stat_mm = [stat_acc7["a1"], stat_acc7["a2"]]
            prev_cb = m2
        tmu = P("vector", lambda e: e.tensor_scalar(out=mu7_sb[:], in0=acc17_sb[:],
               scalar1=1.0 / E, scalar2=None, op0=ALU.mult), stat_mm)
        tex2 = P("vector", lambda e: e.tensor_scalar(out=ex27_sb[:],
                in0=acc27_sb[:], scalar1=1.0 / E, scalar2=None, op0=ALU.mult),
                stat_mm)
        s_ = SPV([tmu, tex2])
        P("vector", lambda e: e.tensor_mul(out=var7_sb[:], in0=mu7_sb[:],
          in1=mu7_sb[:]), [s_])
        s_ = SPV()
        P("vector", lambda e: e.tensor_sub(out=var7_sb[:], in0=ex27_sb[:],
          in1=var7_sb[:]), [s_])
        s_ = SPV()
        tve = P("vector", lambda e: e.tensor_scalar(out=var7_sb[:],
               in0=var7_sb[:], scalar1=1e-5, scalar2=None, op0=ALU.add), [s_])
        s_ = SPV([tve])
        tln = P("scalar", lambda e: e.activation(rstd7_sb[:], var7_sb[:], AF.Ln,
               bias=0.0, scale=1.0), [s_])
        sa_ = SPA([tln])
        tsx = P("scalar", lambda e: e.activation(rstd7_sb[:], rstd7_sb[:], AF.Exp,
               bias=0.0, scale=-0.5), [sa_])
        sa2_ = SPA([tsx])
        tcbs = P("vector", lambda e: e.tensor_copy(out=mu7b_sb[:], in_=mu7_sb[:]),
                [sa2_, prev_bc7[0]])
        tcbs = P("vector", lambda e: e.tensor_copy(out=rstd7b_sb[:], in_=rstd7_sb[:]),
                [tcbs])
        tcbs = SPV([tcbs])
        tbmu = P("tensor", lambda e: e.matmul(mu7PS[:], ones_b[0:1, :], mu7b_sb[:],
                start=True, stop=True), [tcbs, stat_free7[0]], fence=True)
        tbrs = P("tensor", lambda e: e.matmul(rs7PS[:], ones_b[0:1, :], rstd7b_sb[:],
                start=True, stop=True), [tcbs, stat_free7[0]], fence=True)
        prev_bc7[0] = tbrs
        if DBG:
            sc.dma("sync", dbg_ln2[:, half * 1024:half * 1024 + 512], mu7_sb[:], [tbmu])
            sc.dma("sync", dbg_ln2[:, half * 1024 + 512:half * 1024 + 1024],
                   rstd7_sb[:], [tbmu])
        prev_n1 = None
        for oc in range(NEC):
            tld = sc.dma("gpsimd", x2pre_sb[:],
                         x2pre_sc[oc, :, half * 512:(half + 1) * 512],
                         [prev_str, prev_n1] + stat_mm)
            ts1 = P("vector", lambda e: e.tensor_sub(out=x2n_sb[:], in0=x2pre_sb[:],
                   in1=mu7PS[:]), [tld, prev_out_store[0]], pw=[tbmu, tbrs])
            prev_n1 = ts1
            ts2 = P("vector", lambda e: e.tensor_mul(out=x2n_sb[:], in0=x2n_sb[:],
                   in1=rs7PS[:]), [SPV([ts1]), tbrs])
            prev_out_store[0] = sc.dma("sync",
                   outT[oc * 128:(oc + 1) * 128, half * 512:(half + 1) * 512],
                   x2n_sb[:], [ts2])
        stat_free7[0] = prev_n1

    if DBG:
        endt7 = sc.snap()
        sc.dma("sync", dbg_x2pre, x2pre_sc[0].rearrange("p s -> p s"), endt7)
        sc.dma("sync", dbg_w2s, w2bf_sc[0, :, 0:1024].rearrange("p s -> p s"), endt7)
    p7.close()
    if DBG:
        endt = sc.snap()
        sc.dma("sync", dbg_qrT, qrT_sc[0].rearrange("p s -> p s"), endt)
        sc.dma("sync", dbg_krT, krT_sc[0].rearrange("p s -> p s"), endt)
        sc.dma("sync", dbg_v, v_sc[0].rearrange("p s -> p s"), endt)
        sc.dma("sync", dbg_attnT, attnT_sc[0].rearrange("p s -> p s"), endt)
        sc.dma("sync", dbg_x1, x1_sc[0].rearrange("p s -> p s"), endt)
        sc.dma("sync", dbg_ht, hT_sc[0].rearrange("p s -> p s"), endt)
    return finish()


def _rope_tables():
    inv = 1.0 / 10000.0 ** (np.arange(0, D, 2, dtype=np.float32) / D)
    t = np.arange(S, dtype=np.float32)
    fr = np.outer(t, inv)
    emb = np.concatenate([fr, fr], -1)
    cos = np.cos(emb).astype(np.float32)
    nsin = np.sin(emb).astype(np.float32)
    nsin[:, :D // 2] *= -1.0
    return cos, nsin


_NC_CACHE = [None]


def kernel(**inp):
    x = inp["L0_attn_hidden_states"]
    Wq, Wk, Wv, Wo = (inp["L0_attn_Wq"], inp["L0_attn_Wk"], inp["L0_attn_Wv"],
                      inp["L0_attn_Wo"])
    p1_, b1_ = inp["L0_attn_hash_proj1"], inp["L0_attn_hash_bias1"]
    p2_, b2_ = inp["L0_attn_hash_proj2"], inp["L0_attn_hash_bias2"]
    w1, s1 = inp["L0_ffn_w1"], inp["L0_ffn_s1"]
    w2, s2 = inp["L0_ffn_w2"], inp["L0_ffn_s2"]
    w3, s3 = inp["L0_ffn_w3"], inp["L0_ffn_s3"]

    cos, nsin = _rope_tables()
    cosM = np.ascontiguousarray(cos.reshape(NTB, 128, D).transpose(1, 0, 2))
    nsinM = np.ascontiguousarray(nsin.reshape(NTB, 128, D).transpose(1, 0, 2))
    wqTt = np.ascontiguousarray(
        Wq.T.reshape(NEC, 128, NEC, 128).transpose(2, 1, 0, 3).reshape(NEC, 128, NEC * 128))
    wkvT = np.concatenate([Wk.T, Wv.T], axis=1)
    wkvTt = np.ascontiguousarray(
        wkvT.reshape(NEC, 128, 16, 128).transpose(2, 1, 0, 3).reshape(16, 128, NEC * 128))
    woTt = np.ascontiguousarray(
        Wo.T.reshape(NEC, 128, NEC, 128).transpose(2, 1, 0, 3).reshape(NEC, 128, NEC * 128))
    w1p = np.ascontiguousarray(w1.reshape(NHC, 128, 32, 64).transpose(0, 3, 2, 1))
    w3p = np.ascontiguousarray(w3.reshape(NHC, 128, 32, 64).transpose(0, 3, 2, 1))
    w2p = np.ascontiguousarray(w2.reshape(NEC, 128, NHC, 64).transpose(0, 3, 2, 1))
    s1t = np.ascontiguousarray(s1.reshape(NHC, 128, 64).transpose(0, 2, 1)
                               ).reshape(NHC * 64, 128)
    s3t = np.ascontiguousarray(s3.reshape(NHC, 128, 64).transpose(0, 2, 1)
                               ).reshape(NHC * 64, 128)
    s2t = np.ascontiguousarray(s2.reshape(NEC, 128, 172).transpose(0, 2, 1)
                               ).reshape(NEC * 172, 128)
    b1c = np.ascontiguousarray(b1_[:, 0, :].T)
    b2c = np.ascontiguousarray(b2_[:, 0, :].T)
    iotaF = np.broadcast_to(np.arange(S, dtype=np.float32)[None, :], (128, S)).copy()
    identM = np.eye(128, dtype=np.float32)
    onesM = np.ones((128, 128), dtype=np.float32)

    in_maps = []
    for core in range(8):
        b = core // 4
        j = core % 4
        blks = [blk_of(j, s_) for s_ in range(NSLOT)]
        xTb = np.ascontiguousarray(x[b].T)
        xqb = np.ascontiguousarray(
            np.concatenate([xTb[:, bk * 128:(bk + 1) * 128] for bk in blks], axis=1))
        cosQ = np.ascontiguousarray(cosM[:, blks, :])
        nsinQ = np.ascontiguousarray(nsinM[:, blks, :])
        rowvb = (np.array(blks, dtype=np.float32)[None, :] * 128
                 + np.arange(128, dtype=np.float32)[:, None]).astype(np.float32)
        in_maps.append(dict(
            xT=xTb, xq=xqb, wqTt=wqTt, wkvTt=wkvTt, woTt=woTt,
            hp1=p1_, hp2=p2_, b1c=b1c, b2c=b2c,
            cosM=cosM, nsinM=nsinM, cosQ=cosQ, nsinQ=nsinQ, rowv=rowvb,
            iotaF=iotaF, identM=identM, onesM=onesM,
            zers=np.zeros((128, 64), np.float32),
            w1p=w1p, w3p=w3p, w2p=w2p, s1t=s1t, s3t=s3t, s2t=s2t,
        ))

    if _NC_CACHE[0] is None:
        _NC_CACHE[0] = build_nc()
    nc = _NC_CACHE[0]
    res = run_bass_kernel_spmd(nc, in_maps, core_ids=list(range(8)))

    out = np.empty((2, S, E), dtype=np.float32)
    for core in range(8):
        b = core // 4
        j = core % 4
        blks = [blk_of(j, s_) for s_ in range(NSLOT)]
        oT = res.results[core]["outT"]
        for si, bk in enumerate(blks):
            out[b, bk * 128:(bk + 1) * 128, :] = oT[:, si * 128:(si + 1) * 128].T
    return out
